# version 04
# baseline (speedup 1.0000x reference)
import numpy as np
import ml_dtypes
import concourse.bass as bass
import concourse.mybir as mybir
from concourse.bass_utils import run_bass_kernel_spmd

F32 = mybir.dt.float32
BF16 = mybir.dt.bfloat16
I32 = mybir.dt.int32
AF = mybir.ActivationFunctionType
ALU = mybir.AluOpType
AX = mybir.AxisListType

B, L, D, H = 2, 8192, 1024, 4
DK = DV = 256
GH = 2048
FI = 3072
TOK = 2048
C = 128
NCH = L // C
MAGIC = 12582912.0
GROUPS = [[0, 1, 2, 3], [4, 5, 6, 7]]
ALL8 = [list(range(8))]
PHASE_ALL = ("boot", "ln1", "gmix", "proj", "fir", "scan", "branch", "oproj", "ffn")


class Ctx:
    POOL = 42

    def __init__(self, nc):
        self.nc = nc
        self.t = {}
        self.sb = {}
        self.sem_h = {}
        self.sem_base = {}
        self.sem_n = {}
        self.phys = []
        self.phys_tot = {}
        self.rr = 0

    EXCL = ("y_done", "gm_w", "pj_sp", "pj_z",
            "gm_st", "fc_st", "fir_sp0", "fir_sp1", "sc_spn0", "sc_spn1",
            "sc_ost", "br_ost0", "br_ost1", "op_st",
            "ldg", "ag1", "ag2", "rs", "ar0", "ar1", "ar2", "ar3",
            "f_xq", "f_ug0", "f_ug1", "f_g2", "f_op", "f_r1", "f_upq",
            "arf0", "arf1", "arf2", "arf3", "arq0", "arq1", "arq2", "arq3",
            "arg0", "arg1", "arg2", "arg3", "arB0", "arB1", "arB2", "arB3",
            "out_st", "barA", "barD", "barG")

    def sem(self, name):
        if name not in self.sem_h:
            assert name not in getattr(self, "retired", set()), f"{name} used after retire"
            if name in self.EXCL:
                h = self.nc.semaphore("x_" + name).__enter__()
                self.sem_h[name] = h
                self.sem_base[name] = 0
                self.sem_n[name] = 0
                self.phys_tot[h.num] = 0
                return h
            if not hasattr(self, "free_h"):
                self.free_h = []
                self.tenants = {}
                self.retired = set()
                self.cooling = []
            if self.free_h:
                h = self.free_h.pop()
            else:
                h = self.nc.semaphore(f"s{len(self.phys)}").__enter__()
                self.phys.append(h)
                self.phys_tot[h.num] = 0
            self.tenants.setdefault(h.num, set()).add(name)
            self.sem_h[name] = h
            self.sem_base[name] = self.phys_tot[h.num]
            self.sem_n[name] = 0
        return self.sem_h[name]

    def retire(self, *names):
        if not hasattr(self, "free_h"):
            self.free_h = []
            self.tenants = {}
            self.retired = set()
            self.cooling = []
        self.free_h.extend(self.cooling)
        self.cooling = []
        for name in names:
            if name not in self.sem_h or name in self.EXCL:
                continue
            self.retired.add(name)
            h = self.sem_h[name]
            tn = self.tenants.get(h.num, set())
            tn.discard(name)
            if not tn and h not in self.cooling and h not in self.free_h:
                self.cooling.append(h)

    def barrier(self, blk):
        self._bar = getattr(self, "_bar", 0) + 1
        k = self._bar
        cx = self

        def scl(s):
            cx.inc(s.copy(out=cx.sb["bscr"][0:1, 0:1], in_=cx.sb["bscr"][0:1, 1:2]),
                   "barA")
            s.wait_ge(cx.sem("barA"), 2 * k)
            s.wait_ge(cx.sem("barD"), 16 * k)
            s.wait_ge(cx.sem("barG"), 16 * k)
        blk.scalar(scl)

        def vec(v):
            cx.inc(v.tensor_copy(out=cx.sb["bscr"][0:1, 2:3],
                                 in_=cx.sb["bscr"][0:1, 3:4]), "barA")
            v.wait_ge(cx.sem("barA"), 2 * k)
            v.wait_ge(cx.sem("barD"), 16 * k)
            v.wait_ge(cx.sem("barG"), 16 * k)
        blk.vector(vec)

        def syn(s):
            cx.inc(s.dma_start(out=cx.t["bscrD"][0:1, :], in_=cx.t["bscrD"][1:2, :]),
                   "barD", 16)
            s.wait_ge(cx.sem("barA"), 2 * k)
            s.wait_ge(cx.sem("barD"), 16 * k)
            s.wait_ge(cx.sem("barG"), 16 * k)
        blk.sync(syn)

        def gps(g):
            cx.inc(g.dma_start(out=cx.t["bscrG"][0:1, :], in_=cx.t["bscrG"][1:2, :]),
                   "barG", 16)
            g.wait_ge(cx.sem("barA"), 2 * k)
            g.wait_ge(cx.sem("barD"), 16 * k)
            g.wait_ge(cx.sem("barG"), 16 * k)
        blk.gpsimd(gps)

    def inc(self, instr, name, k=1):
        h = self.sem(name)
        instr.then_inc(h, k)
        self.sem_n[name] += k
        self.phys_tot[h.num] += k
        return instr

    def wait(self, eng, name, n=None):
        h = self.sem(name)
        eng.wait_ge(h, self.sem_base[name] + (self.sem_n[name] if n is None else n))

    def n(self, name):
        return self.sem_n.get(name, 0)


def build_program(upto="ffn", taps=()):
    nc = bass.Bass("TRN2", target_bir_lowering=False, debug=False, detect_race_conditions=False)
    cx = Ctx(nc)
    t, sb = cx.t, cx.sb
    cx.tapset = set(taps)
    phases = PHASE_ALL[:PHASE_ALL.index(upto) + 1]

    def din(name, shape, dt=F32):
        t[name] = nc.dram_tensor(name, shape, dt, kind="ExternalInput")

    def dout(name, shape, dt=F32):
        t[name] = nc.dram_tensor(name, shape, dt, kind="ExternalOutput")

    def dscr(name, shape, dt=F32):
        t[name] = nc.dram_tensor(name, shape, dt)

    def sbuf(name, shape, dt=F32):
        sb[name] = nc.sbuf_tensor(name, shape, dt).__enter__()

    def psum(name, shape, dt=F32):
        sb[name] = nc.psum_tensor(name, shape, dt).__enter__()

    cx.dout = dout

    din("x_slice", [TOK, D])
    din("wqkvb_t", [D, 772])
    din("convw", [768, 4])
    din("gw1t", [D, GH])
    din("gb1", [128, 16])
    din("gw2t", [GH, 16])
    din("gb2", [16, 1])
    din("alpha_h", [1, 1])
    din("onw", [128, 2])
    din("wot", [DV, D])
    din("fir_s_pad", [DV, 384])
    din("fir_l_pad", [DV, 384])
    din("wup", [FI, D])
    din("wgate", [FI, D])
    din("wdown", [D, FI])
    din("hsel", [128, 4])
    din("ident", [128, 128], BF16)
    din("m_sl_neg", [128, 128], BF16)
    din("m_su_neg", [128, 128], BF16)
    din("m_u_incl", [128, 128], BF16)
    dout("out", [TOK, D])

    dscr("ag1_in", [TOK, D], BF16)
    dscr("ag1_out", [L, D], BF16)
    dscr("ag2_in", [32, TOK])
    dscr("ag2_out", [128, TOK])
    dscr("rs_in", [L, D])
    dscr("rs_out", [TOK, D])
    for i in range(4):
        dscr(f"ar{i}_in", [1, 4])
        dscr(f"ar{i}_out", [1, 4])
    dscr("q_sp", [DK, L], BF16)
    dscr("k_sp", [DK, L], BF16)
    dscr("v_sp", [DV, 65 * 128], BF16)
    dscr("qh_tm", [L, DK], BF16)
    dscr("kh_tm", [L, DK], BF16)
    dscr("kb_tm", [L, DK], BF16)
    dscr("od_tm", [L, DV], BF16)
    dscr("fs_sp", [DV, L], BF16)
    dscr("fl_sp", [DV, L], BF16)
    dscr("o_tm", [L, DV], BF16)
    dscr("bandsD", [2 * 16 * 128 * 16 * 256], BF16)
    dscr("xq_sp", [TOK, D], BF16)
    dscr("up_sp", [FI, TOK], BF16)
    dscr("gate_sp", [FI, TOK], BF16)
    dscr("opre_sp", [FI, TOK], BF16)
    dout("wup_dq", [FI, D], BF16)
    dout("wg_dq", [FI, D], BF16)
    dout("wd_dq", [D, FI], BF16)
    dscr("r1_sp", [TOK, D])
    dscr("flat_sc", [8, 128])
    dscr("bscrD", [2, 8])
    dscr("bscrG", [2, 8])

    sbuf("bscr", [1, 8])
    sbuf("k_ident", [128, 128], BF16)
    sbuf("k_msl", [128, 128], BF16)
    sbuf("k_msu", [128, 128], BF16)
    sbuf("k_mui", [128, 128], BF16)
    sbuf("zt", [128, 128], BF16)
    sbuf("wqkvb", [128, 8, 772], BF16)
    sbuf("convw_s", [128, 6, 4], F32)
    sbuf("gw2", [128, 16, 16], BF16)
    sbuf("gb1_s", [128, 16, 1], F32)
    sbuf("gb2_s", [16, 1], F32)
    sbuf("alpha_s", [1, 1], F32)
    sbuf("hsel_s", [128, 4])
    sbuf("onw_s", [128, 2, 1], F32)
    sbuf("junk", [128, 2, D], BF16)
    sbuf("beta_t", [128, NCH], F32)
    sbuf("k_eps", [128, 1], F32)
    sbuf("abc", [128, 1], F32)
    sbuf("magt", [128, 1], F32)
    psum("p_a", [128, 512], F32)
    psum("p_b", [128, 512], F32)
    cx.psum_cms = {}
    def pget(name, shape):
        cm = nc.psum_tensor(name, shape, F32)
        sb[name] = cm.__enter__()
        cx.psum_cms[name] = cm
    def pfree(name):
        cx.psum_cms.pop(name).__exit__(None, None, None)
        del sb[name]
    cx.pget, cx.pfree = pget, pfree
    cx.sbuf_cms = {}
    def sget(name, shape, dt=F32):
        cm = nc.sbuf_tensor(name, shape, dt)
        sb[name] = cm.__enter__()
        cx.sbuf_cms[name] = cm
    def sfree(*names):
        for name in names:
            cx.sbuf_cms.pop(name).__exit__(None, None, None)
            del sb[name]
    cx.sget, cx.sfree = sget, sfree
    pget("p_bt", [4, 512])
    pget("p_g2", [16, TOK])
    pget("p_btp", [128, 8])

    for nm, shape, dt in [("x2", [128, 4, D], F32), ("y2", [128, 2, D], BF16),
                          ("lnsc", [128, 16, 4], F32), ("yt_w", [128, 8, 4, 512], BF16),
                          ("gw1b", [128, 8, 2, 128], BF16), ("gh", [128, 2, TOK], BF16),
                          ("lg32", [32, TOK], F32), ("lgT", [32, TOK], F32),
                          ("raw", [128, 6, 515], F32), ("cvo", [128, 6, 512], F32),
                          ("qkvb", [128, 2, 6, 512], BF16), ("brow", [1, L], BF16)]:
        sget(nm, shape, dt)
    with nc.Block() as blk:
        emit_boot(cx, blk)
        if "ln1" in phases:
            emit_ln1(cx, blk)
        if "gmix" in phases:
            emit_gmix(cx, blk)
        if "proj" in phases:
            cx.barrier(blk)
            cx.retire("x_ld0", "x_ld1", "x_ld2", "x_ld3", "ln_st", "ln_a", "ln_vb",
                      "ln_sq", "ln_nb", "y_cmp", "gm_yt", "gm_mm1", "gm_mm2")
            emit_proj(cx, blk)
        if "fir" in phases:
            cx.barrier(blk)
            cx.retire("gm_ev", "pj_yt0", "pj_yt1", "pj_yt2", "pj_yt3", "pj_mm",
                      "pj_ev", "pj_cv_v", "pj_bmm", "pj_btmm")
            cx.sfree("brow", "qkvb", "cvo", "raw", "lgT", "lg32", "gh",
                     "gw1b", "yt_w", "lnsc", "y2", "x2")
            sget("fpst", [128, 4, 384], F32)
            sget("fpbf", [128, 4, 384], BF16)
            sget("vtb", [128, 16640], BF16)
            sget("bands", [128, 2, 2, 16, 256], BF16)
            sget("fev", [128, 2, 2, 512], BF16)
            cx.pfree("p_btp"); cx.pfree("p_g2"); cx.pfree("p_bt")
            cx.pget("pf_s", [128, 512]); cx.pget("pf_l", [128, 512])
            cx.pget("pf_s2", [128, 512]); cx.pget("pf_l2", [128, 512])
            emit_fir(cx, blk)
        if "scan" in phases:
            cx.barrier(blk)
            cx.retire("fc_ld", "fc_cast", "fir_vt0", "fir_vt1", "fir_vt2", "fir_vt3",
                      "fir_bd0", "fir_bd1", "gm_lgT", "pj_si", "pj_bt", "fir_mm",
                      "fir_ev")
            sget("qkvt", [128, 4, 3, 256], BF16)
            sget("nrm", [128, 4, 4, 256], BF16)
            sget("fT", [128, 4, 6, 128], BF16)
            sget("sstat", [128, 4, 8], F32)
            sget("tb", [128, 8, 512], BF16)
            sget("rbuf", [128, 256], BF16)
            sget("ubuf", [128, 256], BF16)
            sget("obuf", [128, 2, 256], BF16)
            sget("sbf", [128, 2, 256], BF16)
            sget("sq256", [128, 256], F32)
            cx.pfree("pf_l2"); cx.pfree("pf_s2")
            cx.pfree("pf_l"); cx.pfree("pf_s")
            cx.pget("pT1", [128, 512]); cx.pget("pT2", [128, 512])
            cx.pget("pR", [128, 256]); cx.pget("pU", [128, 256])
            cx.pget("pO", [128, 256]); cx.pget("pS", [128, 512])
            emit_scan(cx, blk)
        if "branch" in phases:
            cx.barrier(blk)
            cx.retire("sc_tl0", "sc_tl1", "sc_tl2", "sc_tl3", "sc_fl0", "sc_fl1",
                      "sc_fl2", "sc_fl3", "sc_ss", "sc_sd", "sc_inv", "sc_nm",
                      "sc_tb", "sc_tbe", "sc_sb0", "sc_rmm", "sc_rev", "sc_umm",
                      "sc_uev", "sc_omm", "sc_oev", "sc_ds")
            cx.sfree("sq256", "sbf", "obuf", "ubuf", "rbuf", "tb", "sstat",
                     "fT", "nrm", "qkvt", "fev", "bands", "vtb", "fpbf", "fpst")
            sget("ld16", [128, 64, 16], F32)
            sget("brt", [128, 4, 4, 256], BF16)
            sget("bst", [128, 64, 4, 4], F32)
            sget("wts", [128, 64, 4], F32)
            sget("stt2", [128, 64, 4, 2], F32)
            sget("obx", [128, 2, 256], BF16)
            sget("osc", [128, 4], F32)
            sget("sq2", [128, 512], F32)
            sget("oxb", [128, 2, 256], F32)
            sget("selb", [128, 64, 4])
            emit_branch(cx, blk)
        if "oproj" in phases:
            cx.barrier(blk)
            cx.retire("br_lg", "br_tl0", "br_tl1", "br_tl2", "br_tl3", "br_st",
                      "br_c1", "br_c2", "br_c3", "br_sm1", "br_sm2", "br_sm3",
                      "br_sm4", "br_wts", "br_t20", "br_t21", "br_t22", "br_t23",
                      "br_mx", "br_ms", "br_sd", "br_inv", "sc_scp", "br_on")
            sget("oT", [128, 4, 2, 128], BF16)
            sget("wotf", [128, 2, D], BF16)
            sget("wost", [128, 2, D], F32)
            sget("rsev", [128, 2, D], F32)
            emit_oproj(cx, blk)
        if "ffn" in phases:
            cx.barrier(blk)
            cx.retire("op_w", "op_wf", "op_tl0", "op_tl1", "op_tl2", "op_tl3",
                      "op_mm")
            cx.pfree("pS"); cx.pfree("pO"); cx.pfree("pU")
            cx.pfree("pR"); cx.pfree("pT2"); cx.pfree("pT1")
            emit_ffn(cx, blk)
        emit_tail(cx, blk)
    return nc


def tapped(cx, name):
    return name in cx.tapset


def emit_boot(cx, blk):
    nc, t, sb = cx.nc, cx.t, cx.sb

    def sync(s):
        for nm, dst in [("ident", "k_ident"), ("m_sl_neg", "k_msl"),
                        ("m_su_neg", "k_msu"), ("m_u_incl", "k_mui")]:
            cx.inc(s.dma_start(out=sb[dst][:, :], in_=t[nm][:, :]), "ld", 16)
        cx.inc(s.dma_start(out=sb["gb2_s"][:, :], in_=t["gb2"][:, :]), "ld", 16)
        cx.inc(s.dma_start(out=sb["alpha_s"][:, :], in_=t["alpha_h"][:, :]), "ld", 16)
        cx.inc(s.dma_start(out=sb["hsel_s"][:, :], in_=t["hsel"][:, :]), "ld", 16)
        cx.inc(s.dma_start(out=sb["abc"][:, :],
                           in_=t["alpha_h"][:, :].to_broadcast((128, 1))), "ld", 16)
        cx.inc(s.dma_start(out=sb["onw_s"][:, :, 0], in_=t["onw"][:, :]), "ld", 16)
        cx.inc(s.dma_start(out=sb["convw_s"][:, :, :],
                           in_=t["convw"][:, :].rearrange("(m p) k -> p m k", p=128)), "ld", 16)
        cx.inc(s.dma_start(out=sb["gb1_s"][:, :, 0], in_=t["gb1"][:, :]), "ld", 16)
    blk.sync(sync)

    def vec(v):
        cx.inc(v.memset(sb["zt"][:, :], 0.0), "zt")
        v.memset(sb["k_eps"][:, :], 1e-5)
        cx.inc(v.memset(sb["magt"][:, :], MAGIC), "zt")
    blk.vector(vec)

    def gps(g):
        cx.inc(g.dma_start(
            out=sb["wqkvb"][:, :, :],
            in_=t["wqkvb_t"][:, :].rearrange("(kt p) m -> p kt m", p=128)), "ldg", 16)
        cx.inc(g.dma_start(
            out=sb["gw2"][:, :, :],
            in_=t["gw2t"][:, :].rearrange("(kt p) m -> p kt m", p=128)), "ldg", 16)
    blk.gpsimd(gps)


def emit_ln1(cx, blk):
    nc, t, sb = cx.nc, cx.t, cx.sb
    NT = 16

    def sync(s):
        for i in range(NT):
            if i >= 4:
                cx.wait(s, "y_cmp", i - 3)
            cx.inc(s.dma_start(out=sb["x2"][:, i % 4, :],
                               in_=t["x_slice"][128 * i:128 * (i + 1), :]), f"x_ld{i % 4}", 16)
    blk.sync(sync)

    def vecA(v, i):
        cx.wait(v, f"x_ld{i % 4}", 16 * (i // 4 + 1))
        xt = sb["x2"][:, i % 4, :]
        sc = sb["lnsc"][:, i, :]
        v.tensor_scalar(out=sb["junk"][:, 0, :], in0=xt, scalar1=0.0, scalar2=None,
                        op0=ALU.add, op1=ALU.add, accum_out=sc[:, 0:1])
        sq = sb["cvo"][:, 0:2, :]
        v.tensor_mul(out=sq, in0=xt.rearrange("p (a b) -> p a b", a=2),
                     in1=xt.rearrange("p (a b) -> p a b", a=2))
        cx.inc(v.tensor_scalar(out=sb["junk"][:, 1, :].rearrange("p (a b) -> p a b", a=2),
                               in0=sq, scalar1=0.0, scalar2=None,
                               op0=ALU.add, op1=ALU.add, accum_out=sc[:, 1:2]), "ln_st")

    def actA(s, i):
        cx.wait(s, "ln_st", i + 1)
        sc = sb["lnsc"][:, i, :]
        s.activation(out=sc[:, 0:1], in_=sc[:, 0:1], func=AF.Identity, scale=1.0 / D)
        s.activation(out=sc[:, 1:2], in_=sc[:, 1:2], func=AF.Identity, scale=1.0 / D)
        cx.inc(s.activation(out=sc[:, 2:3], in_=sc[:, 0:1], func=AF.Square), "ln_a")

    def vecB(v, i):
        cx.wait(v, "ln_a", i + 1)
        sc = sb["lnsc"][:, i, :]
        v.tensor_sub(out=sc[:, 1:2], in0=sc[:, 1:2], in1=sc[:, 2:3])
        cx.inc(v.tensor_scalar(out=sc[:, 1:2], in0=sc[:, 1:2], scalar1=1e-5,
                               scalar2=None, op0=ALU.add), "ln_vb")

    def actB(s, i):
        cx.wait(s, "ln_vb", i + 1)
        sc = sb["lnsc"][:, i, :]
        cx.inc(s.activation(out=sc[:, 1:2], in_=sc[:, 1:2], func=AF.Sqrt), "ln_sq")

    def vecC(v, i):
        cx.wait(v, "ln_sq", i + 1)
        sc = sb["lnsc"][:, i, :]
        v.reciprocal(out=sc[:, 2:3], in_=sc[:, 1:2])
        cx.inc(v.tensor_scalar(out=sc[:, 3:4], in0=sc[:, 0:1], scalar1=-1.0,
                               scalar2=None, op0=ALU.mult), "ln_nb")

    def actC(s, i):
        cx.wait(s, "ln_nb", i + 1)
        if i >= 2:
            cx.wait(s, "y_done", 16 * (i - 1))
        sc = sb["lnsc"][:, i, :]
        s.activation(out=sc[:, 3:4], in_=sc[:, 3:4], func=AF.Identity,
                     scale=sc[:, 2:3])
        cx.inc(s.activation(out=sb["y2"][:, i % 2, :], in_=sb["x2"][:, i % 4, :],
                            func=AF.Identity, bias=sb["lnsc"][:, i, 3:4],
                            scale=sb["lnsc"][:, i, 2:3]), "y_cmp")

    def vec(v):
        for i in range(NT):
            vecA(v, i)
            if i >= 1:
                vecB(v, i - 1)
            if i >= 2:
                vecC(v, i - 2)
        vecB(v, NT - 1)
        vecC(v, NT - 2)
        vecC(v, NT - 1)
    blk.vector(vec)

    def sc(s):
        for i in range(NT):
            actA(s, i)
            if i >= 1:
                actB(s, i - 1)
            if i >= 2:
                actC(s, i - 2)
        actB(s, NT - 1)
        actC(s, NT - 2)
        actC(s, NT - 1)
    blk.scalar(sc)

    def gps(g):
        for i in range(NT):
            cx.wait(g, "y_cmp", i + 1)
            cx.inc(g.dma_start(out=t["ag1_in"][128 * i:128 * (i + 1), :],
                               in_=sb["y2"][:, i % 2, :]), "y_done", 16)
        cx.wait(g, "y_done")
        cx.inc(g.collective_compute(
            "AllGather", ALU.bypass,
            ins=[t["ag1_in"].ap().opt()], outs=[t["ag1_out"].ap().opt()],
            replica_groups=GROUPS), "ag1")
    blk.gpsimd(gps)

    if tapped(cx, "t_y"):
        cx.dout("t_y", [TOK, D], BF16)
        cx.dout("t_lnsc", [128, 16, 4])
        def syncT(s):
            cx.wait(s, "y_done")
            cx.inc(s.dma_start(out=t["t_y"][:, :], in_=t["ag1_in"][:, :]), "taps", 16)
            cx.inc(s.dma_start(out=t["t_lnsc"][:, :, :], in_=sb["lnsc"][:, :, :]), "taps", 16)
        blk.sync(syncT)


def emit_gmix(cx, blk):
    nc, t, sb = cx.nc, cx.t, cx.sb

    def sync(s):
        cx.wait(s, "y_done", 16 * 16)
        for w in range(4):
            for kt in range(8):
                cx.inc(s.dma_start(
                    out=sb["yt_w"][:, kt, w, :],
                    in_=t["ag1_in"][512 * w:512 * (w + 1), 128 * kt:128 * (kt + 1)],
                    transpose=True), "gm_yt", 16)
    blk.sync(sync)

    def gps(g):
        for m in range(16):
            if m >= 2:
                cx.wait(g, "gm_mm1", 4 * (m - 1))
            cx.inc(g.dma_start(
                out=sb["gw1b"][:, :, m % 2, :],
                in_=t["gw1t"][:, 128 * m:128 * (m + 1)].rearrange("(kt p) m -> p kt m", p=128)),
                "gm_w", 16)
    blk.gpsimd(gps)

    def mm2(te, m):
        cx.wait(te, "gm_ev", 4 * (m + 1))
        mm = None
        for w in range(4):
            mm = te.matmul(sb["p_g2"][:, 512 * w:512 * (w + 1)],
                           sb["gw2"][:, m, :], sb["gh"][:, m % 2, 512 * w:512 * (w + 1)],
                           start=(m == 0), stop=(m == 15))
        cx.inc(mm, "gm_mm2")

    def ten(te):
        cx.wait(te, "ld")
        cx.wait(te, "ldg")
        cx.wait(te, "gm_yt", 16 * 32)
        for m in range(16):
            cx.wait(te, "gm_w", 16 * min(m + 2, 16))
            for w in range(4):
                idx = m * 4 + w
                ps = sb["p_a"] if idx % 2 == 0 else sb["p_b"]
                if idx >= 2:
                    cx.wait(te, "gm_ev", idx - 1)
                mm = None
                for kt in range(8):
                    mm = te.matmul(ps[:, :], sb["gw1b"][:, kt, m % 2, :],
                                   sb["yt_w"][:, kt, w, :], start=(kt == 0), stop=(kt == 7))
                cx.inc(mm, "gm_mm1")
            if m >= 1:
                mm2(te, m - 1)
        mm2(te, 15)
    blk.tensor(ten)

    def sc(s):
        for m in range(16):
            for w in range(4):
                idx = m * 4 + w
                cx.wait(s, "gm_mm1", idx + 1)
                if m >= 2:
                    cx.wait(s, "gm_mm2", m - 1)
                ps = sb["p_a"] if idx % 2 == 0 else sb["p_b"]
                cx.inc(s.activation(out=sb["gh"][:, m % 2, 512 * w:512 * (w + 1)],
                                    in_=ps[:, :], func=AF.Gelu,
                                    bias=sb["gb1_s"][:, m, 0:1], scale=1.0), "gm_ev")
    blk.scalar(sc)

    def vec(v):
        cx.wait(v, "gm_mm2", 16)
        v.memset(sb["lg32"][:, :], 0.0)
        v.tensor_scalar(out=sb["lg32"][0:16, :], in0=sb["p_g2"][:, :],
                        scalar1=sb["gb2_s"][:, :], scalar2=None, op0=ALU.add)
        cx.inc(v.transpose(out=sb["lgT"][:, :], in_=sb["lg32"][:, :]), "gm_lgT")
    blk.vector(vec)

    def gps2(g):
        cx.wait(g, "gm_lgT")
        cx.inc(g.dma_start(out=t["ag2_in"][:, :], in_=sb["lgT"][:, :]), "gm_st", 16)
        cx.wait(g, "gm_st")
        cx.inc(g.collective_compute(
            "AllGather", ALU.bypass,
            ins=[t["ag2_in"].ap().opt()], outs=[t["ag2_out"].ap().opt()],
            replica_groups=GROUPS), "ag2")
    blk.gpsimd(gps2)

    if tapped(cx, "t_logits"):
        cx.dout("t_logits", [32, TOK])
        cx.dout("t_lgT", [32, TOK])
        def syncT(s):
            cx.wait(s, "gm_lgT")
            cx.inc(s.dma_start(out=t["t_logits"][:, :], in_=sb["lg32"][:, :]), "taps", 16)
            cx.inc(s.dma_start(out=t["t_lgT"][:, :], in_=sb["lgT"][:, :]), "taps", 16)
        blk.sync(syncT)


def emit_proj(cx, blk):
    nc, t, sb = cx.nc, cx.t, cx.sb
    NW = 16

    def sync(s):
        cx.wait(s, "ag1")
        for w in range(NW):
            if w >= 4:
                cx.wait(s, "pj_mm", 6 * (w - 3))
            for kt in range(8):
                cx.inc(s.dma_start(
                    out=sb["yt_w"][:, kt, w % 4, :],
                    in_=t["ag1_out"][512 * w:512 * (w + 1), 128 * kt:128 * (kt + 1)],
                    transpose=True), f"pj_yt{w % 4}", 16)
    blk.sync(sync)

    def ten(te):
        cx.wait(te, "gm_ev")
        for w in range(NW):
            cx.wait(te, f"pj_yt{w % 4}", 16 * 8 * (w // 4 + 1))
            for m in range(6):
                idx = w * 6 + m
                ps = sb["p_a"] if idx % 2 == 0 else sb["p_b"]
                if idx >= 2:
                    cx.wait(te, "pj_ev", idx - 1)
                mm = None
                for kt in range(8):
                    mm = te.matmul(ps[:, :], sb["wqkvb"][:, kt, 128 * m:128 * (m + 1)],
                                   sb["yt_w"][:, kt, w % 4, :], start=(kt == 0), stop=(kt == 7))
                cx.inc(mm, "pj_mm")
            if w >= 1:
                cx.wait(te, "pj_si", 7 * w)
            mm = None
            for kt in range(8):
                mm = te.matmul(sb["p_bt"][:, :], sb["wqkvb"][:, kt, 768:772],
                               sb["yt_w"][:, kt, w % 4, :], start=(kt == 0), stop=(kt == 7))
            cx.inc(mm, "pj_bmm")
        for c in range(NCH):
            cx.wait(te, "pj_si", 7 * ((c // 4) + 1))
            if c % 8 == 0 and c > 0:
                cx.wait(te, "pj_bt", c // 8)
            mm = te.matmul(sb["p_btp"][:, c % 8:c % 8 + 1],
                           sb["brow"][0:1, 128 * c:128 * (c + 1)],
                           sb["k_ident"][0:1, 0:1], start=True, stop=True)
            if c % 8 == 7:
                cx.inc(mm, "pj_btmm")
    blk.tensor(ten)

    def vec(v):
        for w in range(NW):
            for m in range(6):
                idx = w * 6 + m
                cx.wait(v, "pj_ev", idx + 1)
                if w >= 1:
                    cx.wait(v, "pj_si", 7 * (w - 1) + m + 1)
                if w >= 2:
                    cx.wait(v, "pj_sp", 16 * 6 * w)
                cx.inc(emit_conv_tile(cx, v, m), "pj_cv_v")
    blk.vector(vec)

    def gps(g):
        for w in range(NW):
            cx.wait(g, "pj_si", 7 * (w + 1) - 1)
            for m in range(6):
                name = ["q_sp", "q_sp", "k_sp", "k_sp", "v_sp", "v_sp"][m]
                roff = (m % 2) * 128
                if name == "v_sp":
                    dst = t["v_sp"][roff:roff + 128, 128 + 512 * w: 128 + 512 * (w + 1)]
                else:
                    dst = t[name][roff:roff + 128, 512 * w:512 * (w + 1)]
                cx.inc(g.dma_start(out=dst, in_=sb["qkvb"][:, w % 2, m, :]), "pj_sp", 16)
        cx.wait(g, "zt")
        cx.inc(g.dma_start(out=t["v_sp"][0:128, 0:128], in_=sb["zt"][:, :]), "pj_z", 16)
        cx.inc(g.dma_start(out=t["v_sp"][128:256, 0:128], in_=sb["zt"][:, :]), "pj_z", 16)
    blk.gpsimd(gps)

    def sc(s):
        for w in range(NW):
            for m in range(6):
                idx = w * 6 + m
                cx.wait(s, "pj_mm", idx + 1)
                if w == 0:
                    s.memzero(sb["raw"][:, m, 0:3])
                else:
                    cx.wait(s, "pj_cv_v", 6 * (w - 1) + m + 1)
                    s.copy(out=sb["raw"][:, m, 0:3], in_=sb["raw"][:, m, 512:515])
                ps = sb["p_a"] if idx % 2 == 0 else sb["p_b"]
                cx.inc(s.copy(out=sb["raw"][:, m, 3:515], in_=ps[:, :]), "pj_ev")
            for m in range(6):
                cx.wait(s, "pj_cv_v", 6 * w + m + 1)
                cx.inc(s.activation(out=sb["qkvb"][:, w % 2, m, :], in_=sb["cvo"][:, m, :],
                                    func=AF.Silu), "pj_si")
            cx.wait(s, "pj_bmm", w + 1)
            cx.inc(s.activation(out=sb["brow"][0:1, 512 * w:512 * (w + 1)],
                                in_=sb["p_bt"][0:1, :], func=AF.Sigmoid), "pj_si")
    blk.scalar(sc)


    def vec2(v):
        for g8 in range(NCH // 8):
            cx.wait(v, "pj_btmm", g8 + 1)
            cx.inc(v.tensor_copy(out=sb["beta_t"][:, 8 * g8:8 * (g8 + 1)],
                                 in_=sb["p_btp"][:, :]), "pj_bt")
    blk.vector(vec2)

    if tapped(cx, "t_qkv"):
        cx.dout("t_q", [DK, L], BF16)
        cx.dout("t_k", [DK, L], BF16)
        cx.dout("t_v", [DV, L], BF16)
        cx.dout("t_beta", [128, NCH])
        def syncT(s):
            cx.wait(s, "pj_sp")
            cx.wait(s, "pj_z")
            cx.inc(s.dma_start(out=t["t_q"][:, :], in_=t["q_sp"][:, :]), "taps", 16)
            cx.inc(s.dma_start(out=t["t_k"][:, :], in_=t["k_sp"][:, :]), "taps", 16)
            cx.inc(s.dma_start(out=t["t_v"][:, :], in_=t["v_sp"][:, 128:128 + L]), "taps", 16)
            cx.wait(s, "pj_bt")
            cx.inc(s.dma_start(out=t["t_beta"][:, :], in_=sb["beta_t"][:, :]), "taps", 16)
        blk.sync(syncT)


def emit_conv_tile(cx, eng, m):
    sb = cx.sb
    raw = sb["raw"][:, m, :]
    out = sb["cvo"][:, m, :]
    wts = cx.sb["convw_s"]
    eng.tensor_scalar(out=out, in0=raw[:, 3:515], scalar1=wts[:, m, 3:4],
                      scalar2=None, op0=ALU.mult)
    last = None
    for tau in (2, 1, 0):
        last = eng.scalar_tensor_tensor(out=out, in0=raw[:, tau:tau + 512],
                                        scalar=wts[:, m, tau:tau + 1], in1=out,
                                        op0=ALU.mult, op1=ALU.add)
    return last


def mkap(handle, offset, dims):
    from concourse.ap import AP
    return AP(tensor=handle, offset=offset, ap=[list(d) for d in dims])


def emit_fir(cx, blk):
    nc, t, sb = cx.nc, cx.t, cx.sb

    def sync(s):
        cx.wait(s, "pj_bt")
        cx.inc(s.dma_start(out=sb["fpst"][:, 0:2, :],
                           in_=t["fir_s_pad"][:, :].rearrange("(a p) c -> p a c", p=128)),
               "fc_ld", 16)
        cx.inc(s.dma_start(out=sb["fpst"][:, 2:4, :],
                           in_=t["fir_l_pad"][:, :].rearrange("(a p) c -> p a c", p=128)),
               "fc_ld", 16)
    blk.sync(sync)

    def vec0(v):
        cx.wait(v, "fc_ld")
        cx.inc(v.tensor_copy(out=sb["fpbf"][:, :, :], in_=sb["fpst"][:, :, :]), "fc_cast")
    blk.vector(vec0)

    GSZ = 128 * 16 * 256
    FIRSZ = 16 * GSZ

    def gps0(g):
        cx.wait(g, "fc_cast")
        for p in range(128):
            for a in range(4):
                fir, half = a // 2, a % 2
                dst = mkap(t["bandsD"], fir * FIRSZ + half * 8 * GSZ + p * 4096,
                           [[GSZ, 8], [256, 16], [1, 256]])
                cx.inc(g.dma_start(out=dst, in_=sb["fpbf"][:, a, 128 - p:384 - p]),
                       "fc_st", 16)
    blk.gpsimd(gps0)

    def scl(s):
        cx.wait(s, "pj_sp"); cx.wait(s, "pj_z")
        cx.wait(s, "pj_bt"); cx.wait(s, "pj_si")
        vsrc = t["v_sp"][:, :].rearrange("d (b c) -> (d b) c", c=128)
        for q in range(4):
            cx.inc(s.dma_start(out=sb["vtb"][:, 4160 * q:4160 * (q + 1)],
                               in_=vsrc[4160 * q:4160 * (q + 1), :], transpose=True),
                   f"fir_vt{q}", 16)
    blk.scalar(scl)

    def sync2(s):
        cx.wait(s, "fc_st")
        for g in range(16):
            if g >= 2:
                cx.wait(s, "fir_mm", 16 * (g - 1))
            for fir in range(2):
                srcap = mkap(t["bandsD"], fir * FIRSZ + g * GSZ,
                             [[16 * 256, 128], [256, 16], [1, 256]])
                cx.inc(s.dma_start(out=sb["bands"][:, g % 2, fir, :, :], in_=srcap),
                       f"fir_bd{g % 2}", 16)
    blk.sync(sync2)

    def ten(te):
        cx.wait(te, "gm_lgT")
        for q in range(4):
            cx.wait(te, f"fir_vt{q}", 16)
        for ch in range(256):
            g = ch // 16
            if ch % 16 == 0:
                cx.wait(te, f"fir_bd{g % 2}", 16 * 2 * (g // 2 + 1))
            G8 = ch // 8
            if ch % 8 == 0 and G8 >= 2:
                cx.wait(te, "fir_ev", 2 * (G8 - 1))
            pfs = sb["pf_s"] if G8 % 2 == 0 else sb["pf_s2"]
            pfl = sb["pf_l"] if G8 % 2 == 0 else sb["pf_l2"]
            ro = (ch % 2) * 64
            co = ((ch // 2) % 4) * 128
            lm = sb["vtb"][:, ch * 65 + 1: ch * 65 + 65]
            lc = sb["vtb"][:, ch * 65: ch * 65 + 64]
            bs = sb["bands"][:, g % 2, 0, ch % 16, :]
            bl = sb["bands"][:, g % 2, 1, ch % 16, :]
            te.matmul(pfs[ro:ro + 64, co:co + 128], lm, bs[:, 0:128], start=True, stop=False,
                      skip_group_check=True)
            te.matmul(pfl[ro:ro + 64, co:co + 128], lm, bl[:, 0:128], start=True, stop=False,
                      skip_group_check=True)
            te.matmul(pfs[ro:ro + 64, co:co + 128], lc, bs[:, 128:256], start=False, stop=True,
                      skip_group_check=True)
            mm = te.matmul(pfl[ro:ro + 64, co:co + 128], lc, bl[:, 128:256], start=False,
                           stop=True, skip_group_check=True)
            cx.inc(mm, "fir_mm")
    blk.tensor(ten)

    def vec(v):
        for G8 in range(32):
            cx.wait(v, "fir_mm", 8 * (G8 + 1))
            if G8 >= 2:
                cx.wait(v, f"fir_sp{G8 % 2}", 16 * 4 * ((G8 - 2) // 2 + 1))
            pfs = sb["pf_s"] if G8 % 2 == 0 else sb["pf_s2"]
            pfl = sb["pf_l"] if G8 % 2 == 0 else sb["pf_l2"]
            v.tensor_copy(out=sb["fev"][:, G8 % 2, 0, :], in_=pfs[:, :])
            cx.inc(v.tensor_copy(out=sb["fev"][:, G8 % 2, 1, :], in_=pfl[:, :]), "fir_ev", 2)
    blk.vector(vec)

    def gps(g):
        for G8 in range(32):
            cx.wait(g, "fir_ev", 2 * (G8 + 1))
            for fir, dname in ((0, "fs_sp"), (1, "fl_sp")):
                for ch2 in range(2):
                    dst = mkap(t[dname], (G8 * 8 + ch2) * 8192,
                               [[128, 64], [2 * 8192, 4], [1, 128]])
                    cx.inc(g.dma_start(
                        out=dst,
                        in_=sb["fev"][64 * ch2:64 * (ch2 + 1), G8 % 2, fir, :]
                            .rearrange("p (s c) -> p s c", s=4)),
                        f"fir_sp{G8 % 2}", 16)
    blk.gpsimd(gps)

    if tapped(cx, "t_fs"):
        cx.dout("t_fs", [DV, L], BF16)
        cx.dout("t_fl", [DV, L], BF16)
        def syncT(s):
            cx.wait(s, "fir_sp0"); cx.wait(s, "fir_sp1")
            cx.inc(s.dma_start(out=t["t_fs"][:, :], in_=t["fs_sp"][:, :]), "taps", 16)
            cx.inc(s.dma_start(out=t["t_fl"][:, :], in_=t["fl_sp"][:, :]), "taps", 16)
        blk.sync(syncT)


def emit_scan(cx, blk):
    nc, t, sb = cx.nc, cx.t, cx.sb
    RING = 4

    def bcast4(tile):
        return tile[:, None, :].broadcast_to([128, 4, 128])

    def syncA(s):
        cx.wait(s, "pj_sp"); cx.wait(s, "pj_z")
        for c in range(NCH):
            r = c % RING
            if c >= RING:
                cx.wait(s, "sc_ds", c - RING + 1)
            for i, (nm, co) in enumerate((("q_sp", 0), ("k_sp", 0), ("v_sp", 128))):
                cx.inc(s.dma_start(
                    out=sb["qkvt"][:, r, i, :],
                    in_=t[nm][:, co + 128 * c: co + 128 * (c + 1)], transpose=True),
                    f"sc_tl{r}", 16)
            cx.wait(s, f"sc_spn{c % 2}", 16 * 3 * (c // 2 + 1))
            for i, nm in enumerate(("qh_tm", "kh_tm", "kb_tm")):
                for kt in range(2):
                    cx.inc(s.dma_start(
                        out=sb["fT"][:, r, 2 * i + kt, :],
                        in_=t[nm][128 * c:128 * (c + 1), 128 * kt:128 * (kt + 1)],
                        transpose=True), f"sc_fl{r}", 16)
    blk.sync(syncA)

    def vecB_c(v, c):
            r = c % RING
            cx.wait(v, f"sc_tl{r}", 16 * 3 * (c // RING + 1))
            if c >= 4:
                cx.wait(v, "sc_nm", 5 * (c - 3))
            st = sb["sstat"][:, r, :]
            for i in range(2):
                x = sb["qkvt"][:, r, i, :]
                v.tensor_mul(out=sb["sq256"][:, :], in0=x, in1=x)
                v.tensor_scalar(out=sb["junk"][:, 0, 0:256], in0=sb["sq256"][:, :],
                                scalar1=0.0, scalar2=None, op0=ALU.add, op1=ALU.add,
                                accum_out=st[:, i:i + 1])
            cx.inc(v.tensor_copy(out=st[:, 7:8], in_=sb["beta_t"][:, c:c + 1]), "sc_ss")
            cx.wait(v, "sc_sd", 2 * (c + 1))
            v.reciprocal(out=st[:, 2:3], in_=st[:, 0:1])
            cx.inc(v.reciprocal(out=st[:, 3:4], in_=st[:, 1:2]), "sc_inv")

    def actB_c(s, c):
            r = c % RING
            st = sb["sstat"][:, r, :]
            cx.wait(s, "sc_ss", c + 1)
            if c >= 4:
                cx.wait(s, "sc_ds", c - 3)
                cx.wait(s, f"sc_spn{c % 2}", 16 * 3 * ((c - 4) // 2 + 1))
            s.activation(out=st[:, 0:1], in_=st[:, 0:1], func=AF.Sqrt)
            cx.inc(s.activation(out=st[:, 1:2], in_=st[:, 1:2], func=AF.Sqrt), "sc_sd", 2)
            cx.wait(s, "sc_inv", c + 1)
            s.activation(out=st[:, 4:5], in_=st[:, 7:8], func=AF.Identity, scale=st[:, 3:4])
            s.activation(out=sb["nrm"][:, r, 0, :], in_=sb["qkvt"][:, r, 0, :],
                         func=AF.Identity, scale=st[:, 2:3])
            s.activation(out=sb["nrm"][:, r, 1, :], in_=sb["qkvt"][:, r, 1, :],
                         func=AF.Identity, scale=st[:, 3:4])
            s.activation(out=sb["nrm"][:, r, 2, :], in_=sb["qkvt"][:, r, 1, :],
                         func=AF.Identity, scale=st[:, 4:5])
            cx.inc(s.activation(out=sb["nrm"][:, r, 3, :], in_=sb["qkvt"][:, r, 2, :],
                                func=AF.Identity, scale=st[:, 7:8]), "sc_nm", 5)

    def gpsC_c(g, c):
            r = c % RING
            cx.wait(g, "sc_nm", 5 * (c + 1))
            for i, nm in enumerate(("qh_tm", "kh_tm", "kb_tm")):
                cx.inc(g.dma_start(out=t[nm][128 * c:128 * (c + 1), :],
                                   in_=sb["nrm"][:, r, i, :]), f"sc_spn{c % 2}", 16)

    NSL = {0: (1, 2), 1: (3, 4)}

    def ten(te):
        def fl(c, which, kt):
            return sb["fT"][:, c % RING, 2 * which + kt, :]

        for w in range(NCH // 4):
            cs = [4 * w + i for i in range(4)]
            base = 14 * w
            for c in cs:
                cx.wait(te, f"sc_fl{c % RING}", 16 * 6 * (c // RING + 1))
            cx.wait(te, "sc_tbe", base)
            mm = None
            for c in cs:
                i = c % 4
                sl = slice(128 * i, 128 * (i + 1))
                te.matmul(sb["pT1"][:, sl], fl(c, 1, 0), fl(c, 0, 0), start=True, stop=False)
                mm = te.matmul(sb["pT1"][:, sl], fl(c, 1, 1), fl(c, 0, 1),
                               start=False, stop=True)
            cx.inc(mm, "sc_tb")
            for c in cs:
                i = c % 4
                sl = slice(128 * i, 128 * (i + 1))
                te.matmul(sb["pT2"][:, sl], fl(c, 2, 0), fl(c, 1, 0), start=True, stop=False)
                te.matmul(sb["pT2"][:, sl], fl(c, 2, 1), fl(c, 1, 1), start=False, stop=True)
            cx.wait(te, "sc_tbe", base + 1)
            for c in cs:
                i = c % 4
                sl = slice(128 * i, 128 * (i + 1))
                te.matmul(sb["pT1"][:, sl], fl(c, 1, 0), fl(c, 2, 0), start=True, stop=False)
                mm = te.matmul(sb["pT1"][:, sl], fl(c, 1, 1), fl(c, 2, 1),
                               start=False, stop=True)
            cx.inc(mm, "sc_tb")
            for lev in range(6):
                nsl, ntsl = NSL[lev % 2]
                cx.wait(te, "sc_tbe", base + 2 + 2 * lev)
                mm = None
                for c in cs:
                    i = c % 4
                    sl = slice(128 * i, 128 * (i + 1))
                    mm = te.matmul(sb["pT2"][:, sl], sb["tb"][:, ntsl, sl],
                                   sb["tb"][:, nsl, sl], start=True, stop=True)
                    if lev < 5:
                        mm = te.matmul(sb["pT1"][:, sl], sb["tb"][:, nsl, sl],
                                       sb["tb"][:, ntsl, sl], start=True, stop=True)
                cx.inc(mm, "sc_tb")
                cx.wait(te, "sc_tbe", base + 3 + 2 * lev)
                n2sl = NSL[(lev + 1) % 2][0]
                Pc = sb["tb"][:, 5 if lev % 2 == 0 else 6, :]
                mm = None
                for c in cs:
                    i = c % 4
                    sl = slice(128 * i, 128 * (i + 1))
                    mm = te.matmul(sb["pT1"][:, sl], sb["tb"][:, n2sl, sl], Pc[:, sl],
                                   start=True, stop=True)
                cx.inc(mm, "sc_tb")
            cx.wait(te, "sc_tbe", base + 14)
            for c in cs:
                i = c % 4
                r = c % RING
                if c > 0:
                    cx.wait(te, "sc_scp", c)
                else:
                    cx.wait(te, "sc_sb0", 1)
                te.matmul(sb["pR"][:, :], fl(c, 2, 0), sb["sbf"][:, 0, :],
                          start=True, stop=False)
                cx.inc(te.matmul(sb["pR"][:, :], fl(c, 2, 1), sb["sbf"][:, 1, :],
                                 start=False, stop=True), "sc_rmm")
                cx.wait(te, "sc_rev", c + 1)
                cx.wait(te, "sc_uev", c)
                cx.inc(te.matmul(sb["pU"][:, :], sb["tb"][:, 5, 128 * i:128 * (i + 1)],
                                 sb["rbuf"][:, :], start=True, stop=True), "sc_umm")
                cx.wait(te, "sc_oev", c)
                te.matmul(sb["pO"][:, :], fl(c, 0, 0), sb["sbf"][:, 0, :],
                          start=True, stop=False)
                te.matmul(sb["pO"][:, :], fl(c, 0, 1), sb["sbf"][:, 1, :],
                          start=False, stop=False)
                cx.wait(te, "sc_uev", c + 1)
                cx.inc(te.matmul(sb["pO"][:, :], sb["tb"][:, 0, 128 * i:128 * (i + 1)],
                                 sb["ubuf"][:, :], start=False, stop=True), "sc_omm")
                te.matmul(sb["pS"][:, 0:256], sb["nrm"][:, r, 1, 0:128],
                          sb["ubuf"][:, :], start=(c == 0), stop=False,
                          skip_group_check=True)
                cx.inc(te.matmul(sb["pS"][:, 256:512], sb["nrm"][:, r, 1, 128:256],
                                 sb["ubuf"][:, :], start=False, stop=False,
                                 skip_group_check=True), "sc_ds")
    blk.tensor(ten)

    def vecE(v):
        cx.inc(v.memset(sb["sbf"][:, :, :], 0.0), "sc_sb0")
        tbn = 0
        for w in range(NCH // 4):
            cs = [4 * w + i for i in range(4)]
            for c in cs:
                vecB_c(v, c)
            tbn += 1
            cx.wait(v, "sc_tb", tbn)
            cx.inc(v.tensor_tensor(
                out=sb["tb"][:, 0, :].rearrange("p (a b) -> p a b", a=4),
                in0=sb["pT1"][:, :].rearrange("p (a b) -> p a b", a=4),
                in1=bcast4(sb["k_mui"]), op=ALU.mult), "sc_tbe")
            tbn += 1
            cx.wait(v, "sc_tb", tbn)
            v.tensor_tensor(out=sb["tb"][:, 1, :].rearrange("p (a b) -> p a b", a=4),
                            in0=sb["pT2"][:, :].rearrange("p (a b) -> p a b", a=4),
                            in1=bcast4(sb["k_msl"]), op=ALU.mult)
            v.tensor_tensor(out=sb["tb"][:, 2, :].rearrange("p (a b) -> p a b", a=4),
                            in0=sb["pT1"][:, :].rearrange("p (a b) -> p a b", a=4),
                            in1=bcast4(sb["k_msu"]), op=ALU.mult)
            cx.inc(v.tensor_tensor(
                out=sb["tb"][:, 5, :].rearrange("p (a b) -> p a b", a=4),
                in0=sb["tb"][:, 2, :].rearrange("p (a b) -> p a b", a=4),
                in1=bcast4(sb["k_ident"]), op=ALU.add), "sc_tbe")
            for lev in range(6):
                n2sl, nt2sl = NSL[(lev + 1) % 2]
                tbn += 1
                cx.wait(v, "sc_tb", tbn)
                v.tensor_copy(out=sb["tb"][:, n2sl, :], in_=sb["pT2"][:, :])
                if lev < 5:
                    cx.inc(v.tensor_copy(out=sb["tb"][:, nt2sl, :], in_=sb["pT1"][:, :]),
                           "sc_tbe")
                else:
                    cx.inc(v.tensor_copy(out=sb["junk"][:, 0, 0:8],
                                         in_=sb["tb"][:, n2sl, 0:8]), "sc_tbe")
                tbn += 1
                cx.wait(v, "sc_tb", tbn)
                Psrc = sb["tb"][:, 5 if lev % 2 == 0 else 6, :]
                Pdst = sb["tb"][:, 6 if lev % 2 == 0 else 5, :]
                cx.inc(v.tensor_add(out=Pdst, in0=sb["pT1"][:, :], in1=Psrc), "sc_tbe")
            for c in cs:
                r = c % RING
                cx.wait(v, "sc_rmm", c + 1)
                cx.wait(v, "sc_umm", c)
                cx.inc(v.scalar_tensor_tensor(out=sb["rbuf"][:, :], in0=sb["pR"][:, :],
                                              scalar=-1.0, in1=sb["nrm"][:, r, 3, :],
                                              op0=ALU.mult, op1=ALU.add), "sc_rev")
                cx.wait(v, "sc_omm", c + 1)
                if c >= 2:
                    cx.wait(v, "sc_ost", 16 * (c - 1))
                cx.inc(v.tensor_copy(out=sb["obuf"][:, c % 2, :], in_=sb["pO"][:, :]),
                       "sc_oev")
    blk.vector(vecE)

    def actE_c(s, c):
            cx.wait(s, "sc_umm", c + 1)
            cx.inc(s.activation(out=sb["ubuf"][:, :], in_=sb["pU"][:, :],
                                func=AF.Identity), "sc_uev")
            cx.wait(s, "sc_ds", c + 1)
            s.activation(out=sb["sbf"][:, 0, :], in_=sb["pS"][:, 0:256], func=AF.Identity)
            cx.inc(s.activation(out=sb["sbf"][:, 1, :], in_=sb["pS"][:, 256:512],
                                func=AF.Identity), "sc_scp")

    def actE2(s):
        for w in range(NCH // 4):
            for c in [4 * w + i for i in range(4)]:
                actB_c(s, c)
            for c in [4 * w + i for i in range(4)]:
                actE_c(s, c)
    blk.scalar(actE2)

    def gpsF(g):
        for w in range(NCH // 4):
            for c in [4 * w + i for i in range(4)]:
                gpsC_c(g, c)
            for c in [4 * w + i for i in range(4)]:
                cx.wait(g, "sc_oev", c + 1)
                cx.inc(g.dma_start(out=t["od_tm"][128 * c:128 * (c + 1), :],
                                   in_=sb["obuf"][:, c % 2, :]), "sc_ost", 16)
    blk.gpsimd(gpsF)

    if tapped(cx, "t_od"):
        cx.dout("t_od", [L, DV], BF16)
        def syncT(s):
            cx.wait(s, "sc_ost")
            cx.inc(s.dma_start(out=t["t_od"][:, :], in_=t["od_tm"][:, :]), "taps", 16)
        blk.sync(syncT)


def emit_branch(cx, blk):
    nc, t, sb = cx.nc, cx.t, cx.sb
    NT = 64

    def syncL(s):
        cx.wait(s, "ag2")
        v2 = t["ag2_out"][:, :].rearrange("(j r) (b ph f) -> r j b f ph",
                                          j=4, ph=4, f=32)
        for ph in range(4):
            for j in range(4):
                cx.inc(s.dma_start(
                    out=sb["ld16"][32 * ph:32 * (ph + 1), 16 * j:16 * (j + 1), :],
                    in_=v2[:, j, :, 0:16, ph]), "br_lg", 16)
        cx.wait(s, "sc_ost"); cx.wait(s, "sc_scp")
        cx.wait(s, "fir_sp0"); cx.wait(s, "fir_sp1")
        for tt in range(NT):
            r = tt % 4
            if tt >= 4:
                cx.wait(s, "br_st", tt - 3)
            cx.inc(s.dma_start(out=sb["brt"][:, r, 0, :],
                               in_=t["fs_sp"][:, 128 * tt:128 * (tt + 1)],
                               transpose=True), f"br_tl{r}", 16)
            cx.inc(s.dma_start(out=sb["brt"][:, r, 1, :],
                               in_=t["fl_sp"][:, 128 * tt:128 * (tt + 1)],
                               transpose=True), f"br_tl{r}", 16)
            cx.inc(s.dma_start(out=sb["brt"][:, r, 2, :],
                               in_=t["od_tm"][128 * tt:128 * (tt + 1), :]),
                   f"br_tl{r}", 16)
            cx.inc(s.dma_start(out=sb["brt"][:, r, 3, :],
                               in_=t["v_sp"][:, 128 + 128 * tt:128 + 128 * (tt + 1)],
                               transpose=True), f"br_tl{r}", 16)
    blk.sync(syncL)

    def vecS(v):
        for tt in range(NT):
            r = tt % 4
            cx.wait(v, f"br_tl{r}", 16 * 4 * (tt // 4 + 1))
            last = None
            for br in range(4):
                x = sb["brt"][:, r, br, :]
                bs = sb["bst"][:, tt, br, :]
                v.tensor_scalar(out=sb["junk"][:, 0, 0:256], in0=x, scalar1=0.0,
                                scalar2=None, op0=ALU.add, op1=ALU.add,
                                accum_out=bs[:, 0:1])
                v.tensor_mul(out=sb["junk"][:, 1, 0:256], in0=x, in1=x)
                v.tensor_scalar(out=sb["junk"][:, 0, 256:512],
                                in0=sb["junk"][:, 1, 0:256], scalar1=0.0, scalar2=None,
                                op0=ALU.add, op1=ALU.add, accum_out=bs[:, 1:2])
                last = v.tensor_reduce(out=bs[:, 2:3], in_=x, axis=AX.X, op=ALU.max)
            cx.inc(last, "br_st")
    blk.vector(vecS)

    def actC(s):
        cx.wait(s, "br_st", NT)
        bs = sb["bst"]
        s2 = sb["stt2"]
        s.activation(out=s2[:, :, :, 0], in_=bs[:, :, :, 0], func=AF.Identity,
                     scale=1.0 / 256.0)
        s.activation(out=s2[:, :, :, 1], in_=bs[:, :, :, 1], func=AF.Identity,
                     scale=1.0 / 255.0)
        cx.inc(s.activation(out=bs[:, :, :, 3], in_=s2[:, :, :, 0], func=AF.Square,
                            scale=float(np.sqrt(256.0 / 255.0))), "br_c1")
    blk.scalar(actC)

    def vecC(v):
        cx.wait(v, "br_c1", 1)
        s2, bs = sb["stt2"], sb["bst"]
        v.tensor_sub(out=s2[:, :, :, 1], in0=s2[:, :, :, 1], in1=bs[:, :, :, 3])
        cx.inc(v.tensor_scalar(out=s2[:, :, :, 1], in0=s2[:, :, :, 1], scalar1=0.0,
                               scalar2=None, op0=ALU.max), "br_c2")
    blk.vector(vecC)

    def actC2(s):
        cx.wait(s, "br_c2", 1)
        cx.inc(s.activation(out=sb["stt2"][:, :, :, 1], in_=sb["stt2"][:, :, :, 1],
                            func=AF.Sqrt), "br_c3")
    blk.scalar(actC2)

    def vecC2(v):
        cx.wait(v, "br_c3", 1)
        cx.wait(v, "br_lg")
        s2, bs = sb["stt2"], sb["bst"]
        v.tensor_add(out=s2[:, :, :, 0], in0=s2[:, :, :, 0], in1=s2[:, :, :, 1])
        v.tensor_add(out=s2[:, :, :, 0], in0=s2[:, :, :, 0], in1=bs[:, :, :, 2])
        v.tensor_scalar(out=s2[:, :, :, 0], in0=s2[:, :, :, 0], scalar1=1.0 / 3.0,
                        scalar2=None, op0=ALU.mult)
        sel4 = sb["ld16"][:, :, :].rearrange("p b (c hh) -> p b hh c", hh=4)
        selb = sb["selb"][:, :, :]
        v.tensor_scalar(out=selb, in0=sel4[:, :, 0, :], scalar1=sb["hsel_s"][:, 0:1],
                        scalar2=None, op0=ALU.mult)
        for hh in range(1, 4):
            v.scalar_tensor_tensor(out=selb, in0=sel4[:, :, hh, :],
                                   scalar=sb["hsel_s"][:, hh:hh + 1], in1=selb,
                                   op0=ALU.mult, op1=ALU.add)
        v.scalar_tensor_tensor(out=sb["wts"][:, :, :], in0=s2[:, :, :, 0],
                               scalar=sb["abc"][:, 0:1], in1=selb,
                               op0=ALU.mult, op1=ALU.add)
        cx.inc(v.tensor_reduce(out=s2[:, :, 0, 1], in_=sb["wts"][:, :, :],
                               axis=AX.X, op=ALU.max), "br_sm1")
    blk.vector(vecC2)

    def gpsS(g):
        cx.wait(g, "br_sm1", 1)
        cx.inc(g.tensor_tensor(out=sb["wts"][:, :, :], in0=sb["wts"][:, :, :],
                               in1=sb["stt2"][:, :, 0:1, 1].broadcast_to([128, NT, 4]),
                               op=ALU.subtract), "br_sm2")
    blk.gpsimd(gpsS)

    def actS(s):
        cx.wait(s, "br_sm2", 1)
        cx.inc(s.activation(out=sb["wts"][:, :, :], in_=sb["wts"][:, :, :],
                            func=AF.Exp), "br_sm3")
    blk.scalar(actS)

    def vecS2(v):
        cx.wait(v, "br_sm3", 1)
        v.tensor_reduce(out=sb["stt2"][:, :, 0, 1], in_=sb["wts"][:, :, :],
                        axis=AX.X, op=ALU.add)
        cx.inc(v.reciprocal(out=sb["stt2"][:, :, 1, 1],
                            in_=sb["stt2"][:, :, 0, 1]), "br_sm4")
    blk.vector(vecS2)

    def gpsS2(g):
        cx.wait(g, "br_sm4", 1)
        cx.inc(g.tensor_tensor(out=sb["wts"][:, :, :], in0=sb["wts"][:, :, :],
                               in1=sb["stt2"][:, :, 1:2, 1].broadcast_to([128, NT, 4]),
                               op=ALU.mult), "br_wts")
    blk.gpsimd(gpsS2)

    def syncM(s):
        cx.wait(s, "br_wts", 1)
        for tt in range(NT):
            r = tt % 4
            if tt >= 4:
                cx.wait(s, "br_mx", tt - 3)
            cx.inc(s.dma_start(out=sb["brt"][:, r, 0, :],
                               in_=t["fs_sp"][:, 128 * tt:128 * (tt + 1)],
                               transpose=True), f"br_t2{r}", 16)
            cx.inc(s.dma_start(out=sb["brt"][:, r, 1, :],
                               in_=t["fl_sp"][:, 128 * tt:128 * (tt + 1)],
                               transpose=True), f"br_t2{r}", 16)
            cx.inc(s.dma_start(out=sb["brt"][:, r, 2, :],
                               in_=t["od_tm"][128 * tt:128 * (tt + 1), :]),
                   f"br_t2{r}", 16)
            cx.inc(s.dma_start(out=sb["brt"][:, r, 3, :],
                               in_=t["v_sp"][:, 128 + 128 * tt:128 + 128 * (tt + 1)],
                               transpose=True), f"br_t2{r}", 16)
    blk.sync(syncM)

    def vecM(v):
        for tt in range(NT):
            r = tt % 4
            cx.wait(v, f"br_t2{r}", 16 * 4 * (tt // 4 + 1))
            if tt >= 2:
                cx.wait(v, "br_on", tt - 1)
            ox = sb["oxb"][:, tt % 2, :]
            v.tensor_scalar(out=ox, in0=sb["brt"][:, r, 0, :],
                            scalar1=sb["wts"][:, tt, 0:1], scalar2=None, op0=ALU.mult)
            last = None
            for br in range(1, 4):
                last = v.scalar_tensor_tensor(out=ox, in0=sb["brt"][:, r, br, :],
                                              scalar=sb["wts"][:, tt, br:br + 1], in1=ox,
                                              op0=ALU.mult, op1=ALU.add)
            cx.inc(last, "br_mx")
            v.tensor_mul(out=sb["sq2"][:, 0:256], in0=ox, in1=ox)
            cx.inc(v.tensor_scalar(out=sb["sq2"][:, 256:512], in0=sb["sq2"][:, 0:256],
                                   scalar1=0.0, scalar2=None, op0=ALU.add, op1=ALU.add,
                                   accum_out=sb["osc"][:, tt % 4:tt % 4 + 1]), "br_ms")
            cx.wait(v, "br_sd", tt + 1)
            cx.inc(v.reciprocal(out=sb["osc"][:, tt % 4:tt % 4 + 1],
                                in_=sb["osc"][:, tt % 4:tt % 4 + 1]), "br_inv")
    blk.vector(vecM)

    def actM(s):
        for tt in range(NT):
            cx.wait(s, "br_ms", tt + 1)
            cx.inc(s.activation(out=sb["osc"][:, tt % 4:tt % 4 + 1],
                                in_=sb["osc"][:, tt % 4:tt % 4 + 1], func=AF.Sqrt,
                                scale=1.0 / 256.0, bias=sb["k_eps"][:, 0:1]), "br_sd")
            cx.wait(s, "br_inv", tt + 1)
            if tt >= 2:
                cx.wait(s, f"br_ost{tt % 2}", 16 * ((tt - 2) // 2 + 1))
            ox = sb["oxb"][:, tt % 2, :]
            cx.inc(s.activation(out=sb["obx"][:, tt % 2, :], in_=ox, func=AF.Identity,
                                scale=sb["osc"][:, tt % 4:tt % 4 + 1]), "br_on")
    blk.scalar(actM)

    def gpsM(g):
        for tt in range(NT):
            cx.wait(g, "br_on", tt + 1)
            cx.inc(g.dma_start(out=t["o_tm"][128 * tt:128 * (tt + 1), :],
                               in_=sb["obx"][:, tt % 2, :]), f"br_ost{tt % 2}", 16)
    blk.gpsimd(gpsM)

    if tapped(cx, "t_ag2"):
        cx.dout("t_ag2", [128, TOK])
        cx.dout("t_agi", [32, TOK])
        def syncTa(s):
            cx.wait(s, "ag2")
            cx.inc(s.dma_start(out=t["t_ag2"][:, :], in_=t["ag2_out"][:, :]), "taps", 16)
            cx.inc(s.dma_start(out=t["t_agi"][:, :], in_=t["ag2_in"][:, :]), "taps", 16)
        blk.sync(syncTa)

    if tapped(cx, "t_brdbg"):
        cx.dout("t_selb", [128, 64 * 4])
        cx.dout("t_stat", [128, 64 * 4])
        cx.dout("t_wts", [128, 64 * 4])
        cx.dout("t_bst", [128, 64 * 4 * 4])
        def syncTb(s):
            cx.wait(s, "br_wts")
            cx.inc(s.dma_start(out=t["t_selb"][:, :],
                               in_=sb["selb"][:, :, :].rearrange("p a b -> p (a b)")),
                   "taps", 16)
            cx.inc(s.dma_start(out=t["t_stat"][:, :],
                               in_=sb["stt2"][:, :, :, 0].rearrange("p a b -> p (a b)"),
                               allow_slow_non_contiguous=True),
                   "taps", 16)
            cx.inc(s.dma_start(out=t["t_wts"][:, :],
                               in_=sb["wts"][:, :, :].rearrange("p a b -> p (a b)")),
                   "taps", 16)
            cx.inc(s.dma_start(out=t["t_bst"][:, :],
                               in_=sb["bst"][:, :, :, :].rearrange("p a b c -> p (a b c)")),
                   "taps", 16)
        blk.sync(syncTb)

    if tapped(cx, "t_o"):
        cx.dout("t_o", [L, DV], BF16)
        def syncT(s):
            cx.wait(s, "br_ost0"); cx.wait(s, "br_ost1")
            cx.inc(s.dma_start(out=t["t_o"][:, :], in_=t["o_tm"][:, :]), "taps", 16)
        blk.sync(syncT)


def emit_oproj(cx, blk):
    nc, t, sb = cx.nc, cx.t, cx.sb
    NT = 64

    def sync0(s):
        cx.inc(s.dma_start(out=sb["wost"][:, :, :],
                           in_=t["wot"][:, :].rearrange("(kt p) d -> p kt d", p=128)),
               "op_w", 16)
    blk.sync(sync0)

    def act0(s):
        cx.wait(s, "op_w", 16)
        for kt in range(2):
            cx.inc(s.activation(out=sb["wotf"][:, kt, :], in_=sb["wost"][:, kt, :],
                                func=AF.Identity, scale=sb["onw_s"][:, kt, 0:1]),
                   "op_wf")
    blk.scalar(act0)

    def syncL(s):
        for tt in range(NT):
            r = tt % 4
            cx.wait(s, f"br_ost{tt % 2}", 16 * (tt // 2 + 1))
            if tt >= 4:
                cx.wait(s, "op_mm", 2 * (tt - 3))
            for kt in range(2):
                cx.inc(s.dma_start(out=sb["oT"][:, r, kt, :],
                                   in_=t["o_tm"][128 * tt:128 * (tt + 1),
                                                 128 * kt:128 * (kt + 1)],
                                   transpose=True), f"op_tl{r}", 16)
    blk.sync(syncL)

    def ten(te):
        cx.wait(te, "op_wf")
        for tt in range(NT):
            r = tt % 4
            cx.wait(te, f"op_tl{r}", 16 * 2 * (tt // 4 + 1))
            if tt >= 1:
                cx.wait(te, "op_ev", 2 * tt)
            te.matmul(sb["p_a"][:, :], sb["oT"][:, r, 0, :], sb["wotf"][:, 0, 0:512],
                      start=True, stop=False)
            cx.inc(te.matmul(sb["p_a"][:, :], sb["oT"][:, r, 1, :],
                             sb["wotf"][:, 1, 0:512], start=False, stop=True), "op_mm")
            te.matmul(sb["p_b"][:, :], sb["oT"][:, r, 0, :], sb["wotf"][:, 0, 512:1024],
                      start=True, stop=False)
            cx.inc(te.matmul(sb["p_b"][:, :], sb["oT"][:, r, 1, :],
                             sb["wotf"][:, 1, 512:1024], start=False, stop=True), "op_mm")
    blk.tensor(ten)

    def vecE(v):
        for tt in range(NT):
            cx.wait(v, "op_mm", 2 * (tt + 1))
            if tt >= 2:
                cx.wait(v, "op_st", 16 * (tt - 1))
            v.tensor_copy(out=sb["rsev"][:, tt % 2, 0:512], in_=sb["p_a"][:, :])
            cx.inc(v.tensor_copy(out=sb["rsev"][:, tt % 2, 512:1024], in_=sb["p_b"][:, :]),
                   "op_ev", 2)
    blk.vector(vecE)

    def gps(g):
        for tt in range(NT):
            cx.wait(g, "op_ev", 2 * (tt + 1))
            cx.inc(g.dma_start(out=t["rs_in"][128 * tt:128 * (tt + 1), :],
                               in_=sb["rsev"][:, tt % 2, :]), "op_st", 16)
        cx.wait(g, "op_st")
        cx.inc(g.collective_compute(
            "ReduceScatter", ALU.add,
            ins=[t["rs_in"].ap().opt()], outs=[t["rs_out"].ap().opt()],
            replica_groups=GROUPS), "rs")
    blk.gpsimd(gps)

    if tapped(cx, "t_rs"):
        cx.dout("t_rs", [TOK, D])
        def syncT(s):
            cx.wait(s, "rs")
            cx.inc(s.dma_start(out=t["t_rs"][:, :], in_=t["rs_out"][:, :]), "taps", 16)
        blk.sync(syncT)


def lnh_vec_a(cx, v, x, sc, pfx, i):
    v.tensor_scalar(out=cx.sb["junk"][:, 0, :], in0=x, scalar1=0.0, scalar2=None,
                    op0=ALU.add, op1=ALU.add, accum_out=sc[:, 0:1])
    sq = cx.sb["fsq"][:, :]
    v.tensor_mul(out=sq, in0=x, in1=x)
    cx.inc(v.tensor_scalar(out=cx.sb["junk"][:, 1, :], in0=sq, scalar1=0.0,
                           scalar2=None, op0=ALU.add, op1=ALU.add,
                           accum_out=sc[:, 1:2]), pfx)

def lnh_act_a(cx, s, sc, pfx, i):
    cx.wait(s, pfx, 6 * i + 1)
    s.activation(out=sc[:, 0:1], in_=sc[:, 0:1], func=AF.Identity, scale=1.0 / D)
    s.activation(out=sc[:, 1:2], in_=sc[:, 1:2], func=AF.Identity, scale=1.0 / D)
    cx.inc(s.activation(out=sc[:, 2:3], in_=sc[:, 0:1], func=AF.Square), pfx)

def lnh_vec_b(cx, v, sc, pfx, i):
    cx.wait(v, pfx, 6 * i + 2)
    v.tensor_sub(out=sc[:, 1:2], in0=sc[:, 1:2], in1=sc[:, 2:3])
    cx.inc(v.tensor_scalar(out=sc[:, 1:2], in0=sc[:, 1:2], scalar1=1e-5,
                           scalar2=None, op0=ALU.add), pfx)

def lnh_act_b(cx, s, sc, pfx, i):
    cx.wait(s, pfx, 6 * i + 3)
    cx.inc(s.activation(out=sc[:, 1:2], in_=sc[:, 1:2], func=AF.Sqrt), pfx)

def lnh_vec_c(cx, v, sc, pfx, i):
    cx.wait(v, pfx, 6 * i + 4)
    v.reciprocal(out=sc[:, 2:3], in_=sc[:, 1:2])
    cx.inc(v.tensor_scalar(out=sc[:, 3:4], in0=sc[:, 0:1], scalar1=-1.0,
                           scalar2=None, op0=ALU.mult), pfx)

def lnh_gps_nb(cx, g, sc, pfx, i):
    cx.wait(g, pfx, 6 * i + 5)
    cx.inc(g.tensor_tensor(out=sc[:, 3:4], in0=sc[:, 3:4], in1=sc[:, 2:3],
                           op=ALU.mult), pfx)

def lnh_act_apply(cx, s, x, out, sc, pfx, i):
    cx.wait(s, pfx, 6 * i + 6)
    return s.activation(out=out, in_=x, func=AF.Identity, bias=sc[:, 3:4],
                        scale=sc[:, 2:3])


def emit_ffn(cx, blk):
    nc, t, sb = cx.nc, cx.t, cx.sb
    sget, sfree = cx.sget, cx.sfree
    NT = 16
    sfree("rsev", "wost", "wotf", "oT")
    sfree("selb", "oxb", "sq2", "osc", "obx", "stt2", "wts", "bst", "brt", "ld16")
    sget("fin", [128, 4, 2, D], F32)
    sget("lnsc2", [128, 16, 16], F32)
    sget("fmx", [128, 12, 24], F32)
    sget("fsc", [128, 16], F32)
    sget("flt", [4, 128], F32)
    sget("flt2", [4, 1], F32)
    sget("arsb", [1, 8], F32)
    sget("arv", [1, 16], F32)
    sget("fft", [128, 2, 2048], F32)
    sget("fsq", [128, D], F32)
    sget("xqb", [128, 2, D], BF16)
    sget("xnb", [128, 16, D], F32)

    def sync1(s):
        cx.wait(s, "rs")
        for i in range(NT):
            if i >= 4:
                cx.wait(s, "ffx", i - 3)
                cx.wait(s, "f_r1", 16 * (i - 3))
            cx.inc(s.dma_start(out=sb["fin"][:, i % 4, 0, :],
                               in_=t["rs_out"][128 * i:128 * (i + 1), :]),
                   f"ff_ld{i % 4}", 16)
            cx.inc(s.dma_start(out=sb["fin"][:, i % 4, 1, :],
                               in_=t["x_slice"][128 * i:128 * (i + 1), :]),
                   f"ff_ld{i % 4}", 16)
    blk.sync(sync1)

    def vec1(v):
        for i in range(NT):
            cx.wait(v, f"ff_ld{i % 4}", 32 * (i // 4 + 1))
            sc2 = sb["lnsc2"][:, i, 0:8]
            sc3 = sb["lnsc2"][:, i, 8:16]
            lnh_vec_a(cx, v, sb["fin"][:, i % 4, 0, :], sc2, "l2", i)
            lnh_vec_b(cx, v, sc2, "l2", i)
            lnh_vec_c(cx, v, sc2, "l2", i)
            cx.wait(v, "l2f", i + 1)
            cx.inc(v.tensor_add(out=sb["fin"][:, i % 4, 1, :],
                                in0=sb["fin"][:, i % 4, 1, :],
                                in1=sb["fft"][:, i % 2, 0:1024]), "ffr1")
            lnh_vec_a(cx, v, sb["fin"][:, i % 4, 1, :], sc3, "l3", i)
            lnh_vec_b(cx, v, sc3, "l3", i)
            lnh_vec_c(cx, v, sc3, "l3", i)
            cx.wait(v, "l3f", i + 1)
            v.tensor_scalar(out=sb["junk"][:, 0, :], in0=sb["xnb"][:, i, :],
                            scalar1=1.0, scalar2=None, op0=ALU.mult, op1=ALU.max,
                            accum_out=sb["fmx"][:, 0, i:i + 1])
            cx.inc(v.tensor_scalar(out=sb["junk"][:, 1, :], in0=sb["xnb"][:, i, :],
                                   scalar1=-1.0, scalar2=None, op0=ALU.mult,
                                   op1=ALU.max, accum_out=sb["fmx"][:, 1, i:i + 1]),
                   "ffx")
    blk.vector(vec1)

    def act1(s):
        for i in range(NT):
            sc2 = sb["lnsc2"][:, i, 0:8]
            sc3 = sb["lnsc2"][:, i, 8:16]
            lnh_act_a(cx, s, sc2, "l2", i)
            lnh_act_b(cx, s, sc2, "l2", i)
            cx.inc(lnh_act_apply(cx, s, sb["fin"][:, i % 4, 0, :],
                                 sb["fft"][:, i % 2, 0:1024], sc2, "l2", i), "l2f")
            cx.wait(s, "ffr1", i + 1)
            lnh_act_a(cx, s, sc3, "l3", i)
            lnh_act_b(cx, s, sc3, "l3", i)
            cx.inc(lnh_act_apply(cx, s, sb["fin"][:, i % 4, 1, :],
                                 sb["xnb"][:, i, :], sc3, "l3", i), "l3f")
    blk.scalar(act1)

    def gps1(g):
        for i in range(NT):
            lnh_gps_nb(cx, g, sb["lnsc2"][:, i, 0:8], "l2", i)
            cx.wait(g, "ffr1", i + 1)
            lnh_gps_nb(cx, g, sb["lnsc2"][:, i, 8:16], "l3", i)
            cx.inc(g.dma_start(out=t["r1_sp"][128 * i:128 * (i + 1), :],
                               in_=sb["fin"][:, i % 4, 1, :]), "f_r1", 16)
    blk.gpsimd(gps1)

    def armax2(idx, cols, nvals, vecpost, prewait, width=24):
        nco = len(cols)
        def vecr(v):
            cx.wait(v, prewait[0], prewait[1])
            for k, col in enumerate(cols):
                cx.inc(v.tensor_reduce(out=sb["fsc"][:, k:k + 1],
                                       in_=sb["fmx"][:, col, 0:width], axis=AX.X,
                                       op=ALU.max), f"arc{idx}")
        blk.vector(vecr)

        def gpsr(g):
            cx.wait(g, f"arc{idx}", nco)
            cx.inc(g.dma_start(out=mkap(t["flat_sc"], 0, [[1, 128], [128, nco]]),
                               in_=sb["fsc"][:, 0:nco],
                               allow_slow_non_contiguous=True), f"arf{idx}", 16)
        blk.gpsimd(gpsr)

        def syncr(s):
            cx.wait(s, f"arf{idx}", 16)
            cx.inc(s.dma_start(out=sb["flt"][0:nco, :],
                               in_=t["flat_sc"][0:nco, :]), f"arc{idx}", 16)
        blk.sync(syncr)

        def vecr2(v):
            cx.wait(v, f"arc{idx}", nco + 16)
            v.memset(sb["flt2"][:, :], 0.0)
            cx.inc(v.tensor_reduce(out=sb["flt2"][0:nco, 0:1],
                                   in_=sb["flt"][0:nco, :], axis=AX.X,
                                   op=ALU.max), f"arc{idx}")
        blk.vector(vecr2)

        def gpsr2(g):
            cx.wait(g, f"arc{idx}", nco + 17)
            cx.inc(g.dma_start(out=t[f"ar{idx}_in"][0:1, 0:4],
                               in_=sb["flt2"][0:4, 0:1],
                               allow_slow_non_contiguous=True), f"arq{idx}", 16)
            cx.wait(g, f"arq{idx}", 16)
            cx.inc(g.collective_compute(
                "AllReduce", ALU.max,
                ins=[t[f"ar{idx}_in"].ap().opt()], outs=[t[f"ar{idx}_out"].ap().opt()],
                replica_groups=ALL8), f"ar{idx}")
        blk.gpsimd(gpsr2)

        def syncr2(s):
            cx.wait(s, f"ar{idx}")
            cx.inc(s.dma_start(out=sb["arsb"][0:1, 0:nco],
                               in_=t[f"ar{idx}_out"][0:1, 0:nco]), f"arc{idx}", 16)
        blk.sync(syncr2)
        npost = vecpost(nco + 33)

        def gpsr3(g):
            cx.wait(g, f"arc{idx}", nco + 33 + npost)
            cx.inc(g.dma_start(out=t["flat_sc"][7:8, 0:nvals],
                               in_=sb["arv"][0:1, 0:nvals]), f"arg{idx}", 16)
            cx.wait(g, f"arg{idx}", 16)
            cx.inc(g.dma_start(out=sb["fsc"][:, 8:8 + nvals],
                               in_=t["flat_sc"][7:8, 0:nvals].to_broadcast((128, nvals))),
                   f"arB{idx}", 16)
        blk.gpsimd(gpsr3)

    def mkpost(idx, groups, mul, eps, sign, fac):
        def post(base):
            def vA(v):
                cx.wait(v, f"arc{idx}", base)
                a = sb["arsb"]; o = sb["arv"]
                last = None
                for k, (i0, i1) in enumerate(groups):
                    last = v.tensor_max(out=o[0:1, 6 + k:7 + k], in0=a[0:1, i0:i0 + 1],
                                        in1=a[0:1, i1:i1 + 1])
                cx.inc(last, f"arc{idx}")
            blk.vector(vA)

            def gB(g):
                cx.wait(g, f"arc{idx}", base + 1)
                o = sb["arv"]
                last = None
                for k in range(len(groups)):
                    tt = o[0:1, 6 + k:7 + k]
                    g.tensor_scalar(out=o[0:1, 3 * k + 1:3 * k + 2], in0=tt,
                                    scalar1=mul, scalar2=eps, op0=ALU.mult, op1=ALU.max)
                    g.tensor_scalar(out=o[0:1, 3 * k + 2:3 * k + 3], in0=tt,
                                    scalar1=-MAGIC * mul, scalar2=-MAGIC * eps,
                                    op0=ALU.mult, op1=ALU.min)
                    last = g.tensor_scalar(out=o[0:1, 12 + k:13 + k], in0=tt,
                                           scalar1=sign * mul * fac,
                                           scalar2=sign * eps * fac, op0=ALU.mult,
                                           op1=(ALU.max if sign > 0 else ALU.min))
                cx.inc(last, f"arc{idx}")
            blk.gpsimd(gB)

            def vC(v):
                cx.wait(v, f"arc{idx}", base + 2)
                o = sb["arv"]
                last = None
                for k in range(len(groups)):
                    last = v.reciprocal(out=o[0:1, 3 * k:3 * k + 1],
                                        in_=o[0:1, 12 + k:13 + k])
                cx.inc(last, f"arc{idx}")
            blk.vector(vC)
            return 3
        return post

    armax2(0, [0, 1], 3, mkpost(0, [(0, 1)], 1.0 / 32767.0, 1e-12, 1.0, 1.0),
           ("ffx", 16), width=16)

    if tapped(cx, "t_ar0"):
        cx.dout("t_fsc", [128, 16])
        cx.dout("t_arv", [1, 16])
        cx.dout("t_arsb", [1, 8])
        cx.dout("t_fmx", [128, 12 * 24])
        def syncTa0(s):
            cx.wait(s, "arB0", 16)
            cx.inc(s.dma_start(out=t["t_fsc"][:, :], in_=sb["fsc"][:, :]), "taps", 16)
            cx.inc(s.dma_start(out=t["t_arv"][:, :], in_=sb["arv"][:, :]), "taps", 16)
            cx.inc(s.dma_start(out=t["t_arsb"][:, :], in_=sb["arsb"][:, :]), "taps", 16)
            cx.inc(s.dma_start(out=t["t_fmx"][:, :],
                               in_=sb["fmx"][:, :, :].rearrange("p a b -> p (a b)")),
                   "taps", 16)
        blk.sync(syncTa0)

    def act2(s):
        for i in range(NT):
            cx.wait(s, "arB0", 16)
            if i >= 2:
                cx.wait(s, "f_xq", 16 * (i - 1))
            s.activation(out=sb["fft"][:, i % 2, 0:1024], in_=sb["xnb"][:, i, :],
                         func=AF.Identity, scale=sb["fsc"][:, 8:9],
                         bias=sb["magt"][:, 0:1])
            cx.inc(s.activation(out=sb["xqb"][:, i % 2, :], in_=sb["fft"][:, i % 2, 0:1024],
                                func=AF.Identity, scale=sb["fsc"][:, 9:10],
                                bias=sb["fsc"][:, 10:11]), "ffxq")
    blk.scalar(act2)

    def gps2(g):
        for i in range(NT):
            cx.wait(g, "ffxq", i + 1)
            cx.inc(g.dma_start(out=t["xq_sp"][128 * i:128 * (i + 1), :],
                               in_=sb["xqb"][:, i % 2, :]), "f_xq", 16)
    blk.gpsimd(gps2)

    if tapped(cx, "t_lnsc2"):
        cx.dout("t_lnsc2", [128, 16 * 16])
        def syncTl(s):
            cx.wait(s, "l3f", NT)
            cx.wait(s, "ffx", NT)
            cx.inc(s.dma_start(out=t["t_lnsc2"][:, :],
                               in_=sb["lnsc2"][:, :, :].rearrange("p a b -> p (a b)")),
                   "taps", 16)
        blk.sync(syncTl)

    cx.barrier(blk)
    cx.retire("l2", "l2f", "l3", "l3f", "ffr1", "ff_ld0", "ff_ld1", "ff_ld2", "ff_ld3",
              "ffx", "arc0")
    sfree("xnb")
    sget("xqT", [128, 8, 4, 512], BF16)
    sget("wuT", [128, 2, 8, 128], BF16)
    sget("upgb", [128, 2, 2048], BF16)
    sget("wdq_st", [128, 2, FI], F32)
    sget("wq_bf", [128, 2, FI], BF16)
    sget("bmx", [128, 2, 288], F32)
    sget("fbf", [128, 2, 2048], BF16)
    sget("fbf2", [128, 2, 2048], BF16)
    sget("fsg", [128, 2, 2048], F32)

    def sync3(s):
        cx.wait(s, "f_xq")
        for w4 in range(4):
            for kt in range(8):
                cx.inc(s.dma_start(out=sb["xqT"][:, kt, w4, :],
                                   in_=t["xq_sp"][512 * w4:512 * (w4 + 1),
                                                  128 * kt:128 * (kt + 1)],
                                   transpose=True), "ff_xqt", 16)
    blk.sync(sync3)

    DQT = [("wup", "wup_dq", 24, 1024), ("wgate", "wg_dq", 24, 1024),
           ("wdown", "wd_dq", 8, 3072)]

    def sync4(s):
        cx.wait(s, "ffxq", 16)
        n = 0
        for nm, dnm, ntile, width in DQT:
            for m in range(ntile):
                if n >= 2:
                    cx.wait(s, "dq35", n - 1)
                cx.inc(s.dma_start(out=sb["wdq_st"][:, n % 2, 0:width],
                                   in_=t[nm][128 * m:128 * (m + 1), :]), f"dq_ld{n % 2}", 16)
                n += 1
    blk.sync(sync4)

    def vec4(v):
        n = 0
        for nm, dnm, ntile, width in DQT:
            nb = width // 32
            for m in range(ntile):
                cx.wait(v, f"dq_ld{n % 2}", 16 * (n // 2 + 1))
                if n >= 2:
                    cx.wait(v, "dqs", n - 1)
                wsrc = sb["wdq_st"][:, n % 2, 0:width]
                cx.inc(v.tensor_reduce(out=sb["bmx"][:, n % 2, 0:nb],
                                       in_=wsrc.rearrange("p (b c) -> p b c", c=32),
                                       axis=AX.X, op=ALU.max,
                                       apply_absolute_value=True), "dqr")
                cx.wait(v, "dqs", n + 1)
                cx.inc(v.reciprocal(out=sb["bmx"][:, n % 2, 192:192 + nb],
                                    in_=sb["bmx"][:, n % 2, 96:96 + nb]), "dq1")
                cx.wait(v, "dq2", n + 1)
                cx.inc(v.tensor_scalar(out=sb["wdq_st"][:, n % 2, 0:width],
                                       in0=sb["wdq_st"][:, n % 2, 0:width], scalar1=MAGIC,
                                       scalar2=MAGIC, op0=ALU.add, op1=ALU.subtract),
                       "dq3")
                n += 1
    blk.vector(vec4)

    def gps4(g):
        n = 0
        for nm, dnm, ntile, width in DQT:
            nb = width // 32
            for m in range(ntile):
                cx.wait(g, "dqr", n + 1)
                cx.inc(g.tensor_scalar(out=sb["bmx"][:, n % 2, 96:96 + nb],
                                       in0=sb["bmx"][:, n % 2, 0:nb],
                                       scalar1=1.0 / 7.0, scalar2=1e-12,
                                       op0=ALU.mult, op1=ALU.max), "dqs")
                cx.wait(g, "dq1", n + 1)
                cx.inc(g.tensor_tensor(
                    out=sb["wdq_st"][:, n % 2, 0:width].rearrange("p (b c) -> p b c", c=32),
                    in0=sb["wdq_st"][:, n % 2, 0:width].rearrange("p (b c) -> p b c", c=32),
                    in1=sb["bmx"][:, n % 2, 192:192 + nb, None].broadcast_to([128, nb, 32]),
                    op=ALU.mult), "dq2")
                cx.wait(g, "dq3", n + 1)
                if n >= 2:
                    cx.wait(g, "f_ug0", 16 * (n - 1))
                cx.inc(g.tensor_tensor(
                    out=sb["wq_bf"][:, n % 2, 0:width].rearrange("p (b c) -> p b c", c=32),
                    in0=sb["wdq_st"][:, n % 2, 0:width].rearrange("p (b c) -> p b c", c=32),
                    in1=sb["bmx"][:, n % 2, 96:96 + nb, None].broadcast_to([128, nb, 32]),
                    op=ALU.mult), "dq35")
                cx.inc(g.dma_start(out=t[dnm][128 * m:128 * (m + 1), :],
                                   in_=sb["wq_bf"][:, n % 2, 0:width]), "f_ug0", 16)
                n += 1
    blk.gpsimd(gps4)

    def sync5(s):
        nmm = 0
        for te2 in range(2):
            dnm = ("wup_dq", "wg_dq")[te2]
            cx.wait(s, "f_ug0", 16 * 24 * (te2 + 1))
            for m in range(24):
                if nmm >= 2:
                    cx.wait(s, "ffmm", 4 * (nmm - 1))
                for kt in range(8):
                    cx.inc(s.dma_start(out=sb["wuT"][:, nmm % 2, kt, :],
                                       in_=t[dnm][128 * m:128 * (m + 1),
                                                  128 * kt:128 * (kt + 1)],
                                       transpose=True), f"ffw{nmm % 2}", 16)
                nmm += 1
    blk.sync(sync5)

    def ten5(te):
        cx.wait(te, "ff_xqt", 16 * 32)
        cx.wait(te, "op_ev")
        nmm = 0
        for te2 in range(2):
            for m in range(24):
                cx.wait(te, f"ffw{nmm % 2}", 16 * 8 * (nmm // 2 + 1))
                for w4 in range(4):
                    idx = nmm * 4 + w4
                    ps = sb["p_a"] if idx % 2 == 0 else sb["p_b"]
                    if idx >= 2:
                        cx.wait(te, "ffev", idx - 1)
                    mm = None
                    for kt in range(8):
                        mm = te.matmul(ps[:, :], sb["wuT"][:, nmm % 2, kt, :],
                                       sb["xqT"][:, kt, w4, :],
                                       start=(kt == 0), stop=(kt == 7))
                    cx.inc(mm, "ffmm")
                nmm += 1
    blk.tensor(ten5)

    def act5(s):
        nmm = 0
        for te2 in range(2):
            for m in range(24):
                for w4 in range(4):
                    idx = nmm * 4 + w4
                    cx.wait(s, "ffmm", idx + 1)
                    if nmm >= 2:
                        cx.wait(s, "f_ug1", 16 * (nmm - 1))
                    ps = sb["p_a"] if idx % 2 == 0 else sb["p_b"]
                    cx.inc(s.activation(out=sb["upgb"][:, nmm % 2, 512 * w4:512 * (w4 + 1)],
                                        in_=ps[:, :], func=AF.Identity, scale=-1.0),
                           "ffev")
                nmm += 1
    blk.scalar(act5)

    def vec5(v):
        nmm = 0
        for te2 in range(2):
            for m in range(24):
                cx.wait(v, "ffev", 4 * (nmm + 1))
                jk = sb["junk"][:, :, :].rearrange("p a b -> p (a b)")
                v.tensor_scalar(out=jk, in0=sb["upgb"][:, nmm % 2, :],
                                scalar1=-1.0, scalar2=None, op0=ALU.mult, op1=ALU.max,
                                accum_out=sb["fmx"][:, 2 + 2 * te2, m:m + 1])
                cx.inc(v.tensor_scalar(out=jk, in0=sb["upgb"][:, nmm % 2, :],
                                       scalar1=1.0, scalar2=None, op0=ALU.mult,
                                       op1=ALU.max,
                                       accum_out=sb["fmx"][:, 3 + 2 * te2, m:m + 1]),
                       "ffmxp")
                nmm += 1
    blk.vector(vec5)

    def gps5(g):
        nmm = 0
        for te2 in range(2):
            dnm = ("up_sp", "gate_sp")[te2]
            for m in range(24):
                cx.wait(g, "ffmxp", nmm + 1)
                cx.inc(g.dma_start(out=t[dnm][128 * m:128 * (m + 1), :],
                                   in_=sb["upgb"][:, nmm % 2, :]), "f_ug1", 16)
                nmm += 1
    blk.gpsimd(gps5)

    if tapped(cx, "t_ffn2"):
        cx.dout("t_uppre", [FI, TOK], BF16)
        cx.dout("t_gpre", [FI, TOK], BF16)
        cx.dout("t_wdq", [FI, D], BF16)
        def syncTf2(s):
            cx.wait(s, "f_ug1")
            cx.inc(s.dma_start(out=t["t_uppre"][:, :], in_=t["up_sp"][:, :]), "taps", 16)
            cx.inc(s.dma_start(out=t["t_gpre"][:, :], in_=t["gate_sp"][:, :]), "taps", 16)
            cx.inc(s.dma_start(out=t["t_wdq"][:, :], in_=t["wup_dq"][:, :]), "taps", 16)
        blk.sync(syncTf2)

    armax2(1, [2, 3, 4, 5], 6,
           mkpost(1, [(0, 1), (2, 3)], 1.0 / 32767.0, 1e-12, -1.0, 1.0),
           ("ffmxp", 48))

    def sync6(s):
        for m in range(24):
            cx.wait(s, "arB1", 16)
            if m >= 2:
                cx.wait(s, "f_upq", 16 * (m - 1))
            cx.inc(s.dma_start(out=sb["fbf"][:, m % 2, :],
                               in_=t["up_sp"][128 * m:128 * (m + 1), :]), f"upr{m % 2}", 16)
    blk.sync(sync6)

    def act6(s):
        for m in range(24):
            cx.wait(s, f"upr{m % 2}", 16 * (m // 2 + 1))
            s.activation(out=sb["fft"][:, m % 2, :], in_=sb["fbf"][:, m % 2, :],
                         func=AF.Identity, scale=sb["fsc"][:, 8:9],
                         bias=sb["magt"][:, 0:1])
            cx.inc(s.activation(out=sb["fbf"][:, m % 2, :], in_=sb["fft"][:, m % 2, :],
                                func=AF.Identity, scale=sb["fsc"][:, 9:10],
                                bias=sb["fsc"][:, 10:11]), "ffupq")
    blk.scalar(act6)

    def gps6(g):
        for m in range(24):
            cx.wait(g, "ffupq", m + 1)
            cx.inc(g.dma_start(out=t["up_sp"][128 * m:128 * (m + 1), :],
                               in_=sb["fbf"][:, m % 2, :]), "f_upq", 16)
    blk.gpsimd(gps6)

    def sync7(s):
        for m in range(24):
            cx.wait(s, "arB1", 16)
            if m >= 2:
                cx.wait(s, "f_g2", 16 * (m - 1))
            cx.inc(s.dma_start(out=sb["fbf2"][:, m % 2, :],
                               in_=t["gate_sp"][128 * m:128 * (m + 1), :]), f"gtr{m % 2}", 16)
    blk.sync(sync7)

    def act7(s):
        for m in range(24):
            cx.wait(s, f"gtr{m % 2}", 16 * (m // 2 + 1))
            s.activation(out=sb["fft"][:, m % 2, :], in_=sb["fbf2"][:, m % 2, :],
                         func=AF.Identity, scale=sb["fsc"][:, 11:12],
                         bias=sb["magt"][:, 0:1])
            s.activation(out=sb["fbf2"][:, m % 2, :], in_=sb["fft"][:, m % 2, :],
                         func=AF.Identity, scale=sb["fsc"][:, 12:13],
                         bias=sb["fsc"][:, 13:14])
            cx.inc(s.activation(out=sb["fsg"][:, m % 2, :], in_=sb["fbf2"][:, m % 2, :],
                                func=AF.Sigmoid), "ffsig")
    blk.scalar(act7)

    def gps7(g):
        for m in range(24):
            cx.wait(g, "ffsgq", m + 1)
            cx.inc(g.tensor_tensor(out=sb["fbf2"][:, m % 2, :], in0=sb["fbf2"][:, m % 2, :],
                                   in1=sb["fsg"][:, m % 2, :], op=ALU.mult), "ffg2p")
            cx.wait(g, "ffg2m", m + 1)
            cx.inc(g.dma_start(out=t["gate_sp"][128 * m:128 * (m + 1), :],
                               in_=sb["fbf2"][:, m % 2, :]), "f_g2", 16)
    blk.gpsimd(gps7)

    def vec7(v):
        def sigfix(m):
            cx.wait(v, "ffsig", m + 1)
            v.tensor_scalar(out=sb["fsg"][:, m % 2, :], in0=sb["fsg"][:, m % 2, :],
                            scalar1=65536.0, scalar2=None, op0=ALU.mult)
            v.tensor_scalar(out=sb["fsg"][:, m % 2, :], in0=sb["fsg"][:, m % 2, :],
                            scalar1=MAGIC, scalar2=MAGIC, op0=ALU.add, op1=ALU.subtract)
            cx.inc(v.tensor_scalar(out=sb["fsg"][:, m % 2, :], in0=sb["fsg"][:, m % 2, :],
                                   scalar1=65535.0, scalar2=None, op0=ALU.min), "ffsgq")

        def g2max(m):
            cx.wait(v, "ffg2p", m + 1)
            jk = sb["junk"][:, :, :].rearrange("p a b -> p (a b)")
            v.tensor_scalar(out=jk, in0=sb["fbf2"][:, m % 2, :],
                            scalar1=1.0, scalar2=None, op0=ALU.mult, op1=ALU.max,
                            accum_out=sb["fmx"][:, 6, m:m + 1])
            cx.inc(v.tensor_scalar(out=jk, in0=sb["fbf2"][:, m % 2, :],
                                   scalar1=-1.0, scalar2=None, op0=ALU.mult, op1=ALU.max,
                                   accum_out=sb["fmx"][:, 7, m:m + 1]), "ffg2m")

        for m in range(24):
            sigfix(m)
            if m >= 1:
                g2max(m - 1)
        g2max(23)
    blk.vector(vec7)

    armax2(2, [6, 7], 3,
           mkpost(2, [(0, 1)], 2.0 ** -16 / 32767.0, 1e-12, 1.0, 2.0 ** 16),
           ("ffg2m", 24))

    def sync8(s):
        for m in range(24):
            cx.wait(s, "arB2", 16)
            cx.wait(s, "f_upq", 16 * min(m + 1, 24))
            if m >= 2:
                cx.wait(s, "f_op", 16 * (m - 1))
            cx.inc(s.dma_start(out=sb["fbf2"][:, m % 2, :],
                               in_=t["gate_sp"][128 * m:128 * (m + 1), :]), f"g2r{m % 2}", 16)
            cx.inc(s.dma_start(out=sb["fbf"][:, m % 2, :],
                               in_=t["up_sp"][128 * m:128 * (m + 1), :]), f"g2r{m % 2}", 16)
    blk.sync(sync8)

    def act8(s):
        for m in range(24):
            cx.wait(s, f"g2r{m % 2}", 32 * (m // 2 + 1))
            s.activation(out=sb["fft"][:, m % 2, :], in_=sb["fbf2"][:, m % 2, :],
                         func=AF.Identity, scale=sb["fsc"][:, 8:9],
                         bias=sb["magt"][:, 0:1])
            cx.inc(s.activation(out=sb["fbf2"][:, m % 2, :], in_=sb["fft"][:, m % 2, :],
                                func=AF.Identity, scale=sb["fsc"][:, 9:10],
                                bias=sb["fsc"][:, 10:11]), "ffg2q")
    blk.scalar(act8)

    def gps8(g):
        for m in range(24):
            cx.wait(g, "ffg2q", m + 1)
            cx.inc(g.tensor_tensor(out=sb["fbf2"][:, m % 2, :], in0=sb["fbf2"][:, m % 2, :],
                                   in1=sb["fbf"][:, m % 2, :], op=ALU.mult), "ffopre")
            cx.wait(g, "ffoprm", m + 1)
            cx.inc(g.dma_start(out=t["opre_sp"][128 * m:128 * (m + 1), :],
                               in_=sb["fbf2"][:, m % 2, :]), "f_op", 16)
    blk.gpsimd(gps8)

    def vec8(v):
        for m in range(24):
            cx.wait(v, "ffopre", m + 1)
            jk = sb["junk"][:, :, :].rearrange("p a b -> p (a b)")
            v.tensor_scalar(out=jk, in0=sb["fbf2"][:, m % 2, :],
                            scalar1=1.0, scalar2=None, op0=ALU.mult, op1=ALU.max,
                            accum_out=sb["fmx"][:, 8, m:m + 1])
            cx.inc(v.tensor_scalar(out=jk, in0=sb["fbf2"][:, m % 2, :],
                                   scalar1=-1.0, scalar2=None, op0=ALU.mult, op1=ALU.max,
                                   accum_out=sb["fmx"][:, 9, m:m + 1]), "ffoprm")
    blk.vector(vec8)

    armax2(3, [8, 9], 3, mkpost(3, [(0, 1)], 1.0 / 32767.0, 1e-12, 1.0, 1.0),
           ("ffoprm", 24))

    if tapped(cx, "t_ffn"):
        cx.dout("t_xq", [TOK, D], BF16)
        cx.dout("t_r1", [TOK, D])
        cx.dout("t_up", [FI, TOK], BF16)
        cx.dout("t_g2", [FI, TOK], BF16)
        cx.dout("t_opre", [FI, TOK], BF16)
        def syncTf(s):
            cx.wait(s, "f_xq")
            cx.inc(s.dma_start(out=t["t_xq"][:, :], in_=t["xq_sp"][:, :]), "taps", 16)
            cx.wait(s, "f_r1")
            cx.inc(s.dma_start(out=t["t_r1"][:, :], in_=t["r1_sp"][:, :]), "taps", 16)
            cx.wait(s, "f_upq")
            cx.inc(s.dma_start(out=t["t_up"][:, :], in_=t["up_sp"][:, :]), "taps", 16)
            cx.wait(s, "f_g2")
            cx.inc(s.dma_start(out=t["t_g2"][:, :], in_=t["gate_sp"][:, :]), "taps", 16)
            cx.wait(s, "f_op")
            cx.inc(s.dma_start(out=t["t_opre"][:, :], in_=t["opre_sp"][:, :]), "taps", 16)
        blk.sync(syncTf)

    cx.barrier(blk)
    cx.retire("ff_xqt", "dq_ld0", "dq_ld1", "dq1", "dq2", "dq3", "dq35", "dqr", "dqs",
              "ffw0", "ffw1", "ffmm", "ffmxp", "arc1",
              "upr0", "upr1", "ffupq", "gtr0", "gtr1", "ffsig", "ffsgq",
              "ffg2p", "ffg2m", "arc2", "g2r0",
              "g2r1", "ffg2q", "ffopre", "ffoprm", "arc3", "op_ev")
    sfree("fsg", "fbf2", "fbf", "bmx", "wq_bf", "wdq_st", "upgb", "wuT", "xqT")
    sget("wdT", [128, 24, D], BF16)
    sget("oqg", [128, 24, 512], BF16)
    sget("fq", [128, 2, 512], BF16)
    sget("fr1", [128, 2, D], F32)
    sget("fdo", [128, 16, D], BF16)
    sget("fob", [128, 2, D], F32)

    def sync9(s):
        cx.wait(s, "f_ug0", 16 * 56)
        cx.wait(s, "f_op", 16 * 24)
        for kc in range(24):
            cx.inc(s.dma_start(out=sb["wdT"][:, kc, :],
                               in_=t["wd_dq"][:, 128 * kc:128 * (kc + 1)],
                               transpose=True), "ffwd", 16)
        for tg in range(4):
            for kc in range(24):
                n = tg * 24 + kc
                cx.wait(s, "arB3", 16)
                if n >= 2:
                    cx.wait(s, "ffoq", n - 1)
                cx.inc(s.dma_start(out=sb["fq"][:, n % 2, :],
                                   in_=t["opre_sp"][128 * kc:128 * (kc + 1),
                                                    512 * tg:512 * (tg + 1)]),
                       f"fq{n % 2}", 16)
        for i in range(NT):
            cx.wait(s, "f_r1", 16 * 16)
            if i >= 2:
                cx.wait(s, "out_st", 16 * (i - 1))
            cx.inc(s.dma_start(out=sb["fr1"][:, i % 2, :],
                               in_=t["r1_sp"][128 * i:128 * (i + 1), :]), f"fr{i % 2}", 16)
    blk.sync(sync9)

    def act9(s):
        for tg in range(4):
            for kc in range(24):
                n = tg * 24 + kc
                cx.wait(s, f"fq{n % 2}", 16 * (n // 2 + 1))
                if tg >= 1 and kc == 0:
                    cx.wait(s, "ffdmm", 8 * tg)
                s.activation(out=sb["fft"][:, n % 2, 0:512], in_=sb["fq"][:, n % 2, :],
                             func=AF.Identity, scale=sb["fsc"][:, 8:9],
                             bias=sb["magt"][:, 0:1])
                cx.inc(s.activation(out=sb["oqg"][:, kc, :], in_=sb["fft"][:, n % 2, 0:512],
                                    func=AF.Identity, scale=sb["fsc"][:, 9:10],
                                    bias=sb["fsc"][:, 10:11]), "ffoq")
    blk.scalar(act9)

    def ten9(te):
        cx.wait(te, "ffwd", 16 * 24)
        cx.wait(te, "ffev")
        for tg in range(4):
            for tsub in range(4):
                i = tg * 4 + tsub
                cx.wait(te, "ffoq", 24 * (tg + 1))
                if i >= 1:
                    cx.wait(te, "ffdev", 2 * i)
                mma = mmb = None
                for kc in range(24):
                    lh = sb["oqg"][:, kc, 128 * tsub:128 * (tsub + 1)]
                    mma = te.matmul(sb["p_a"][:, :], lh, sb["wdT"][:, kc, 0:512],
                                    start=(kc == 0), stop=(kc == 23))
                    mmb = te.matmul(sb["p_b"][:, :], lh, sb["wdT"][:, kc, 512:1024],
                                    start=(kc == 0), stop=(kc == 23))
                cx.inc(mmb, "ffdmm", 2)
    blk.tensor(ten9)

    def vec9(v):
        for i in range(NT):
            cx.wait(v, "ffdmm", 2 * (i + 1))
            v.tensor_copy(out=sb["fdo"][:, i, 0:512], in_=sb["p_a"][:, :])
            cx.inc(v.tensor_copy(out=sb["fdo"][:, i, 512:1024], in_=sb["p_b"][:, :]),
                   "ffdev", 2)
    blk.vector(vec9)

    def vec10(v):
        for i in range(NT):
            sc4 = sb["lnsc2"][:, i, 0:8]
            lnh_vec_a(cx, v, sb["fdo"][:, i, :], sc4, "l4", i)
            lnh_vec_b(cx, v, sc4, "l4", i)
            lnh_vec_c(cx, v, sc4, "l4", i)
            cx.wait(v, "l4f", i + 1)
            cx.wait(v, f"fr{i % 2}", 16 * (i // 2 + 1))
            cx.inc(v.tensor_add(out=sb["fr1"][:, i % 2, :], in0=sb["fr1"][:, i % 2, :],
                                in1=sb["fob"][:, i % 2, :]), "ffout")
    blk.vector(vec10)

    def act10(s):
        for i in range(NT):
            sc4 = sb["lnsc2"][:, i, 0:8]
            lnh_act_a(cx, s, sc4, "l4", i)
            lnh_act_b(cx, s, sc4, "l4", i)
            if i >= 2:
                cx.wait(s, "ffout", i - 1)
            cx.inc(lnh_act_apply(cx, s, sb["fdo"][:, i, :], sb["fob"][:, i % 2, :],
                                 sc4, "l4", i), "l4f")
    blk.scalar(act10)

    def gps9(g):
        for i in range(NT):
            lnh_gps_nb(cx, g, sb["lnsc2"][:, i, 0:8], "l4", i)
            cx.wait(g, "ffout", i + 1)
            cx.inc(g.dma_start(out=t["out"][128 * i:128 * (i + 1), :],
                               in_=sb["fr1"][:, i % 2, :]), "out_st", 16)
    blk.gpsimd(gps9)


def mkap_sb(handle_slice, n):
    return handle_slice[0:n, 0:1]


def emit_tail(cx, blk):
    def sync(s):
        for nm in ("taps", "pj_sp", "pj_z", "gm_st", "y_done",
                   "out_st", "ag1", "ag2", "pj_bt", "ld", "ldg",
                   "fc_st", "fir_sp0", "fir_sp1", "sc_ost",
                   "sc_spn0", "sc_spn1", "rs", "f_r1", "f_xq", "f_ug0", "f_ug1",
                   "f_g2", "f_op", "f_upq"):
            if cx.n(nm) > 0:
                cx.wait(s, nm)
    blk.sync(sync)



def stage_inputs(core, inputs):
    b, j = core // 4, core % 4
    h = j
    hs = slice(256 * h, 256 * (h + 1))
    g = lambda n: np.asarray(inputs[n])
    m = {}
    x = g("L0_attn_hidden_states")
    m["x_slice"] = np.ascontiguousarray(x[b, TOK * j:TOK * (j + 1), :], np.float32)
    bundle = np.concatenate([
        g("L0_attn_q_proj_w")[hs], g("L0_attn_k_proj_w")[hs],
        g("L0_attn_v_proj_w")[hs], g("L0_attn_b_proj_w")[h:h + 1],
        np.zeros((3, D), np.float32)], 0)
    m["wqkvb_t"] = np.ascontiguousarray(bundle.T, np.float32)
    m["convw"] = np.ascontiguousarray(np.concatenate([
        g("L0_attn_q_conv_w")[hs], g("L0_attn_k_conv_w")[hs],
        g("L0_attn_v_conv_w")[hs]], 0), np.float32)
    m["gw1t"] = np.ascontiguousarray(g("L0_attn_gmix_w1").T, np.float32)
    m["gb1"] = np.ascontiguousarray(g("L0_attn_gmix_b1").reshape(16, 128).T, np.float32)
    m["gw2t"] = np.ascontiguousarray(g("L0_attn_gmix_w2").T, np.float32)
    m["gb2"] = np.ascontiguousarray(g("L0_attn_gmix_b2").reshape(16, 1), np.float32)
    m["alpha_h"] = np.ascontiguousarray(g("L0_attn_alpha")[h].reshape(1, 1), np.float32)
    m["onw"] = np.ascontiguousarray(g("L0_attn_o_norm_w").reshape(2, 128).T, np.float32)
    m["wot"] = np.ascontiguousarray(g("L0_attn_o_proj_w")[:, hs].T, np.float32)
    pad_s = np.zeros((DV, 384), np.float32)
    pad_s[:, 128:135] = g("L0_attn_fir_short_w")[h][:, ::-1]
    pad_l = np.zeros((DV, 384), np.float32)
    pad_l[:, 128:159] = g("L0_attn_fir_long_w")[h][:, ::-1]
    m["fir_s_pad"], m["fir_l_pad"] = pad_s, pad_l
    m["wup"] = np.ascontiguousarray(g("L0_ffn_w_up"), np.float32)
    m["wgate"] = np.ascontiguousarray(g("L0_ffn_w_gate"), np.float32)
    m["wdown"] = np.ascontiguousarray(g("L0_ffn_w_down"), np.float32)
    hsel = np.zeros((128, 4), np.float32); hsel[:, h] = 1.0
    m["hsel"] = hsel
    m["ident"] = np.eye(128, dtype=ml_dtypes.bfloat16)
    m["m_sl_neg"] = (-np.tril(np.ones((128, 128)), -1)).astype(ml_dtypes.bfloat16)
    m["m_su_neg"] = (-np.triu(np.ones((128, 128)), 1)).astype(ml_dtypes.bfloat16)
    m["m_u_incl"] = np.triu(np.ones((128, 128)), 0).astype(ml_dtypes.bfloat16)
    return m


def kernel(**inputs):
    nc = build_program(upto="ffn")
    in_maps = [stage_inputs(c, inputs) for c in range(8)]
    res = run_bass_kernel_spmd(nc, in_maps, core_ids=list(range(8)))
    out = np.zeros((B, L, D), np.float32)
    for c in range(8):
        b, j = c // 4, c % 4
        out[b, TOK * j:TOK * (j + 1), :] = res.results[c]["out"]
    return out


# version 07
# speedup vs baseline: 1.0032x; 1.0032x over previous
import numpy as np
import ml_dtypes
import concourse.bass as bass
import concourse.mybir as mybir
from concourse.bass_utils import run_bass_kernel_spmd

F32 = mybir.dt.float32
BF16 = mybir.dt.bfloat16
I32 = mybir.dt.int32
AF = mybir.ActivationFunctionType
ALU = mybir.AluOpType
AX = mybir.AxisListType

B, L, D, H = 2, 8192, 1024, 4
DK = DV = 256
GH = 2048
FI = 3072
TOK = 2048
C = 128
NCH = L // C
MAGIC = 12582912.0
GROUPS = [[0, 1, 2, 3], [4, 5, 6, 7]]
ALL8 = [list(range(8))]
PHASE_ALL = ("boot", "ln1", "gmix", "proj", "fir", "scan", "branch", "oproj", "ffn")


class Ctx:
    POOL = 42

    def __init__(self, nc):
        self.nc = nc
        self.t = {}
        self.sb = {}
        self.sem_h = {}
        self.sem_base = {}
        self.sem_n = {}
        self.phys = []
        self.phys_tot = {}
        self.rr = 0

    EXCL = ("y_done", "gm_w", "pj_sp", "pj_z",
            "gm_st", "fc_ld", "fc_st", "fir_sp0", "fir_sp1", "sc_spn0", "sc_spn1",
            "sc_ost", "br_ost0", "br_ost1", "op_st",
            "ldg", "ag1", "ag2", "rs", "ar0", "ar1", "ar2", "ar3",
            "f_xq", "f_ug0", "f_ug1", "f_g2", "f_op", "f_r1", "f_upq",
            "arf0", "arf1", "arf2", "arf3", "arq0", "arq1", "arq2", "arq3",
            "arg0", "arg1", "arg2", "arg3", "arB0", "arB1", "arB2", "arB3",
            "out_st", "barA", "barD", "barG")

    def sem(self, name):
        if name not in self.sem_h:
            assert name not in getattr(self, "retired", set()), f"{name} used after retire"
            if name in self.EXCL:
                h = self.nc.semaphore("x_" + name).__enter__()
                self.sem_h[name] = h
                self.sem_base[name] = 0
                self.sem_n[name] = 0
                self.phys_tot[h.num] = 0
                return h
            if not hasattr(self, "free_h"):
                self.free_h = []
                self.tenants = {}
                self.retired = set()
                self.cooling = []
            if self.free_h:
                h = self.free_h.pop()
            else:
                h = self.nc.semaphore(f"s{len(self.phys)}").__enter__()
                self.phys.append(h)
                self.phys_tot[h.num] = 0
            self.tenants.setdefault(h.num, set()).add(name)
            self.sem_h[name] = h
            self.sem_base[name] = self.phys_tot[h.num]
            self.sem_n[name] = 0
        return self.sem_h[name]

    def retire(self, *names):
        if not hasattr(self, "free_h"):
            self.free_h = []
            self.tenants = {}
            self.retired = set()
            self.cooling = []
        self.free_h.extend(self.cooling)
        self.cooling = []
        for name in names:
            if name not in self.sem_h or name in self.EXCL:
                continue
            self.retired.add(name)
            h = self.sem_h[name]
            tn = self.tenants.get(h.num, set())
            tn.discard(name)
            if not tn and h not in self.cooling and h not in self.free_h:
                self.cooling.append(h)

    def barrier(self, blk):
        self._bar = getattr(self, "_bar", 0) + 1
        k = self._bar
        cx = self

        def scl(s):
            cx.inc(s.copy(out=cx.sb["bscr"][0:1, 0:1], in_=cx.sb["bscr"][0:1, 1:2]),
                   "barA")
            s.wait_ge(cx.sem("barA"), 2 * k)
            s.wait_ge(cx.sem("barD"), 16 * k)
            s.wait_ge(cx.sem("barG"), 16 * k)
        blk.scalar(scl)

        def vec(v):
            cx.inc(v.tensor_copy(out=cx.sb["bscr"][0:1, 2:3],
                                 in_=cx.sb["bscr"][0:1, 3:4]), "barA")
            v.wait_ge(cx.sem("barA"), 2 * k)
            v.wait_ge(cx.sem("barD"), 16 * k)
            v.wait_ge(cx.sem("barG"), 16 * k)
        blk.vector(vec)

        def syn(s):
            cx.inc(s.dma_start(out=cx.t["bscrD"][0:1, :], in_=cx.t["bscrD"][1:2, :]),
                   "barD", 16)
            s.wait_ge(cx.sem("barA"), 2 * k)
            s.wait_ge(cx.sem("barD"), 16 * k)
            s.wait_ge(cx.sem("barG"), 16 * k)
        blk.sync(syn)

        def gps(g):
            cx.inc(g.dma_start(out=cx.t["bscrG"][0:1, :], in_=cx.t["bscrG"][1:2, :]),
                   "barG", 16)
            g.wait_ge(cx.sem("barA"), 2 * k)
            g.wait_ge(cx.sem("barD"), 16 * k)
            g.wait_ge(cx.sem("barG"), 16 * k)
        blk.gpsimd(gps)

    def inc(self, instr, name, k=1):
        h = self.sem(name)
        instr.then_inc(h, k)
        self.sem_n[name] += k
        self.phys_tot[h.num] += k
        return instr

    def wait(self, eng, name, n=None):
        h = self.sem(name)
        eng.wait_ge(h, self.sem_base[name] + (self.sem_n[name] if n is None else n))

    def n(self, name):
        return self.sem_n.get(name, 0)


def build_program(upto="ffn", taps=()):
    nc = bass.Bass("TRN2", target_bir_lowering=False, debug=False, detect_race_conditions=False)
    cx = Ctx(nc)
    t, sb = cx.t, cx.sb
    cx.tapset = set(taps)
    phases = PHASE_ALL[:PHASE_ALL.index(upto) + 1]

    def din(name, shape, dt=F32):
        t[name] = nc.dram_tensor(name, shape, dt, kind="ExternalInput")

    def dout(name, shape, dt=F32):
        t[name] = nc.dram_tensor(name, shape, dt, kind="ExternalOutput")

    def dscr(name, shape, dt=F32):
        t[name] = nc.dram_tensor(name, shape, dt)

    def sbuf(name, shape, dt=F32):
        sb[name] = nc.sbuf_tensor(name, shape, dt).__enter__()

    def psum(name, shape, dt=F32):
        sb[name] = nc.psum_tensor(name, shape, dt).__enter__()

    cx.dout = dout

    din("x_slice", [TOK, D])
    din("wqkvb_t", [D, 772])
    din("convw", [768, 4])
    din("gw1t", [D, GH])
    din("gb1", [128, 16])
    din("gw2t", [GH, 16])
    din("gb2", [16, 1])
    din("alpha_h", [1, 1])
    din("onw", [128, 2])
    din("wot", [DV, D])
    din("fir_s_pad", [DV, 384])
    din("fir_l_pad", [DV, 384])
    din("wup", [FI, D])
    din("wgate", [FI, D])
    din("wdown", [D, FI])
    din("hsel", [128, 4])
    din("ident", [128, 128], BF16)
    din("m_sl_neg", [128, 128], BF16)
    din("m_su_neg", [128, 128], BF16)
    din("m_u_incl", [128, 128], BF16)
    dout("out", [TOK, D])

    dscr("ag1_in", [TOK, D], BF16)
    dscr("ag1_out", [L, D], BF16)
    dscr("ag2_in", [32, TOK])
    dscr("ag2_out", [128, TOK])
    dscr("rs_in", [L, D])
    dscr("rs_out", [TOK, D])
    for i in range(4):
        dscr(f"ar{i}_in", [1, 4])
        dscr(f"ar{i}_out", [1, 4])
    dscr("q_sp", [DK, L], BF16)
    dscr("k_sp", [DK, L], BF16)
    dscr("v_sp", [DV, 65 * 128], BF16)
    dscr("qh_tm", [L, DK], BF16)
    dscr("kh_tm", [L, DK], BF16)
    dscr("kb_tm", [L, DK], BF16)
    dscr("od_tm", [L, DV], BF16)
    dscr("fs_sp", [DV, L], BF16)
    dscr("fl_sp", [DV, L], BF16)
    dscr("o_tm", [L, DV], BF16)
    dscr("bandsD", [2 * 16 * 128 * 16 * 256], BF16)
    dscr("xq_sp", [TOK, D], BF16)
    dscr("up_sp", [FI, TOK], BF16)
    dscr("gate_sp", [FI, TOK], BF16)
    dscr("opre_sp", [FI, TOK], BF16)
    dout("wup_dq", [FI, D], BF16)
    dout("wg_dq", [FI, D], BF16)
    dout("wd_dq", [D, FI], BF16)
    dscr("r1_sp", [TOK, D])
    dscr("flat_sc", [8, 128])
    dscr("bscrD", [2, 8])
    dscr("bscrG", [2, 8])

    sbuf("bscr", [1, 8])
    sbuf("k_ident", [128, 128], BF16)
    sbuf("k_msl", [128, 128], BF16)
    sbuf("k_msu", [128, 128], BF16)
    sbuf("k_mui", [128, 128], BF16)
    sbuf("zt", [128, 128], BF16)
    sbuf("wqkvb", [128, 8, 772], BF16)
    sbuf("convw_s", [128, 6, 4], F32)
    sbuf("gw2", [128, 16, 16], BF16)
    sbuf("gb1_s", [128, 16, 1], F32)
    sbuf("gb2_s", [16, 1], F32)
    sbuf("alpha_s", [1, 1], F32)
    sbuf("hsel_s", [128, 4])
    sbuf("onw_s", [128, 2, 1], F32)
    sbuf("junk", [128, 2, D], BF16)
    sbuf("fpbf", [128, 4, 384], BF16)
    sbuf("beta_t", [128, NCH], F32)
    sbuf("k_eps", [128, 1], F32)
    sbuf("abc", [128, 1], F32)
    sbuf("magt", [128, 1], F32)
    psum("p_a", [128, 512], F32)
    psum("p_b", [128, 512], F32)
    cx.psum_cms = {}
    def pget(name, shape):
        cm = nc.psum_tensor(name, shape, F32)
        sb[name] = cm.__enter__()
        cx.psum_cms[name] = cm
    def pfree(name):
        cx.psum_cms.pop(name).__exit__(None, None, None)
        del sb[name]
    cx.pget, cx.pfree = pget, pfree
    cx.sbuf_cms = {}
    def sget(name, shape, dt=F32):
        cm = nc.sbuf_tensor(name, shape, dt)
        sb[name] = cm.__enter__()
        cx.sbuf_cms[name] = cm
    def sfree(*names):
        for name in names:
            cx.sbuf_cms.pop(name).__exit__(None, None, None)
            del sb[name]
    cx.sget, cx.sfree = sget, sfree
    pget("p_bt", [4, 512])
    pget("p_g2", [16, TOK])
    pget("p_btp", [128, 8])

    for nm, shape, dt in [("x2", [128, 4, D], F32), ("y2", [128, 2, D], BF16),
                          ("lnsc", [128, 16, 4], F32), ("yt_w", [128, 8, 4, 512], BF16),
                          ("gw1b", [128, 8, 2, 128], BF16), ("gh", [128, 2, TOK], BF16),
                          ("lg32", [32, TOK], F32), ("lgT", [32, TOK], F32),
                          ("raw", [128, 6, 515], F32), ("cvo", [128, 6, 512], F32),
                          ("qkvb", [128, 2, 6, 512], BF16), ("brow", [1, L], BF16)]:
        sget(nm, shape, dt)
    with nc.Block() as blk:
        emit_boot(cx, blk)
        if "ln1" in phases:
            emit_ln1(cx, blk)
        if "gmix" in phases:
            emit_gmix(cx, blk)
        if "proj" in phases:
            cx.barrier(blk)
            cx.retire("x_ld0", "x_ld1", "x_ld2", "x_ld3", "ln_st", "ln_a", "ln_vb",
                      "ln_sq", "ln_nb", "y_cmp", "gm_yt", "gm_mm1", "gm_mm2")
            emit_proj(cx, blk)
        if "fir" in phases:
            cx.barrier(blk)
            cx.retire("gm_ev", "pj_yt0", "pj_yt1", "pj_yt2", "pj_yt3", "pj_mm",
                      "pj_ev", "pj_cv_v", "pj_bmm", "pj_btmm")
            cx.sfree("brow", "qkvb", "cvo", "raw", "lgT", "lg32", "gh",
                     "gw1b", "yt_w", "lnsc", "y2", "x2")
            sget("vtb", [128, 16640], BF16)
            sget("bands", [128, 2, 2, 16, 256], BF16)
            sget("fev", [128, 2, 2, 512], BF16)
            cx.pfree("p_btp"); cx.pfree("p_g2"); cx.pfree("p_bt")
            cx.pget("pf_s", [128, 512]); cx.pget("pf_l", [128, 512])
            cx.pget("pf_s2", [128, 512]); cx.pget("pf_l2", [128, 512])
            emit_fir(cx, blk)
        if "scan" in phases:
            cx.barrier(blk)
            cx.retire("fc_ld", "fc_cast", "fir_vt0", "fir_vt1", "fir_vt2", "fir_vt3",
                      "fir_bd0", "fir_bd1", "gm_lgT", "pj_si", "pj_bt", "fir_mm",
                      "fir_ev")
            sget("qkvt", [128, 4, 3, 256], BF16)
            sget("nrm", [128, 4, 4, 256], BF16)
            sget("fT", [128, 4, 6, 128], BF16)
            sget("sstat", [128, 4, 8], F32)
            sget("tb", [128, 8, 512], BF16)
            sget("rbuf", [128, 256], BF16)
            sget("ubuf", [128, 256], BF16)
            sget("obuf", [128, 2, 256], BF16)
            sget("sbf", [128, 2, 256], BF16)
            sget("sq256", [128, 256], F32)
            cx.pfree("pf_l2"); cx.pfree("pf_s2")
            cx.pfree("pf_l"); cx.pfree("pf_s")
            cx.pget("pT1", [128, 512]); cx.pget("pT2", [128, 512])
            cx.pget("pR", [128, 256]); cx.pget("pU", [128, 256])
            cx.pget("pO", [128, 256]); cx.pget("pS", [128, 512])
            emit_scan(cx, blk)
        if "branch" in phases:
            cx.barrier(blk)
            cx.retire("sc_tl0", "sc_tl1", "sc_tl2", "sc_tl3", "sc_fl0", "sc_fl1",
                      "sc_fl2", "sc_fl3", "sc_ss", "sc_sd", "sc_inv", "sc_nm",
                      "sc_tb", "sc_tbe", "sc_sb0", "sc_rmm", "sc_rev", "sc_umm",
                      "sc_uev", "sc_omm", "sc_oev", "sc_ds")
            cx.sfree("sq256", "sbf", "obuf", "ubuf", "rbuf", "tb", "sstat",
                     "fT", "nrm", "qkvt", "fev", "bands", "vtb")
            sget("ld16", [128, 64, 16], F32)
            sget("brt", [128, 4, 4, 256], BF16)
            sget("bst", [128, 64, 4, 4], F32)
            sget("wts", [128, 64, 4], F32)
            sget("stt2", [128, 64, 4, 2], F32)
            sget("obx", [128, 2, 256], BF16)
            sget("osc", [128, 4], F32)
            sget("sq2", [128, 512], F32)
            sget("oxb", [128, 2, 256], F32)
            sget("selb", [128, 64, 4])
            emit_branch(cx, blk)
        if "oproj" in phases:
            cx.barrier(blk)
            cx.retire("br_lg", "br_tl0", "br_tl1", "br_tl2", "br_tl3", "br_st",
                      "br_c1", "br_c2", "br_c3", "br_sm1", "br_sm2", "br_sm3",
                      "br_sm4", "br_wts", "br_t20", "br_t21", "br_t22", "br_t23",
                      "br_mx", "br_ms", "br_sd", "br_inv", "sc_scp", "br_on")
            sget("oT", [128, 4, 2, 128], BF16)
            sget("wotf", [128, 2, D], BF16)
            sget("wost", [128, 2, D], F32)
            sget("rsev", [128, 2, D], F32)
            emit_oproj(cx, blk)
        if "ffn" in phases:
            cx.barrier(blk)
            cx.retire("op_w", "op_wf", "op_tl0", "op_tl1", "op_tl2", "op_tl3",
                      "op_mm")
            cx.pfree("pS"); cx.pfree("pO"); cx.pfree("pU")
            cx.pfree("pR"); cx.pfree("pT2"); cx.pfree("pT1")
            emit_ffn(cx, blk)
        emit_tail(cx, blk)
    global _LAST_CX
    _LAST_CX = cx
    return nc


def tapped(cx, name):
    return name in cx.tapset


def emit_boot(cx, blk):
    nc, t, sb = cx.nc, cx.t, cx.sb

    def sync(s):
        for nm, dst in [("ident", "k_ident"), ("m_sl_neg", "k_msl"),
                        ("m_su_neg", "k_msu"), ("m_u_incl", "k_mui")]:
            cx.inc(s.dma_start(out=sb[dst][:, :], in_=t[nm][:, :]), "ld", 16)
        cx.inc(s.dma_start(out=sb["gb2_s"][:, :], in_=t["gb2"][:, :]), "ld", 16)
        cx.inc(s.dma_start(out=sb["alpha_s"][:, :], in_=t["alpha_h"][:, :]), "ld", 16)
        cx.inc(s.dma_start(out=sb["hsel_s"][:, :], in_=t["hsel"][:, :]), "ld", 16)
        cx.inc(s.dma_start(out=sb["abc"][:, :],
                           in_=t["alpha_h"][:, :].to_broadcast((128, 1))), "ld", 16)
        cx.inc(s.dma_start(out=sb["onw_s"][:, :, 0], in_=t["onw"][:, :]), "ld", 16)
        cx.inc(s.dma_start(out=sb["convw_s"][:, :, :],
                           in_=t["convw"][:, :].rearrange("(m p) k -> p m k", p=128)), "ld", 16)
        cx.inc(s.dma_start(out=sb["gb1_s"][:, :, 0], in_=t["gb1"][:, :]), "ld", 16)
    blk.sync(sync)

    def vec(v):
        cx.inc(v.memset(sb["zt"][:, :], 0.0), "zt")
        v.memset(sb["k_eps"][:, :], 1e-5)
        cx.inc(v.memset(sb["magt"][:, :], MAGIC), "zt")
    blk.vector(vec)

    def gps(g):
        cx.inc(g.dma_start(
            out=sb["wqkvb"][:, :, :],
            in_=t["wqkvb_t"][:, :].rearrange("(kt p) m -> p kt m", p=128)), "ldg", 16)
        cx.inc(g.dma_start(
            out=sb["gw2"][:, :, :],
            in_=t["gw2t"][:, :].rearrange("(kt p) m -> p kt m", p=128)), "ldg", 16)
    blk.gpsimd(gps)

    GSZ = 128 * 16 * 256
    FIRSZ = 16 * GSZ

    def gpsF(g):
        cx.inc(g.dma_start(out=sb["fpbf"][:, 0:2, :],
                           in_=t["fir_s_pad"][:, :].rearrange("(a p) c -> p a c", p=128)),
               "fc_ld", 16)
        cx.inc(g.dma_start(out=sb["fpbf"][:, 2:4, :],
                           in_=t["fir_l_pad"][:, :].rearrange("(a p) c -> p a c", p=128)),
               "fc_ld", 16)
    blk.gpsimd(gpsF)

    def bandF(eng, ps):
        cx.wait(eng, "fc_ld")
        for p in ps:
            for a in range(4):
                fir, half = a // 2, a % 2
                dst = mkap(t["bandsD"], fir * FIRSZ + half * 8 * GSZ + p * 4096,
                           [[GSZ, 8], [256, 16], [1, 256]])
                cx.inc(eng.dma_start(out=dst, in_=sb["fpbf"][:, a, 128 - p:384 - p]),
                       "fc_st", 16)
    blk.scalar(lambda s: bandF(s, range(0, 128)))


def emit_ln1(cx, blk):
    nc, t, sb = cx.nc, cx.t, cx.sb
    NT = 16

    def sync(s):
        for i in range(NT):
            if i >= 4:
                cx.wait(s, "y_cmp", i - 3)
            cx.inc(s.dma_start(out=sb["x2"][:, i % 4, :],
                               in_=t["x_slice"][128 * i:128 * (i + 1), :]), f"x_ld{i % 4}", 16)
    blk.sync(sync)

    def vecA(v, i):
        cx.wait(v, f"x_ld{i % 4}", 16 * (i // 4 + 1))
        xt = sb["x2"][:, i % 4, :]
        sc = sb["lnsc"][:, i, :]
        v.tensor_scalar(out=sb["junk"][:, 0, :], in0=xt, scalar1=0.0, scalar2=None,
                        op0=ALU.add, op1=ALU.add, accum_out=sc[:, 0:1])
        sq = sb["cvo"][:, 0:2, :]
        v.tensor_mul(out=sq, in0=xt.rearrange("p (a b) -> p a b", a=2),
                     in1=xt.rearrange("p (a b) -> p a b", a=2))
        cx.inc(v.tensor_scalar(out=sb["junk"][:, 1, :].rearrange("p (a b) -> p a b", a=2),
                               in0=sq, scalar1=0.0, scalar2=None,
                               op0=ALU.add, op1=ALU.add, accum_out=sc[:, 1:2]), "ln_st")

    def actA(s, i):
        cx.wait(s, "ln_st", i + 1)
        sc = sb["lnsc"][:, i, :]
        s.activation(out=sc[:, 0:1], in_=sc[:, 0:1], func=AF.Identity, scale=1.0 / D)
        s.activation(out=sc[:, 1:2], in_=sc[:, 1:2], func=AF.Identity, scale=1.0 / D)
        cx.inc(s.activation(out=sc[:, 2:3], in_=sc[:, 0:1], func=AF.Square), "ln_a")

    def vecB(v, i):
        cx.wait(v, "ln_a", i + 1)
        sc = sb["lnsc"][:, i, :]
        v.tensor_sub(out=sc[:, 1:2], in0=sc[:, 1:2], in1=sc[:, 2:3])
        cx.inc(v.tensor_scalar(out=sc[:, 1:2], in0=sc[:, 1:2], scalar1=1e-5,
                               scalar2=None, op0=ALU.add), "ln_vb")

    def actB(s, i):
        cx.wait(s, "ln_vb", i + 1)
        sc = sb["lnsc"][:, i, :]
        cx.inc(s.activation(out=sc[:, 1:2], in_=sc[:, 1:2], func=AF.Sqrt), "ln_sq")

    def vecC(v, i):
        cx.wait(v, "ln_sq", i + 1)
        sc = sb["lnsc"][:, i, :]
        v.reciprocal(out=sc[:, 2:3], in_=sc[:, 1:2])
        cx.inc(v.tensor_scalar(out=sc[:, 3:4], in0=sc[:, 0:1], scalar1=-1.0,
                               scalar2=None, op0=ALU.mult), "ln_nb")

    def actC(s, i):
        cx.wait(s, "ln_nb", i + 1)
        if i >= 2:
            cx.wait(s, "y_done", 16 * (i - 1))
        sc = sb["lnsc"][:, i, :]
        s.activation(out=sc[:, 3:4], in_=sc[:, 3:4], func=AF.Identity,
                     scale=sc[:, 2:3])
        cx.inc(s.activation(out=sb["y2"][:, i % 2, :], in_=sb["x2"][:, i % 4, :],
                            func=AF.Identity, bias=sb["lnsc"][:, i, 3:4],
                            scale=sb["lnsc"][:, i, 2:3]), "y_cmp")

    def vec(v):
        for i in range(NT):
            vecA(v, i)
            if i >= 1:
                vecB(v, i - 1)
            if i >= 2:
                vecC(v, i - 2)
        vecB(v, NT - 1)
        vecC(v, NT - 2)
        vecC(v, NT - 1)
    blk.vector(vec)

    def sc(s):
        for i in range(NT):
            actA(s, i)
            if i >= 1:
                actB(s, i - 1)
            if i >= 2:
                actC(s, i - 2)
        actB(s, NT - 1)
        actC(s, NT - 2)
        actC(s, NT - 1)
    blk.scalar(sc)

    def gps(g):
        for i in range(NT):
            cx.wait(g, "y_cmp", i + 1)
            cx.inc(g.dma_start(out=t["ag1_in"][128 * i:128 * (i + 1), :],
                               in_=sb["y2"][:, i % 2, :]), "y_done", 16)
        cx.wait(g, "y_done")
        cx.inc(g.collective_compute(
            "AllGather", ALU.bypass,
            ins=[t["ag1_in"].ap().opt()], outs=[t["ag1_out"].ap().opt()],
            replica_groups=GROUPS), "ag1")
    blk.gpsimd(gps)

    if tapped(cx, "t_y"):
        cx.dout("t_y", [TOK, D], BF16)
        cx.dout("t_lnsc", [128, 16, 4])
        def syncT(s):
            cx.wait(s, "y_done")
            cx.inc(s.dma_start(out=t["t_y"][:, :], in_=t["ag1_in"][:, :]), "taps", 16)
            cx.inc(s.dma_start(out=t["t_lnsc"][:, :, :], in_=sb["lnsc"][:, :, :]), "taps", 16)
        blk.sync(syncT)


def emit_gmix(cx, blk):
    nc, t, sb = cx.nc, cx.t, cx.sb

    def sync(s):
        cx.wait(s, "y_done", 16 * 16)
        for w in range(4):
            for kt in range(8):
                cx.inc(s.dma_start(
                    out=sb["yt_w"][:, kt, w, :],
                    in_=t["ag1_in"][512 * w:512 * (w + 1), 128 * kt:128 * (kt + 1)],
                    transpose=True), "gm_yt", 16)
    blk.sync(sync)

    def gps(g):
        for m in range(16):
            if m >= 2:
                cx.wait(g, "gm_mm1", 4 * (m - 1))
            cx.inc(g.dma_start(
                out=sb["gw1b"][:, :, m % 2, :],
                in_=t["gw1t"][:, 128 * m:128 * (m + 1)].rearrange("(kt p) m -> p kt m", p=128)),
                "gm_w", 16)
    blk.gpsimd(gps)

    def mm2(te, m):
        cx.wait(te, "gm_ev", 4 * (m + 1))
        mm = None
        for w in range(4):
            mm = te.matmul(sb["p_g2"][:, 512 * w:512 * (w + 1)],
                           sb["gw2"][:, m, :], sb["gh"][:, m % 2, 512 * w:512 * (w + 1)],
                           start=(m == 0), stop=(m == 15))
        cx.inc(mm, "gm_mm2")

    def ten(te):
        cx.wait(te, "ld")
        cx.wait(te, "ldg")
        cx.wait(te, "gm_yt", 16 * 32)
        for m in range(16):
            cx.wait(te, "gm_w", 16 * min(m + 2, 16))
            for w in range(4):
                idx = m * 4 + w
                ps = sb["p_a"] if idx % 2 == 0 else sb["p_b"]
                if idx >= 2:
                    cx.wait(te, "gm_ev", idx - 1)
                mm = None
                for kt in range(8):
                    mm = te.matmul(ps[:, :], sb["gw1b"][:, kt, m % 2, :],
                                   sb["yt_w"][:, kt, w, :], start=(kt == 0), stop=(kt == 7))
                cx.inc(mm, "gm_mm1")
            if m >= 1:
                mm2(te, m - 1)
        mm2(te, 15)
    blk.tensor(ten)

    def sc(s):
        for m in range(16):
            for w in range(4):
                idx = m * 4 + w
                cx.wait(s, "gm_mm1", idx + 1)
                if m >= 2:
                    cx.wait(s, "gm_mm2", m - 1)
                ps = sb["p_a"] if idx % 2 == 0 else sb["p_b"]
                cx.inc(s.activation(out=sb["gh"][:, m % 2, 512 * w:512 * (w + 1)],
                                    in_=ps[:, :], func=AF.Gelu,
                                    bias=sb["gb1_s"][:, m, 0:1], scale=1.0), "gm_ev")
    blk.scalar(sc)

    def vec(v):
        cx.wait(v, "gm_mm2", 16)
        v.memset(sb["lg32"][:, :], 0.0)
        v.tensor_scalar(out=sb["lg32"][0:16, :], in0=sb["p_g2"][:, :],
                        scalar1=sb["gb2_s"][:, :], scalar2=None, op0=ALU.add)
        cx.inc(v.transpose(out=sb["lgT"][:, :], in_=sb["lg32"][:, :]), "gm_lgT")
    blk.vector(vec)

    def gps2(g):
        cx.wait(g, "gm_lgT")
        cx.inc(g.dma_start(out=t["ag2_in"][:, :], in_=sb["lgT"][:, :]), "gm_st", 16)
        cx.wait(g, "gm_st")
        cx.inc(g.collective_compute(
            "AllGather", ALU.bypass,
            ins=[t["ag2_in"].ap().opt()], outs=[t["ag2_out"].ap().opt()],
            replica_groups=GROUPS), "ag2")
    blk.gpsimd(gps2)

    if tapped(cx, "t_logits"):
        cx.dout("t_logits", [32, TOK])
        cx.dout("t_lgT", [32, TOK])
        def syncT(s):
            cx.wait(s, "gm_lgT")
            cx.inc(s.dma_start(out=t["t_logits"][:, :], in_=sb["lg32"][:, :]), "taps", 16)
            cx.inc(s.dma_start(out=t["t_lgT"][:, :], in_=sb["lgT"][:, :]), "taps", 16)
        blk.sync(syncT)


def emit_proj(cx, blk):
    nc, t, sb = cx.nc, cx.t, cx.sb
    NW = 16

    def sync(s):
        cx.wait(s, "ag1")
        for w in range(NW):
            if w >= 4:
                cx.wait(s, "pj_mm", 6 * (w - 3))
            for kt in range(8):
                cx.inc(s.dma_start(
                    out=sb["yt_w"][:, kt, w % 4, :],
                    in_=t["ag1_out"][512 * w:512 * (w + 1), 128 * kt:128 * (kt + 1)],
                    transpose=True), f"pj_yt{w % 4}", 16)
    blk.sync(sync)

    def ten(te):
        cx.wait(te, "gm_ev")
        for w in range(NW):
            cx.wait(te, f"pj_yt{w % 4}", 16 * 8 * (w // 4 + 1))
            for m in range(6):
                idx = w * 6 + m
                ps = sb["p_a"] if idx % 2 == 0 else sb["p_b"]
                if idx >= 2:
                    cx.wait(te, "pj_ev", idx - 1)
                mm = None
                for kt in range(8):
                    mm = te.matmul(ps[:, :], sb["wqkvb"][:, kt, 128 * m:128 * (m + 1)],
                                   sb["yt_w"][:, kt, w % 4, :], start=(kt == 0), stop=(kt == 7))
                cx.inc(mm, "pj_mm")
            if w >= 1:
                cx.wait(te, "pj_si", 7 * w)
            mm = None
            for kt in range(8):
                mm = te.matmul(sb["p_bt"][:, :], sb["wqkvb"][:, kt, 768:772],
                               sb["yt_w"][:, kt, w % 4, :], start=(kt == 0), stop=(kt == 7))
            cx.inc(mm, "pj_bmm")
        for c in range(NCH):
            cx.wait(te, "pj_si", 7 * ((c // 4) + 1))
            if c % 8 == 0 and c > 0:
                cx.wait(te, "pj_bt", c // 8)
            mm = te.matmul(sb["p_btp"][:, c % 8:c % 8 + 1],
                           sb["brow"][0:1, 128 * c:128 * (c + 1)],
                           sb["k_ident"][0:1, 0:1], start=True, stop=True)
            if c % 8 == 7:
                cx.inc(mm, "pj_btmm")
    blk.tensor(ten)

    def vec(v):
        for w in range(NW):
            for m in range(6):
                idx = w * 6 + m
                cx.wait(v, "pj_ev", idx + 1)
                if w >= 1:
                    cx.wait(v, "pj_si", 7 * (w - 1) + m + 1)
                if w >= 2:
                    cx.wait(v, "pj_sp", 16 * 6 * w)
                cx.inc(emit_conv_tile(cx, v, m), "pj_cv_v")
    blk.vector(vec)

    def gps(g):
        for w in range(NW):
            cx.wait(g, "pj_si", 7 * (w + 1) - 1)
            for m in range(6):
                name = ["q_sp", "q_sp", "k_sp", "k_sp", "v_sp", "v_sp"][m]
                roff = (m % 2) * 128
                if name == "v_sp":
                    dst = t["v_sp"][roff:roff + 128, 128 + 512 * w: 128 + 512 * (w + 1)]
                else:
                    dst = t[name][roff:roff + 128, 512 * w:512 * (w + 1)]
                cx.inc(g.dma_start(out=dst, in_=sb["qkvb"][:, w % 2, m, :]), "pj_sp", 16)
        cx.wait(g, "zt")
        cx.inc(g.dma_start(out=t["v_sp"][0:128, 0:128], in_=sb["zt"][:, :]), "pj_z", 16)
        cx.inc(g.dma_start(out=t["v_sp"][128:256, 0:128], in_=sb["zt"][:, :]), "pj_z", 16)
    blk.gpsimd(gps)

    def sc(s):
        for w in range(NW):
            for m in range(6):
                idx = w * 6 + m
                cx.wait(s, "pj_mm", idx + 1)
                if w == 0:
                    s.memzero(sb["raw"][:, m, 0:3])
                else:
                    cx.wait(s, "pj_cv_v", 6 * (w - 1) + m + 1)
                    s.copy(out=sb["raw"][:, m, 0:3], in_=sb["raw"][:, m, 512:515])
                ps = sb["p_a"] if idx % 2 == 0 else sb["p_b"]
                cx.inc(s.copy(out=sb["raw"][:, m, 3:515], in_=ps[:, :]), "pj_ev")
            for m in range(6):
                cx.wait(s, "pj_cv_v", 6 * w + m + 1)
                cx.inc(s.activation(out=sb["qkvb"][:, w % 2, m, :], in_=sb["cvo"][:, m, :],
                                    func=AF.Silu), "pj_si")
            cx.wait(s, "pj_bmm", w + 1)
            cx.inc(s.activation(out=sb["brow"][0:1, 512 * w:512 * (w + 1)],
                                in_=sb["p_bt"][0:1, :], func=AF.Sigmoid), "pj_si")
    blk.scalar(sc)


    def vec2(v):
        for g8 in range(NCH // 8):
            cx.wait(v, "pj_btmm", g8 + 1)
            cx.inc(v.tensor_copy(out=sb["beta_t"][:, 8 * g8:8 * (g8 + 1)],
                                 in_=sb["p_btp"][:, :]), "pj_bt")
    blk.vector(vec2)

    if tapped(cx, "t_qkv"):
        cx.dout("t_q", [DK, L], BF16)
        cx.dout("t_k", [DK, L], BF16)
        cx.dout("t_v", [DV, L], BF16)
        cx.dout("t_beta", [128, NCH])
        def syncT(s):
            cx.wait(s, "pj_sp")
            cx.wait(s, "pj_z")
            cx.inc(s.dma_start(out=t["t_q"][:, :], in_=t["q_sp"][:, :]), "taps", 16)
            cx.inc(s.dma_start(out=t["t_k"][:, :], in_=t["k_sp"][:, :]), "taps", 16)
            cx.inc(s.dma_start(out=t["t_v"][:, :], in_=t["v_sp"][:, 128:128 + L]), "taps", 16)
            cx.wait(s, "pj_bt")
            cx.inc(s.dma_start(out=t["t_beta"][:, :], in_=sb["beta_t"][:, :]), "taps", 16)
        blk.sync(syncT)


def emit_conv_tile(cx, eng, m):
    sb = cx.sb
    raw = sb["raw"][:, m, :]
    out = sb["cvo"][:, m, :]
    wts = cx.sb["convw_s"]
    eng.tensor_scalar(out=out, in0=raw[:, 3:515], scalar1=wts[:, m, 3:4],
                      scalar2=None, op0=ALU.mult)
    last = None
    for tau in (2, 1, 0):
        last = eng.scalar_tensor_tensor(out=out, in0=raw[:, tau:tau + 512],
                                        scalar=wts[:, m, tau:tau + 1], in1=out,
                                        op0=ALU.mult, op1=ALU.add)
    return last


def mkap(handle, offset, dims):
    from concourse.ap import AP
    return AP(tensor=handle, offset=offset, ap=[list(d) for d in dims])


def emit_fir(cx, blk):
    nc, t, sb = cx.nc, cx.t, cx.sb

    GSZ = 128 * 16 * 256
    FIRSZ = 16 * GSZ

    def scl(s):
        cx.wait(s, "pj_sp"); cx.wait(s, "pj_z")
        cx.wait(s, "pj_bt"); cx.wait(s, "pj_si")
        vsrc = t["v_sp"][:, :].rearrange("d (b c) -> (d b) c", c=128)
        for q in range(4):
            cx.inc(s.dma_start(out=sb["vtb"][:, 4160 * q:4160 * (q + 1)],
                               in_=vsrc[4160 * q:4160 * (q + 1), :], transpose=True),
                   f"fir_vt{q}", 16)
    blk.scalar(scl)

    def sync2(s):
        cx.wait(s, "fc_st")
        for g in range(16):
            if g >= 2:
                cx.wait(s, "fir_mm", 16 * (g - 1))
            for fir in range(2):
                srcap = mkap(t["bandsD"], fir * FIRSZ + g * GSZ,
                             [[16 * 256, 128], [256, 16], [1, 256]])
                cx.inc(s.dma_start(out=sb["bands"][:, g % 2, fir, :, :], in_=srcap),
                       f"fir_bd{g % 2}", 16)
    blk.sync(sync2)

    def ten(te):
        cx.wait(te, "gm_lgT")
        for q in range(4):
            cx.wait(te, f"fir_vt{q}", 16)
        for ch in range(256):
            g = ch // 16
            if ch % 16 == 0:
                cx.wait(te, f"fir_bd{g % 2}", 16 * 2 * (g // 2 + 1))
            G8 = ch // 8
            if ch % 8 == 0 and G8 >= 2:
                cx.wait(te, "fir_ev", 2 * (G8 - 1))
            pfs = sb["pf_s"] if G8 % 2 == 0 else sb["pf_s2"]
            pfl = sb["pf_l"] if G8 % 2 == 0 else sb["pf_l2"]
            ro = (ch % 2) * 64
            co = ((ch // 2) % 4) * 128
            lm = sb["vtb"][:, ch * 65 + 1: ch * 65 + 65]
            lc = sb["vtb"][:, ch * 65: ch * 65 + 64]
            bs = sb["bands"][:, g % 2, 0, ch % 16, :]
            bl = sb["bands"][:, g % 2, 1, ch % 16, :]
            te.matmul(pfs[ro:ro + 64, co:co + 128], lm, bs[:, 0:128], start=True, stop=False,
                      skip_group_check=True)
            te.matmul(pfl[ro:ro + 64, co:co + 128], lm, bl[:, 0:128], start=True, stop=False,
                      skip_group_check=True)
            te.matmul(pfs[ro:ro + 64, co:co + 128], lc, bs[:, 128:256], start=False, stop=True,
                      skip_group_check=True)
            mm = te.matmul(pfl[ro:ro + 64, co:co + 128], lc, bl[:, 128:256], start=False,
                           stop=True, skip_group_check=True)
            cx.inc(mm, "fir_mm")
    blk.tensor(ten)

    def vec(v):
        for G8 in range(32):
            cx.wait(v, "fir_mm", 8 * (G8 + 1))
            if G8 >= 2:
                cx.wait(v, f"fir_sp{G8 % 2}", 16 * 4 * ((G8 - 2) // 2 + 1))
            pfs = sb["pf_s"] if G8 % 2 == 0 else sb["pf_s2"]
            pfl = sb["pf_l"] if G8 % 2 == 0 else sb["pf_l2"]
            v.tensor_copy(out=sb["fev"][:, G8 % 2, 0, :], in_=pfs[:, :])
            cx.inc(v.tensor_copy(out=sb["fev"][:, G8 % 2, 1, :], in_=pfl[:, :]), "fir_ev", 2)
    blk.vector(vec)

    def gps(g):
        for G8 in range(32):
            cx.wait(g, "fir_ev", 2 * (G8 + 1))
            for fir, dname in ((0, "fs_sp"), (1, "fl_sp")):
                for ch2 in range(2):
                    dst = mkap(t[dname], (G8 * 8 + ch2) * 8192,
                               [[128, 64], [2 * 8192, 4], [1, 128]])
                    cx.inc(g.dma_start(
                        out=dst,
                        in_=sb["fev"][64 * ch2:64 * (ch2 + 1), G8 % 2, fir, :]
                            .rearrange("p (s c) -> p s c", s=4)),
                        f"fir_sp{G8 % 2}", 16)
    blk.gpsimd(gps)

    if tapped(cx, "t_fs"):
        cx.dout("t_fs", [DV, L], BF16)
        cx.dout("t_fl", [DV, L], BF16)
        def syncT(s):
            cx.wait(s, "fir_sp0"); cx.wait(s, "fir_sp1")
            cx.inc(s.dma_start(out=t["t_fs"][:, :], in_=t["fs_sp"][:, :]), "taps", 16)
            cx.inc(s.dma_start(out=t["t_fl"][:, :], in_=t["fl_sp"][:, :]), "taps", 16)
        blk.sync(syncT)


def emit_scan(cx, blk):
    nc, t, sb = cx.nc, cx.t, cx.sb
    RING = 4

    def bcast4(tile):
        return tile[:, None, :].broadcast_to([128, 4, 128])

    def syncA(s):
        cx.wait(s, "pj_sp"); cx.wait(s, "pj_z")
        for c in range(NCH):
            r = c % RING
            if c >= RING:
                cx.wait(s, "sc_ds", c - RING + 1)
            for i, (nm, co) in enumerate((("q_sp", 0), ("k_sp", 0), ("v_sp", 128))):
                cx.inc(s.dma_start(
                    out=sb["qkvt"][:, r, i, :],
                    in_=t[nm][:, co + 128 * c: co + 128 * (c + 1)], transpose=True),
                    f"sc_tl{r}", 16)
            cx.wait(s, f"sc_spn{c % 2}", 16 * 3 * (c // 2 + 1))
            for i, nm in enumerate(("qh_tm", "kh_tm", "kb_tm")):
                for kt in range(2):
                    cx.inc(s.dma_start(
                        out=sb["fT"][:, r, 2 * i + kt, :],
                        in_=t[nm][128 * c:128 * (c + 1), 128 * kt:128 * (kt + 1)],
                        transpose=True), f"sc_fl{r}", 16)
    blk.sync(syncA)

    def vecB_c(v, c):
            r = c % RING
            cx.wait(v, f"sc_tl{r}", 16 * 3 * (c // RING + 1))
            if c >= 4:
                cx.wait(v, "sc_nm", 5 * (c - 3))
            st = sb["sstat"][:, r, :]
            for i in range(2):
                x = sb["qkvt"][:, r, i, :]
                v.tensor_mul(out=sb["sq256"][:, :], in0=x, in1=x)
                v.tensor_scalar(out=sb["junk"][:, 0, 0:256], in0=sb["sq256"][:, :],
                                scalar1=0.0, scalar2=None, op0=ALU.add, op1=ALU.add,
                                accum_out=st[:, i:i + 1])
            cx.inc(v.tensor_copy(out=st[:, 7:8], in_=sb["beta_t"][:, c:c + 1]), "sc_ss")
            cx.wait(v, "sc_sd", 2 * (c + 1))
            v.reciprocal(out=st[:, 2:3], in_=st[:, 0:1])
            cx.inc(v.reciprocal(out=st[:, 3:4], in_=st[:, 1:2]), "sc_inv")

    def actB_c(s, c):
            r = c % RING
            st = sb["sstat"][:, r, :]
            cx.wait(s, "sc_ss", c + 1)
            if c >= 4:
                cx.wait(s, "sc_ds", c - 3)
                cx.wait(s, f"sc_spn{c % 2}", 16 * 3 * ((c - 4) // 2 + 1))
            s.activation(out=st[:, 0:1], in_=st[:, 0:1], func=AF.Sqrt)
            cx.inc(s.activation(out=st[:, 1:2], in_=st[:, 1:2], func=AF.Sqrt), "sc_sd", 2)
            cx.wait(s, "sc_inv", c + 1)
            s.activation(out=st[:, 4:5], in_=st[:, 7:8], func=AF.Identity, scale=st[:, 3:4])
            s.activation(out=sb["nrm"][:, r, 0, :], in_=sb["qkvt"][:, r, 0, :],
                         func=AF.Identity, scale=st[:, 2:3])
            s.activation(out=sb["nrm"][:, r, 1, :], in_=sb["qkvt"][:, r, 1, :],
                         func=AF.Identity, scale=st[:, 3:4])
            s.activation(out=sb["nrm"][:, r, 2, :], in_=sb["qkvt"][:, r, 1, :],
                         func=AF.Identity, scale=st[:, 4:5])
            cx.inc(s.activation(out=sb["nrm"][:, r, 3, :], in_=sb["qkvt"][:, r, 2, :],
                                func=AF.Identity, scale=st[:, 7:8]), "sc_nm", 5)

    def gpsC_c(g, c):
            r = c % RING
            cx.wait(g, "sc_nm", 5 * (c + 1))
            for i, nm in enumerate(("qh_tm", "kh_tm", "kb_tm")):
                cx.inc(g.dma_start(out=t[nm][128 * c:128 * (c + 1), :],
                                   in_=sb["nrm"][:, r, i, :]), f"sc_spn{c % 2}", 16)

    NSL = {0: (1, 2), 1: (3, 4)}

    def ten(te):
        def fl(c, which, kt):
            return sb["fT"][:, c % RING, 2 * which + kt, :]

        for w in range(NCH // 4):
            cs = [4 * w + i for i in range(4)]
            base = 14 * w
            for c in cs:
                cx.wait(te, f"sc_fl{c % RING}", 16 * 6 * (c // RING + 1))
            cx.wait(te, "sc_tbe", base)
            mm = None
            for c in cs:
                i = c % 4
                sl = slice(128 * i, 128 * (i + 1))
                te.matmul(sb["pT1"][:, sl], fl(c, 1, 0), fl(c, 0, 0), start=True, stop=False)
                mm = te.matmul(sb["pT1"][:, sl], fl(c, 1, 1), fl(c, 0, 1),
                               start=False, stop=True)
            cx.inc(mm, "sc_tb")
            for c in cs:
                i = c % 4
                sl = slice(128 * i, 128 * (i + 1))
                te.matmul(sb["pT2"][:, sl], fl(c, 2, 0), fl(c, 1, 0), start=True, stop=False)
                te.matmul(sb["pT2"][:, sl], fl(c, 2, 1), fl(c, 1, 1), start=False, stop=True)
            cx.wait(te, "sc_tbe", base + 1)
            for c in cs:
                i = c % 4
                sl = slice(128 * i, 128 * (i + 1))
                te.matmul(sb["pT1"][:, sl], fl(c, 1, 0), fl(c, 2, 0), start=True, stop=False)
                mm = te.matmul(sb["pT1"][:, sl], fl(c, 1, 1), fl(c, 2, 1),
                               start=False, stop=True)
            cx.inc(mm, "sc_tb")
            for lev in range(6):
                nsl, ntsl = NSL[lev % 2]
                cx.wait(te, "sc_tbe", base + 2 + 2 * lev)
                mm = None
                for c in cs:
                    i = c % 4
                    sl = slice(128 * i, 128 * (i + 1))
                    mm = te.matmul(sb["pT2"][:, sl], sb["tb"][:, ntsl, sl],
                                   sb["tb"][:, nsl, sl], start=True, stop=True)
                    if lev < 5:
                        mm = te.matmul(sb["pT1"][:, sl], sb["tb"][:, nsl, sl],
                                       sb["tb"][:, ntsl, sl], start=True, stop=True)
                cx.inc(mm, "sc_tb")
                cx.wait(te, "sc_tbe", base + 3 + 2 * lev)
                n2sl = NSL[(lev + 1) % 2][0]
                Pc = sb["tb"][:, 5 if lev % 2 == 0 else 6, :]
                mm = None
                for c in cs:
                    i = c % 4
                    sl = slice(128 * i, 128 * (i + 1))
                    mm = te.matmul(sb["pT1"][:, sl], sb["tb"][:, n2sl, sl], Pc[:, sl],
                                   start=True, stop=True)
                cx.inc(mm, "sc_tb")
            cx.wait(te, "sc_tbe", base + 14)
            for c in cs:
                i = c % 4
                r = c % RING
                if c > 0:
                    cx.wait(te, "sc_scp", c)
                else:
                    cx.wait(te, "sc_sb0", 1)
                te.matmul(sb["pR"][:, :], fl(c, 2, 0), sb["sbf"][:, 0, :],
                          start=True, stop=False)
                cx.inc(te.matmul(sb["pR"][:, :], fl(c, 2, 1), sb["sbf"][:, 1, :],
                                 start=False, stop=True), "sc_rmm")
                cx.wait(te, "sc_rev", c + 1)
                cx.wait(te, "sc_uev", c)
                cx.inc(te.matmul(sb["pU"][:, :], sb["tb"][:, 5, 128 * i:128 * (i + 1)],
                                 sb["rbuf"][:, :], start=True, stop=True), "sc_umm")
                cx.wait(te, "sc_oev", c)
                te.matmul(sb["pO"][:, :], fl(c, 0, 0), sb["sbf"][:, 0, :],
                          start=True, stop=False)
                te.matmul(sb["pO"][:, :], fl(c, 0, 1), sb["sbf"][:, 1, :],
                          start=False, stop=False)
                cx.wait(te, "sc_uev", c + 1)
                cx.inc(te.matmul(sb["pO"][:, :], sb["tb"][:, 0, 128 * i:128 * (i + 1)],
                                 sb["ubuf"][:, :], start=False, stop=True), "sc_omm")
                te.matmul(sb["pS"][:, 0:256], sb["nrm"][:, r, 1, 0:128],
                          sb["ubuf"][:, :], start=(c == 0), stop=False,
                          skip_group_check=True)
                cx.inc(te.matmul(sb["pS"][:, 256:512], sb["nrm"][:, r, 1, 128:256],
                                 sb["ubuf"][:, :], start=False, stop=False,
                                 skip_group_check=True), "sc_ds")
    blk.tensor(ten)

    def vecE(v):
        cx.inc(v.memset(sb["sbf"][:, :, :], 0.0), "sc_sb0")
        tbn = 0
        for w in range(NCH // 4):
            cs = [4 * w + i for i in range(4)]
            for c in cs:
                vecB_c(v, c)
            tbn += 1
            cx.wait(v, "sc_tb", tbn)
            cx.inc(v.tensor_tensor(
                out=sb["tb"][:, 0, :].rearrange("p (a b) -> p a b", a=4),
                in0=sb["pT1"][:, :].rearrange("p (a b) -> p a b", a=4),
                in1=bcast4(sb["k_mui"]), op=ALU.mult), "sc_tbe")
            tbn += 1
            cx.wait(v, "sc_tb", tbn)
            v.tensor_tensor(out=sb["tb"][:, 1, :].rearrange("p (a b) -> p a b", a=4),
                            in0=sb["pT2"][:, :].rearrange("p (a b) -> p a b", a=4),
                            in1=bcast4(sb["k_msl"]), op=ALU.mult)
            v.tensor_tensor(out=sb["tb"][:, 2, :].rearrange("p (a b) -> p a b", a=4),
                            in0=sb["pT1"][:, :].rearrange("p (a b) -> p a b", a=4),
                            in1=bcast4(sb["k_msu"]), op=ALU.mult)
            cx.inc(v.tensor_tensor(
                out=sb["tb"][:, 5, :].rearrange("p (a b) -> p a b", a=4),
                in0=sb["tb"][:, 2, :].rearrange("p (a b) -> p a b", a=4),
                in1=bcast4(sb["k_ident"]), op=ALU.add), "sc_tbe")
            for lev in range(6):
                n2sl, nt2sl = NSL[(lev + 1) % 2]
                tbn += 1
                cx.wait(v, "sc_tb", tbn)
                v.tensor_copy(out=sb["tb"][:, n2sl, :], in_=sb["pT2"][:, :])
                if lev < 5:
                    cx.inc(v.tensor_copy(out=sb["tb"][:, nt2sl, :], in_=sb["pT1"][:, :]),
                           "sc_tbe")
                else:
                    cx.inc(v.tensor_copy(out=sb["junk"][:, 0, 0:8],
                                         in_=sb["tb"][:, n2sl, 0:8]), "sc_tbe")
                tbn += 1
                cx.wait(v, "sc_tb", tbn)
                Psrc = sb["tb"][:, 5 if lev % 2 == 0 else 6, :]
                Pdst = sb["tb"][:, 6 if lev % 2 == 0 else 5, :]
                cx.inc(v.tensor_add(out=Pdst, in0=sb["pT1"][:, :], in1=Psrc), "sc_tbe")
            for c in cs:
                r = c % RING
                cx.wait(v, "sc_rmm", c + 1)
                cx.wait(v, "sc_umm", c)
                cx.inc(v.scalar_tensor_tensor(out=sb["rbuf"][:, :], in0=sb["pR"][:, :],
                                              scalar=-1.0, in1=sb["nrm"][:, r, 3, :],
                                              op0=ALU.mult, op1=ALU.add), "sc_rev")
                cx.wait(v, "sc_omm", c + 1)
                if c >= 2:
                    cx.wait(v, "sc_ost", 16 * (c - 1))
                cx.inc(v.tensor_copy(out=sb["obuf"][:, c % 2, :], in_=sb["pO"][:, :]),
                       "sc_oev")
    blk.vector(vecE)

    def actE_c(s, c):
            cx.wait(s, "sc_umm", c + 1)
            cx.inc(s.activation(out=sb["ubuf"][:, :], in_=sb["pU"][:, :],
                                func=AF.Identity), "sc_uev")
            cx.wait(s, "sc_ds", c + 1)
            s.activation(out=sb["sbf"][:, 0, :], in_=sb["pS"][:, 0:256], func=AF.Identity)
            cx.inc(s.activation(out=sb["sbf"][:, 1, :], in_=sb["pS"][:, 256:512],
                                func=AF.Identity), "sc_scp")

    def actE2(s):
        for w in range(NCH // 4):
            for c in [4 * w + i for i in range(4)]:
                actB_c(s, c)
            for c in [4 * w + i for i in range(4)]:
                actE_c(s, c)
    blk.scalar(actE2)

    def gpsF(g):
        for w in range(NCH // 4):
            for c in [4 * w + i for i in range(4)]:
                gpsC_c(g, c)
            for c in [4 * w + i for i in range(4)]:
                cx.wait(g, "sc_oev", c + 1)
                cx.inc(g.dma_start(out=t["od_tm"][128 * c:128 * (c + 1), :],
                                   in_=sb["obuf"][:, c % 2, :]), "sc_ost", 16)
    blk.gpsimd(gpsF)

    if tapped(cx, "t_od"):
        cx.dout("t_od", [L, DV], BF16)
        def syncT(s):
            cx.wait(s, "sc_ost")
            cx.inc(s.dma_start(out=t["t_od"][:, :], in_=t["od_tm"][:, :]), "taps", 16)
        blk.sync(syncT)


def emit_branch(cx, blk):
    nc, t, sb = cx.nc, cx.t, cx.sb
    NT = 64

    def syncL(s):
        cx.wait(s, "ag2")
        v2 = t["ag2_out"][:, :].rearrange("(j r) (b ph f) -> r j b f ph",
                                          j=4, ph=4, f=32)
        for ph in range(4):
            for j in range(4):
                cx.inc(s.dma_start(
                    out=sb["ld16"][32 * ph:32 * (ph + 1), 16 * j:16 * (j + 1), :],
                    in_=v2[:, j, :, 0:16, ph]), "br_lg", 16)
        cx.wait(s, "sc_ost"); cx.wait(s, "sc_scp")
        cx.wait(s, "fir_sp0"); cx.wait(s, "fir_sp1")
        for tt in range(NT):
            r = tt % 4
            if tt >= 4:
                cx.wait(s, "br_st", tt - 3)
            cx.inc(s.dma_start(out=sb["brt"][:, r, 0, :],
                               in_=t["fs_sp"][:, 128 * tt:128 * (tt + 1)],
                               transpose=True), f"br_tl{r}", 16)
            cx.inc(s.dma_start(out=sb["brt"][:, r, 1, :],
                               in_=t["fl_sp"][:, 128 * tt:128 * (tt + 1)],
                               transpose=True), f"br_tl{r}", 16)
            cx.inc(s.dma_start(out=sb["brt"][:, r, 2, :],
                               in_=t["od_tm"][128 * tt:128 * (tt + 1), :]),
                   f"br_tl{r}", 16)
            cx.inc(s.dma_start(out=sb["brt"][:, r, 3, :],
                               in_=t["v_sp"][:, 128 + 128 * tt:128 + 128 * (tt + 1)],
                               transpose=True), f"br_tl{r}", 16)
    blk.sync(syncL)

    def vecS(v):
        for tt in range(NT):
            r = tt % 4
            cx.wait(v, f"br_tl{r}", 16 * 4 * (tt // 4 + 1))
            last = None
            for br in range(4):
                x = sb["brt"][:, r, br, :]
                bs = sb["bst"][:, tt, br, :]
                v.tensor_scalar(out=sb["junk"][:, 0, 0:256], in0=x, scalar1=0.0,
                                scalar2=None, op0=ALU.add, op1=ALU.add,
                                accum_out=bs[:, 0:1])
                v.tensor_mul(out=sb["junk"][:, 1, 0:256], in0=x, in1=x)
                v.tensor_scalar(out=sb["junk"][:, 0, 256:512],
                                in0=sb["junk"][:, 1, 0:256], scalar1=0.0, scalar2=None,
                                op0=ALU.add, op1=ALU.add, accum_out=bs[:, 1:2])
                last = v.tensor_reduce(out=bs[:, 2:3], in_=x, axis=AX.X, op=ALU.max)
            cx.inc(last, "br_st")
    blk.vector(vecS)

    def actC(s):
        cx.wait(s, "br_st", NT)
        bs = sb["bst"]
        s2 = sb["stt2"]
        s.activation(out=s2[:, :, :, 0], in_=bs[:, :, :, 0], func=AF.Identity,
                     scale=1.0 / 256.0)
        s.activation(out=s2[:, :, :, 1], in_=bs[:, :, :, 1], func=AF.Identity,
                     scale=1.0 / 255.0)
        cx.inc(s.activation(out=bs[:, :, :, 3], in_=s2[:, :, :, 0], func=AF.Square,
                            scale=float(np.sqrt(256.0 / 255.0))), "br_c1")
    blk.scalar(actC)

    def vecC(v):
        cx.wait(v, "br_c1", 1)
        s2, bs = sb["stt2"], sb["bst"]
        v.tensor_sub(out=s2[:, :, :, 1], in0=s2[:, :, :, 1], in1=bs[:, :, :, 3])
        cx.inc(v.tensor_scalar(out=s2[:, :, :, 1], in0=s2[:, :, :, 1], scalar1=0.0,
                               scalar2=None, op0=ALU.max), "br_c2")
    blk.vector(vecC)

    def actC2(s):
        cx.wait(s, "br_c2", 1)
        cx.inc(s.activation(out=sb["stt2"][:, :, :, 1], in_=sb["stt2"][:, :, :, 1],
                            func=AF.Sqrt), "br_c3")
    blk.scalar(actC2)

    def vecC2(v):
        cx.wait(v, "br_c3", 1)
        cx.wait(v, "br_lg")
        s2, bs = sb["stt2"], sb["bst"]
        v.tensor_add(out=s2[:, :, :, 0], in0=s2[:, :, :, 0], in1=s2[:, :, :, 1])
        v.tensor_add(out=s2[:, :, :, 0], in0=s2[:, :, :, 0], in1=bs[:, :, :, 2])
        v.tensor_scalar(out=s2[:, :, :, 0], in0=s2[:, :, :, 0], scalar1=1.0 / 3.0,
                        scalar2=None, op0=ALU.mult)
        sel4 = sb["ld16"][:, :, :].rearrange("p b (c hh) -> p b hh c", hh=4)
        selb = sb["selb"][:, :, :]
        v.tensor_scalar(out=selb, in0=sel4[:, :, 0, :], scalar1=sb["hsel_s"][:, 0:1],
                        scalar2=None, op0=ALU.mult)
        for hh in range(1, 4):
            v.scalar_tensor_tensor(out=selb, in0=sel4[:, :, hh, :],
                                   scalar=sb["hsel_s"][:, hh:hh + 1], in1=selb,
                                   op0=ALU.mult, op1=ALU.add)
        v.scalar_tensor_tensor(out=sb["wts"][:, :, :], in0=s2[:, :, :, 0],
                               scalar=sb["abc"][:, 0:1], in1=selb,
                               op0=ALU.mult, op1=ALU.add)
        cx.inc(v.tensor_reduce(out=s2[:, :, 0, 1], in_=sb["wts"][:, :, :],
                               axis=AX.X, op=ALU.max), "br_sm1")
    blk.vector(vecC2)

    def gpsS(g):
        cx.wait(g, "br_sm1", 1)
        cx.inc(g.tensor_tensor(out=sb["wts"][:, :, :], in0=sb["wts"][:, :, :],
                               in1=sb["stt2"][:, :, 0:1, 1].broadcast_to([128, NT, 4]),
                               op=ALU.subtract), "br_sm2")
    blk.gpsimd(gpsS)

    def actS(s):
        cx.wait(s, "br_sm2", 1)
        cx.inc(s.activation(out=sb["wts"][:, :, :], in_=sb["wts"][:, :, :],
                            func=AF.Exp), "br_sm3")
    blk.scalar(actS)

    def vecS2(v):
        cx.wait(v, "br_sm3", 1)
        v.tensor_reduce(out=sb["stt2"][:, :, 0, 1], in_=sb["wts"][:, :, :],
                        axis=AX.X, op=ALU.add)
        cx.inc(v.reciprocal(out=sb["stt2"][:, :, 1, 1],
                            in_=sb["stt2"][:, :, 0, 1]), "br_sm4")
    blk.vector(vecS2)

    def gpsS2(g):
        cx.wait(g, "br_sm4", 1)
        cx.inc(g.tensor_tensor(out=sb["wts"][:, :, :], in0=sb["wts"][:, :, :],
                               in1=sb["stt2"][:, :, 1:2, 1].broadcast_to([128, NT, 4]),
                               op=ALU.mult), "br_wts")
    blk.gpsimd(gpsS2)

    def syncM(s):
        cx.wait(s, "br_wts", 1)
        for tt in range(NT):
            r = tt % 4
            if tt >= 4:
                cx.wait(s, "br_mx", tt - 3)
            cx.inc(s.dma_start(out=sb["brt"][:, r, 0, :],
                               in_=t["fs_sp"][:, 128 * tt:128 * (tt + 1)],
                               transpose=True), f"br_t2{r}", 16)
            cx.inc(s.dma_start(out=sb["brt"][:, r, 1, :],
                               in_=t["fl_sp"][:, 128 * tt:128 * (tt + 1)],
                               transpose=True), f"br_t2{r}", 16)
            cx.inc(s.dma_start(out=sb["brt"][:, r, 2, :],
                               in_=t["od_tm"][128 * tt:128 * (tt + 1), :]),
                   f"br_t2{r}", 16)
            cx.inc(s.dma_start(out=sb["brt"][:, r, 3, :],
                               in_=t["v_sp"][:, 128 + 128 * tt:128 + 128 * (tt + 1)],
                               transpose=True), f"br_t2{r}", 16)
    blk.sync(syncM)

    def vecM(v):
        for tt in range(NT):
            r = tt % 4
            cx.wait(v, f"br_t2{r}", 16 * 4 * (tt // 4 + 1))
            if tt >= 2:
                cx.wait(v, "br_on", tt - 1)
            ox = sb["oxb"][:, tt % 2, :]
            v.tensor_scalar(out=ox, in0=sb["brt"][:, r, 0, :],
                            scalar1=sb["wts"][:, tt, 0:1], scalar2=None, op0=ALU.mult)
            last = None
            for br in range(1, 4):
                last = v.scalar_tensor_tensor(out=ox, in0=sb["brt"][:, r, br, :],
                                              scalar=sb["wts"][:, tt, br:br + 1], in1=ox,
                                              op0=ALU.mult, op1=ALU.add)
            cx.inc(last, "br_mx")
            v.tensor_mul(out=sb["sq2"][:, 0:256], in0=ox, in1=ox)
            cx.inc(v.tensor_scalar(out=sb["sq2"][:, 256:512], in0=sb["sq2"][:, 0:256],
                                   scalar1=0.0, scalar2=None, op0=ALU.add, op1=ALU.add,
                                   accum_out=sb["osc"][:, tt % 4:tt % 4 + 1]), "br_ms")
            cx.wait(v, "br_sd", tt + 1)
            cx.inc(v.reciprocal(out=sb["osc"][:, tt % 4:tt % 4 + 1],
                                in_=sb["osc"][:, tt % 4:tt % 4 + 1]), "br_inv")
    blk.vector(vecM)

    def actM(s):
        for tt in range(NT):
            cx.wait(s, "br_ms", tt + 1)
            cx.inc(s.activation(out=sb["osc"][:, tt % 4:tt % 4 + 1],
                                in_=sb["osc"][:, tt % 4:tt % 4 + 1], func=AF.Sqrt,
                                scale=1.0 / 256.0, bias=sb["k_eps"][:, 0:1]), "br_sd")
            cx.wait(s, "br_inv", tt + 1)
            if tt >= 2:
                cx.wait(s, f"br_ost{tt % 2}", 16 * ((tt - 2) // 2 + 1))
            ox = sb["oxb"][:, tt % 2, :]
            cx.inc(s.activation(out=sb["obx"][:, tt % 2, :], in_=ox, func=AF.Identity,
                                scale=sb["osc"][:, tt % 4:tt % 4 + 1]), "br_on")
    blk.scalar(actM)

    def gpsM(g):
        for tt in range(NT):
            cx.wait(g, "br_on", tt + 1)
            cx.inc(g.dma_start(out=t["o_tm"][128 * tt:128 * (tt + 1), :],
                               in_=sb["obx"][:, tt % 2, :]), f"br_ost{tt % 2}", 16)
    blk.gpsimd(gpsM)

    if tapped(cx, "t_ag2"):
        cx.dout("t_ag2", [128, TOK])
        cx.dout("t_agi", [32, TOK])
        def syncTa(s):
            cx.wait(s, "ag2")
            cx.inc(s.dma_start(out=t["t_ag2"][:, :], in_=t["ag2_out"][:, :]), "taps", 16)
            cx.inc(s.dma_start(out=t["t_agi"][:, :], in_=t["ag2_in"][:, :]), "taps", 16)
        blk.sync(syncTa)

    if tapped(cx, "t_brdbg"):
        cx.dout("t_selb", [128, 64 * 4])
        cx.dout("t_stat", [128, 64 * 4])
        cx.dout("t_wts", [128, 64 * 4])
        cx.dout("t_bst", [128, 64 * 4 * 4])
        def syncTb(s):
            cx.wait(s, "br_wts")
            cx.inc(s.dma_start(out=t["t_selb"][:, :],
                               in_=sb["selb"][:, :, :].rearrange("p a b -> p (a b)")),
                   "taps", 16)
            cx.inc(s.dma_start(out=t["t_stat"][:, :],
                               in_=sb["stt2"][:, :, :, 0].rearrange("p a b -> p (a b)"),
                               allow_slow_non_contiguous=True),
                   "taps", 16)
            cx.inc(s.dma_start(out=t["t_wts"][:, :],
                               in_=sb["wts"][:, :, :].rearrange("p a b -> p (a b)")),
                   "taps", 16)
            cx.inc(s.dma_start(out=t["t_bst"][:, :],
                               in_=sb["bst"][:, :, :, :].rearrange("p a b c -> p (a b c)")),
                   "taps", 16)
        blk.sync(syncTb)

    if tapped(cx, "t_o"):
        cx.dout("t_o", [L, DV], BF16)
        def syncT(s):
            cx.wait(s, "br_ost0"); cx.wait(s, "br_ost1")
            cx.inc(s.dma_start(out=t["t_o"][:, :], in_=t["o_tm"][:, :]), "taps", 16)
        blk.sync(syncT)


def emit_oproj(cx, blk):
    nc, t, sb = cx.nc, cx.t, cx.sb
    NT = 64

    def sync0(s):
        cx.inc(s.dma_start(out=sb["wost"][:, :, :],
                           in_=t["wot"][:, :].rearrange("(kt p) d -> p kt d", p=128)),
               "op_w", 16)
    blk.sync(sync0)

    def act0(s):
        cx.wait(s, "op_w", 16)
        for kt in range(2):
            cx.inc(s.activation(out=sb["wotf"][:, kt, :], in_=sb["wost"][:, kt, :],
                                func=AF.Identity, scale=sb["onw_s"][:, kt, 0:1]),
                   "op_wf")
    blk.scalar(act0)

    def syncL(s):
        for tt in range(NT):
            r = tt % 4
            cx.wait(s, f"br_ost{tt % 2}", 16 * (tt // 2 + 1))
            if tt >= 4:
                cx.wait(s, "op_mm", 2 * (tt - 3))
            for kt in range(2):
                cx.inc(s.dma_start(out=sb["oT"][:, r, kt, :],
                                   in_=t["o_tm"][128 * tt:128 * (tt + 1),
                                                 128 * kt:128 * (kt + 1)],
                                   transpose=True), f"op_tl{r}", 16)
    blk.sync(syncL)

    def ten(te):
        cx.wait(te, "op_wf")
        for tt in range(NT):
            r = tt % 4
            cx.wait(te, f"op_tl{r}", 16 * 2 * (tt // 4 + 1))
            if tt >= 1:
                cx.wait(te, "op_ev", 2 * tt)
            te.matmul(sb["p_a"][:, :], sb["oT"][:, r, 0, :], sb["wotf"][:, 0, 0:512],
                      start=True, stop=False)
            cx.inc(te.matmul(sb["p_a"][:, :], sb["oT"][:, r, 1, :],
                             sb["wotf"][:, 1, 0:512], start=False, stop=True), "op_mm")
            te.matmul(sb["p_b"][:, :], sb["oT"][:, r, 0, :], sb["wotf"][:, 0, 512:1024],
                      start=True, stop=False)
            cx.inc(te.matmul(sb["p_b"][:, :], sb["oT"][:, r, 1, :],
                             sb["wotf"][:, 1, 512:1024], start=False, stop=True), "op_mm")
    blk.tensor(ten)

    def vecE(v):
        for tt in range(NT):
            cx.wait(v, "op_mm", 2 * (tt + 1))
            if tt >= 2:
                cx.wait(v, "op_st", 16 * (tt - 1))
            v.tensor_copy(out=sb["rsev"][:, tt % 2, 0:512], in_=sb["p_a"][:, :])
            cx.inc(v.tensor_copy(out=sb["rsev"][:, tt % 2, 512:1024], in_=sb["p_b"][:, :]),
                   "op_ev", 2)
    blk.vector(vecE)

    def gps(g):
        for tt in range(NT):
            cx.wait(g, "op_ev", 2 * (tt + 1))
            cx.inc(g.dma_start(out=t["rs_in"][128 * tt:128 * (tt + 1), :],
                               in_=sb["rsev"][:, tt % 2, :]), "op_st", 16)
        cx.wait(g, "op_st")
        cx.inc(g.collective_compute(
            "ReduceScatter", ALU.add,
            ins=[t["rs_in"].ap().opt()], outs=[t["rs_out"].ap().opt()],
            replica_groups=GROUPS), "rs")
    blk.gpsimd(gps)

    if tapped(cx, "t_rs"):
        cx.dout("t_rs", [TOK, D])
        def syncT(s):
            cx.wait(s, "rs")
            cx.inc(s.dma_start(out=t["t_rs"][:, :], in_=t["rs_out"][:, :]), "taps", 16)
        blk.sync(syncT)


def lnh_vec_a(cx, v, x, sc, pfx, i):
    v.tensor_scalar(out=cx.sb["junk"][:, 0, :], in0=x, scalar1=0.0, scalar2=None,
                    op0=ALU.add, op1=ALU.add, accum_out=sc[:, 0:1])
    sq = cx.sb["fsq"][:, :]
    v.tensor_mul(out=sq, in0=x, in1=x)
    cx.inc(v.tensor_scalar(out=cx.sb["junk"][:, 1, :], in0=sq, scalar1=0.0,
                           scalar2=None, op0=ALU.add, op1=ALU.add,
                           accum_out=sc[:, 1:2]), pfx)

def lnh_act_a(cx, s, sc, pfx, i):
    cx.wait(s, pfx, 6 * i + 1)
    s.activation(out=sc[:, 0:1], in_=sc[:, 0:1], func=AF.Identity, scale=1.0 / D)
    s.activation(out=sc[:, 1:2], in_=sc[:, 1:2], func=AF.Identity, scale=1.0 / D)
    cx.inc(s.activation(out=sc[:, 2:3], in_=sc[:, 0:1], func=AF.Square), pfx)

def lnh_vec_b(cx, v, sc, pfx, i):
    cx.wait(v, pfx, 6 * i + 2)
    v.tensor_sub(out=sc[:, 1:2], in0=sc[:, 1:2], in1=sc[:, 2:3])
    cx.inc(v.tensor_scalar(out=sc[:, 1:2], in0=sc[:, 1:2], scalar1=1e-5,
                           scalar2=None, op0=ALU.add), pfx)

def lnh_act_b(cx, s, sc, pfx, i):
    cx.wait(s, pfx, 6 * i + 3)
    cx.inc(s.activation(out=sc[:, 1:2], in_=sc[:, 1:2], func=AF.Sqrt), pfx)

def lnh_vec_c(cx, v, sc, pfx, i):
    cx.wait(v, pfx, 6 * i + 4)
    v.reciprocal(out=sc[:, 2:3], in_=sc[:, 1:2])
    cx.inc(v.tensor_scalar(out=sc[:, 3:4], in0=sc[:, 0:1], scalar1=-1.0,
                           scalar2=None, op0=ALU.mult), pfx)

def lnh_gps_nb(cx, g, sc, pfx, i):
    cx.wait(g, pfx, 6 * i + 5)
    cx.inc(g.tensor_tensor(out=sc[:, 3:4], in0=sc[:, 3:4], in1=sc[:, 2:3],
                           op=ALU.mult), pfx)

def lnh_act_apply(cx, s, x, out, sc, pfx, i):
    cx.wait(s, pfx, 6 * i + 6)
    return s.activation(out=out, in_=x, func=AF.Identity, bias=sc[:, 3:4],
                        scale=sc[:, 2:3])


def emit_ffn(cx, blk):
    nc, t, sb = cx.nc, cx.t, cx.sb
    sget, sfree = cx.sget, cx.sfree
    NT = 16
    sfree("rsev", "wost", "wotf", "oT")
    sfree("selb", "oxb", "sq2", "osc", "obx", "stt2", "wts", "bst", "brt", "ld16")
    sget("fin", [128, 4, 2, D], F32)
    sget("lnsc2", [128, 16, 16], F32)
    sget("fmx", [128, 12, 24], F32)
    sget("fsc", [128, 16], F32)
    sget("flt", [4, 128], F32)
    sget("flt2", [4, 1], F32)
    sget("arsb", [1, 8], F32)
    sget("arv", [1, 16], F32)
    sget("fft", [128, 2, 2048], F32)
    sget("fsq", [128, D], F32)
    sget("xqb", [128, 2, D], BF16)
    sget("xnb", [128, 16, D], F32)

    def sync1(s):
        cx.wait(s, "rs")
        for i in range(NT):
            if i >= 4:
                cx.wait(s, "ffx", i - 3)
                cx.wait(s, "f_r1", 16 * (i - 3))
            cx.inc(s.dma_start(out=sb["fin"][:, i % 4, 0, :],
                               in_=t["rs_out"][128 * i:128 * (i + 1), :]),
                   f"ff_ld{i % 4}", 16)
            cx.inc(s.dma_start(out=sb["fin"][:, i % 4, 1, :],
                               in_=t["x_slice"][128 * i:128 * (i + 1), :]),
                   f"ff_ld{i % 4}", 16)
    blk.sync(sync1)

    def vec1(v):
        for i in range(NT):
            cx.wait(v, f"ff_ld{i % 4}", 32 * (i // 4 + 1))
            sc2 = sb["lnsc2"][:, i, 0:8]
            sc3 = sb["lnsc2"][:, i, 8:16]
            lnh_vec_a(cx, v, sb["fin"][:, i % 4, 0, :], sc2, "l2", i)
            lnh_vec_b(cx, v, sc2, "l2", i)
            lnh_vec_c(cx, v, sc2, "l2", i)
            cx.wait(v, "l2f", i + 1)
            cx.inc(v.tensor_add(out=sb["fin"][:, i % 4, 1, :],
                                in0=sb["fin"][:, i % 4, 1, :],
                                in1=sb["fft"][:, i % 2, 0:1024]), "ffr1")
            lnh_vec_a(cx, v, sb["fin"][:, i % 4, 1, :], sc3, "l3", i)
            lnh_vec_b(cx, v, sc3, "l3", i)
            lnh_vec_c(cx, v, sc3, "l3", i)
            cx.wait(v, "l3f", i + 1)
            v.tensor_scalar(out=sb["junk"][:, 0, :], in0=sb["xnb"][:, i, :],
                            scalar1=1.0, scalar2=None, op0=ALU.mult, op1=ALU.max,
                            accum_out=sb["fmx"][:, 0, i:i + 1])
            cx.inc(v.tensor_scalar(out=sb["junk"][:, 1, :], in0=sb["xnb"][:, i, :],
                                   scalar1=-1.0, scalar2=None, op0=ALU.mult,
                                   op1=ALU.max, accum_out=sb["fmx"][:, 1, i:i + 1]),
                   "ffx")
    blk.vector(vec1)

    def act1(s):
        for i in range(NT):
            sc2 = sb["lnsc2"][:, i, 0:8]
            sc3 = sb["lnsc2"][:, i, 8:16]
            lnh_act_a(cx, s, sc2, "l2", i)
            lnh_act_b(cx, s, sc2, "l2", i)
            cx.inc(lnh_act_apply(cx, s, sb["fin"][:, i % 4, 0, :],
                                 sb["fft"][:, i % 2, 0:1024], sc2, "l2", i), "l2f")
            cx.wait(s, "ffr1", i + 1)
            lnh_act_a(cx, s, sc3, "l3", i)
            lnh_act_b(cx, s, sc3, "l3", i)
            cx.inc(lnh_act_apply(cx, s, sb["fin"][:, i % 4, 1, :],
                                 sb["xnb"][:, i, :], sc3, "l3", i), "l3f")
    blk.scalar(act1)

    def gps1(g):
        for i in range(NT):
            lnh_gps_nb(cx, g, sb["lnsc2"][:, i, 0:8], "l2", i)
            cx.wait(g, "ffr1", i + 1)
            lnh_gps_nb(cx, g, sb["lnsc2"][:, i, 8:16], "l3", i)
            cx.inc(g.dma_start(out=t["r1_sp"][128 * i:128 * (i + 1), :],
                               in_=sb["fin"][:, i % 4, 1, :]), "f_r1", 16)
    blk.gpsimd(gps1)

    def armax2(idx, cols, nvals, vecpost, prewait, width=24):
        nco = len(cols)
        def vecr(v):
            cx.wait(v, prewait[0], prewait[1])
            for k, col in enumerate(cols):
                cx.inc(v.tensor_reduce(out=sb["fsc"][:, k:k + 1],
                                       in_=sb["fmx"][:, col, 0:width], axis=AX.X,
                                       op=ALU.max), f"arc{idx}")
        blk.vector(vecr)

        def gpsr(g):
            cx.wait(g, f"arc{idx}", nco)
            cx.inc(g.dma_start(out=mkap(t["flat_sc"], 0, [[1, 128], [128, nco]]),
                               in_=sb["fsc"][:, 0:nco],
                               allow_slow_non_contiguous=True), f"arf{idx}", 16)
        blk.gpsimd(gpsr)

        def syncr(s):
            cx.wait(s, f"arf{idx}", 16)
            cx.inc(s.dma_start(out=sb["flt"][0:nco, :],
                               in_=t["flat_sc"][0:nco, :]), f"arc{idx}", 16)
        blk.sync(syncr)

        def vecr2(v):
            cx.wait(v, f"arc{idx}", nco + 16)
            v.memset(sb["flt2"][:, :], 0.0)
            cx.inc(v.tensor_reduce(out=sb["flt2"][0:nco, 0:1],
                                   in_=sb["flt"][0:nco, :], axis=AX.X,
                                   op=ALU.max), f"arc{idx}")
        blk.vector(vecr2)

        def gpsr2(g):
            cx.wait(g, f"arc{idx}", nco + 17)
            cx.inc(g.dma_start(out=t[f"ar{idx}_in"][0:1, 0:4],
                               in_=sb["flt2"][0:4, 0:1],
                               allow_slow_non_contiguous=True), f"arq{idx}", 16)
            cx.wait(g, f"arq{idx}", 16)
            cx.inc(g.collective_compute(
                "AllReduce", ALU.max,
                ins=[t[f"ar{idx}_in"].ap().opt()], outs=[t[f"ar{idx}_out"].ap().opt()],
                replica_groups=ALL8), f"ar{idx}")
        blk.gpsimd(gpsr2)

        def syncr2(s):
            cx.wait(s, f"ar{idx}")
            cx.inc(s.dma_start(out=sb["arsb"][0:1, 0:nco],
                               in_=t[f"ar{idx}_out"][0:1, 0:nco]), f"arc{idx}", 16)
        blk.sync(syncr2)
        npost = vecpost(nco + 33)

        def gpsr3(g):
            cx.wait(g, f"arc{idx}", nco + 33 + npost)
            cx.inc(g.dma_start(out=t["flat_sc"][7:8, 0:nvals],
                               in_=sb["arv"][0:1, 0:nvals]), f"arg{idx}", 16)
            cx.wait(g, f"arg{idx}", 16)
            cx.inc(g.dma_start(out=sb["fsc"][:, 8:8 + nvals],
                               in_=t["flat_sc"][7:8, 0:nvals].to_broadcast((128, nvals))),
                   f"arB{idx}", 16)
        blk.gpsimd(gpsr3)

    def mkpost(idx, groups, mul, eps, sign, fac):
        def post(base):
            def vA(v):
                cx.wait(v, f"arc{idx}", base)
                a = sb["arsb"]; o = sb["arv"]
                last = None
                for k, (i0, i1) in enumerate(groups):
                    last = v.tensor_max(out=o[0:1, 6 + k:7 + k], in0=a[0:1, i0:i0 + 1],
                                        in1=a[0:1, i1:i1 + 1])
                cx.inc(last, f"arc{idx}")
            blk.vector(vA)

            def gB(g):
                cx.wait(g, f"arc{idx}", base + 1)
                o = sb["arv"]
                last = None
                for k in range(len(groups)):
                    tt = o[0:1, 6 + k:7 + k]
                    g.tensor_scalar(out=o[0:1, 3 * k + 1:3 * k + 2], in0=tt,
                                    scalar1=mul, scalar2=eps, op0=ALU.mult, op1=ALU.max)
                    g.tensor_scalar(out=o[0:1, 3 * k + 2:3 * k + 3], in0=tt,
                                    scalar1=-MAGIC * mul, scalar2=-MAGIC * eps,
                                    op0=ALU.mult, op1=ALU.min)
                    last = g.tensor_scalar(out=o[0:1, 12 + k:13 + k], in0=tt,
                                           scalar1=sign * mul * fac,
                                           scalar2=sign * eps * fac, op0=ALU.mult,
                                           op1=(ALU.max if sign > 0 else ALU.min))
                cx.inc(last, f"arc{idx}")
            blk.gpsimd(gB)

            def vC(v):
                cx.wait(v, f"arc{idx}", base + 2)
                o = sb["arv"]
                last = None
                for k in range(len(groups)):
                    last = v.reciprocal(out=o[0:1, 3 * k:3 * k + 1],
                                        in_=o[0:1, 12 + k:13 + k])
                cx.inc(last, f"arc{idx}")
            blk.vector(vC)
            return 3
        return post

    armax2(0, [0, 1], 3, mkpost(0, [(0, 1)], 1.0 / 32767.0, 1e-12, 1.0, 1.0),
           ("ffx", 16), width=16)

    if tapped(cx, "t_ar0"):
        cx.dout("t_fsc", [128, 16])
        cx.dout("t_arv", [1, 16])
        cx.dout("t_arsb", [1, 8])
        cx.dout("t_fmx", [128, 12 * 24])
        def syncTa0(s):
            cx.wait(s, "arB0", 16)
            cx.inc(s.dma_start(out=t["t_fsc"][:, :], in_=sb["fsc"][:, :]), "taps", 16)
            cx.inc(s.dma_start(out=t["t_arv"][:, :], in_=sb["arv"][:, :]), "taps", 16)
            cx.inc(s.dma_start(out=t["t_arsb"][:, :], in_=sb["arsb"][:, :]), "taps", 16)
            cx.inc(s.dma_start(out=t["t_fmx"][:, :],
                               in_=sb["fmx"][:, :, :].rearrange("p a b -> p (a b)")),
                   "taps", 16)
        blk.sync(syncTa0)

    def act2(s):
        for i in range(NT):
            cx.wait(s, "arB0", 16)
            if i >= 2:
                cx.wait(s, "f_xq", 16 * (i - 1))
            s.activation(out=sb["fft"][:, i % 2, 0:1024], in_=sb["xnb"][:, i, :],
                         func=AF.Identity, scale=sb["fsc"][:, 8:9],
                         bias=sb["magt"][:, 0:1])
            cx.inc(s.activation(out=sb["xqb"][:, i % 2, :], in_=sb["fft"][:, i % 2, 0:1024],
                                func=AF.Identity, scale=sb["fsc"][:, 9:10],
                                bias=sb["fsc"][:, 10:11]), "ffxq")
    blk.scalar(act2)

    def gps2(g):
        for i in range(NT):
            cx.wait(g, "ffxq", i + 1)
            cx.inc(g.dma_start(out=t["xq_sp"][128 * i:128 * (i + 1), :],
                               in_=sb["xqb"][:, i % 2, :]), "f_xq", 16)
    blk.gpsimd(gps2)

    if tapped(cx, "t_lnsc2"):
        cx.dout("t_lnsc2", [128, 16 * 16])
        def syncTl(s):
            cx.wait(s, "l3f", NT)
            cx.wait(s, "ffx", NT)
            cx.inc(s.dma_start(out=t["t_lnsc2"][:, :],
                               in_=sb["lnsc2"][:, :, :].rearrange("p a b -> p (a b)")),
                   "taps", 16)
        blk.sync(syncTl)

    cx.barrier(blk)
    cx.retire("l2", "l2f", "l3", "l3f", "ffr1", "ff_ld0", "ff_ld1", "ff_ld2", "ff_ld3",
              "ffx", "arc0")
    sfree("xnb")
    sget("xqT", [128, 8, 4, 512], BF16)
    sget("wuT", [128, 8, 1536], BF16)
    sget("upgb", [128, 2, 2048], BF16)
    sget("wdq_st", [128, 2, 1536], F32)
    sget("wq_bf", [128, 2, 1536], BF16)
    sget("bmx", [128, 2, 288], F32)
    sget("fbf", [128, 2, 2048], BF16)
    sget("fbf2", [128, 2, 2048], BF16)
    sget("fsg", [128, 2, 2048], F32)

    def sync3(s):
        cx.wait(s, "f_xq")
        for w4 in range(4):
            for kt in range(8):
                cx.inc(s.dma_start(out=sb["xqT"][:, kt, w4, :],
                                   in_=t["xq_sp"][512 * w4:512 * (w4 + 1),
                                                  128 * kt:128 * (kt + 1)],
                                   transpose=True), "ff_xqt", 16)
    blk.sync(sync3)

    DQT = [("wup", "wup_dq", 24, 1, 1024), ("wgate", "wg_dq", 24, 1, 1024),
           ("wdown", "wd_dq", 8, 2, 1536)]

    def dqt_pieces():
        n = 0
        for nm, dnm, ntile, nh, width in DQT:
            for m in range(ntile):
                for hc in range(nh):
                    yield n, nm, dnm, width, 3072 * (nh > 1), \
                        (128 * m, 1536 * hc if nh > 1 else 0)
                    n += 1

    def sync4(s):
        cx.wait(s, "ffxq", 16)
        for n, nm, dnm, width, _, (r0, c0) in dqt_pieces():
            if n >= 2:
                cx.wait(s, "dq35", n - 1)
            cx.inc(s.dma_start(out=sb["wdq_st"][:, n % 2, 0:width],
                               in_=t[nm][r0:r0 + 128, c0:c0 + width]),
                   f"dq_ld{n % 2}", 16)
    blk.sync(sync4)

    def vec4(v):
        for n, nm, dnm, width, _, _ in dqt_pieces():
            nb = width // 32
            cx.wait(v, f"dq_ld{n % 2}", 16 * (n // 2 + 1))
            if n >= 2:
                cx.wait(v, "dqs", n - 1)
            wsrc = sb["wdq_st"][:, n % 2, 0:width]
            cx.inc(v.tensor_reduce(out=sb["bmx"][:, n % 2, 0:nb],
                                   in_=wsrc.rearrange("p (b c) -> p b c", c=32),
                                   axis=AX.X, op=ALU.max,
                                   apply_absolute_value=True), "dqr")
            cx.wait(v, "dqs", n + 1)
            cx.inc(v.reciprocal(out=sb["bmx"][:, n % 2, 192:192 + nb],
                                in_=sb["bmx"][:, n % 2, 96:96 + nb]), "dq1")
            cx.wait(v, "dq2", n + 1)
            cx.inc(v.tensor_scalar(out=sb["wdq_st"][:, n % 2, 0:width],
                                   in0=sb["wdq_st"][:, n % 2, 0:width], scalar1=MAGIC,
                                   scalar2=MAGIC, op0=ALU.add, op1=ALU.subtract),
                   "dq3")
    blk.vector(vec4)

    def gps4(g):
        for n, nm, dnm, width, drow, (r0, c0) in dqt_pieces():
            nb = width // 32
            cx.wait(g, "dqr", n + 1)
            cx.inc(g.tensor_scalar(out=sb["bmx"][:, n % 2, 96:96 + nb],
                                   in0=sb["bmx"][:, n % 2, 0:nb],
                                   scalar1=1.0 / 7.0, scalar2=1e-12,
                                   op0=ALU.mult, op1=ALU.max), "dqs")
            cx.wait(g, "dq1", n + 1)
            cx.inc(g.tensor_tensor(
                out=sb["wdq_st"][:, n % 2, 0:width].rearrange("p (b c) -> p b c", c=32),
                in0=sb["wdq_st"][:, n % 2, 0:width].rearrange("p (b c) -> p b c", c=32),
                in1=sb["bmx"][:, n % 2, 192:192 + nb, None].broadcast_to([128, nb, 32]),
                op=ALU.mult), "dq2")
            cx.wait(g, "dq3", n + 1)
            if n >= 2:
                cx.wait(g, "f_ug0", 16 * (n - 1))
            cx.inc(g.tensor_tensor(
                out=sb["wq_bf"][:, n % 2, 0:width].rearrange("p (b c) -> p b c", c=32),
                in0=sb["wdq_st"][:, n % 2, 0:width].rearrange("p (b c) -> p b c", c=32),
                in1=sb["bmx"][:, n % 2, 96:96 + nb, None].broadcast_to([128, nb, 32]),
                op=ALU.mult), "dq35")
            if drow:
                dst = mkap(t[dnm], r0 * drow + c0, [[drow, 128], [1, width]])
            else:
                dst = t[dnm][r0:r0 + 128, :]
            cx.inc(g.dma_start(out=dst, in_=sb["wq_bf"][:, n % 2, 0:width]),
                   "f_ug0", 16)
    blk.gpsimd(gps4)

    def sync5(s):
        for te2 in range(2):
            dnm = ("wup_dq", "wg_dq")[te2]
            cx.wait(s, "f_ug0", 16 * 24 * (te2 + 1))
            for half in range(2):
                k = 2 * te2 + half
                if k >= 1:
                    cx.wait(s, "ffmm", 4 * 12 * k)
                for kt in range(8):
                    cx.inc(s.dma_start(
                        out=sb["wuT"][:, kt, :],
                        in_=t[dnm][1536 * half:1536 * (half + 1),
                                   128 * kt:128 * (kt + 1)],
                        transpose=True), f"ffw{k}", 16)
    blk.sync(sync5)

    def ten5(te):
        cx.wait(te, "ff_xqt", 16 * 32)
        cx.wait(te, "op_ev")
        nmm = 0
        for te2 in range(2):
            for half in range(2):
                k = 2 * te2 + half
                cx.wait(te, f"ffw{k}", 16 * 8)
                for m in range(12):
                    for w4 in range(4):
                        idx = nmm * 4 + w4
                        ps = sb["p_a"] if idx % 2 == 0 else sb["p_b"]
                        if idx >= 2:
                            cx.wait(te, "ffev", idx - 1)
                        mm = None
                        for kt in range(8):
                            mm = te.matmul(ps[:, :],
                                           sb["wuT"][:, kt, 128 * m:128 * (m + 1)],
                                           sb["xqT"][:, kt, w4, :],
                                           start=(kt == 0), stop=(kt == 7))
                        cx.inc(mm, "ffmm")
                    nmm += 1
    blk.tensor(ten5)

    def act5(s):
        nmm = 0
        for te2 in range(2):
            for m in range(24):
                for w4 in range(4):
                    idx = nmm * 4 + w4
                    cx.wait(s, "ffmm", idx + 1)
                    if nmm >= 2:
                        cx.wait(s, "f_ug1", 16 * (nmm - 1))
                    ps = sb["p_a"] if idx % 2 == 0 else sb["p_b"]
                    cx.inc(s.activation(out=sb["upgb"][:, nmm % 2, 512 * w4:512 * (w4 + 1)],
                                        in_=ps[:, :], func=AF.Identity, scale=-1.0),
                           "ffev")
                nmm += 1
    blk.scalar(act5)

    def vec5(v):
        nmm = 0
        for te2 in range(2):
            for m in range(24):
                cx.wait(v, "ffev", 4 * (nmm + 1))
                jk = sb["junk"][:, :, :].rearrange("p a b -> p (a b)")
                v.tensor_scalar(out=jk, in0=sb["upgb"][:, nmm % 2, :],
                                scalar1=-1.0, scalar2=None, op0=ALU.mult, op1=ALU.max,
                                accum_out=sb["fmx"][:, 2 + 2 * te2, m:m + 1])
                cx.inc(v.tensor_scalar(out=jk, in0=sb["upgb"][:, nmm % 2, :],
                                       scalar1=1.0, scalar2=None, op0=ALU.mult,
                                       op1=ALU.max,
                                       accum_out=sb["fmx"][:, 3 + 2 * te2, m:m + 1]),
                       "ffmxp")
                nmm += 1
    blk.vector(vec5)

    def gps5(g):
        nmm = 0
        for te2 in range(2):
            dnm = ("up_sp", "gate_sp")[te2]
            for m in range(24):
                cx.wait(g, "ffmxp", nmm + 1)
                cx.inc(g.dma_start(out=t[dnm][128 * m:128 * (m + 1), :],
                                   in_=sb["upgb"][:, nmm % 2, :]), "f_ug1", 16)
                nmm += 1
    blk.gpsimd(gps5)

    if tapped(cx, "t_ffn2"):
        cx.dout("t_uppre", [FI, TOK], BF16)
        cx.dout("t_gpre", [FI, TOK], BF16)
        cx.dout("t_wdq", [FI, D], BF16)
        def syncTf2(s):
            cx.wait(s, "f_ug1")
            cx.inc(s.dma_start(out=t["t_uppre"][:, :], in_=t["up_sp"][:, :]), "taps", 16)
            cx.inc(s.dma_start(out=t["t_gpre"][:, :], in_=t["gate_sp"][:, :]), "taps", 16)
            cx.inc(s.dma_start(out=t["t_wdq"][:, :], in_=t["wup_dq"][:, :]), "taps", 16)
        blk.sync(syncTf2)

    armax2(1, [2, 3, 4, 5], 6,
           mkpost(1, [(0, 1), (2, 3)], 1.0 / 32767.0, 1e-12, -1.0, 1.0),
           ("ffmxp", 48))

    def sync6(s):
        for m in range(24):
            cx.wait(s, "arB1", 16)
            if m >= 2:
                cx.wait(s, "f_upq", 16 * (m - 1))
            cx.inc(s.dma_start(out=sb["fbf"][:, m % 2, :],
                               in_=t["up_sp"][128 * m:128 * (m + 1), :]), f"upr{m % 2}", 16)
    blk.sync(sync6)

    def act6(s):
        for m in range(24):
            cx.wait(s, f"upr{m % 2}", 16 * (m // 2 + 1))
            s.activation(out=sb["fft"][:, m % 2, :], in_=sb["fbf"][:, m % 2, :],
                         func=AF.Identity, scale=sb["fsc"][:, 8:9],
                         bias=sb["magt"][:, 0:1])
            cx.inc(s.activation(out=sb["fbf"][:, m % 2, :], in_=sb["fft"][:, m % 2, :],
                                func=AF.Identity, scale=sb["fsc"][:, 9:10],
                                bias=sb["fsc"][:, 10:11]), "ffupq")
    blk.scalar(act6)

    def gps6(g):
        for m in range(24):
            cx.wait(g, "ffupq", m + 1)
            cx.inc(g.dma_start(out=t["up_sp"][128 * m:128 * (m + 1), :],
                               in_=sb["fbf"][:, m % 2, :]), "f_upq", 16)
    blk.gpsimd(gps6)

    def sync7(s):
        for m in range(24):
            cx.wait(s, "arB1", 16)
            if m >= 2:
                cx.wait(s, "f_g2", 16 * (m - 1))
            cx.inc(s.dma_start(out=sb["fbf2"][:, m % 2, :],
                               in_=t["gate_sp"][128 * m:128 * (m + 1), :]), f"gtr{m % 2}", 16)
    blk.sync(sync7)

    def act7(s):
        for m in range(24):
            cx.wait(s, f"gtr{m % 2}", 16 * (m // 2 + 1))
            s.activation(out=sb["fft"][:, m % 2, :], in_=sb["fbf2"][:, m % 2, :],
                         func=AF.Identity, scale=sb["fsc"][:, 11:12],
                         bias=sb["magt"][:, 0:1])
            s.activation(out=sb["fbf2"][:, m % 2, :], in_=sb["fft"][:, m % 2, :],
                         func=AF.Identity, scale=sb["fsc"][:, 12:13],
                         bias=sb["fsc"][:, 13:14])
            cx.inc(s.activation(out=sb["fsg"][:, m % 2, :], in_=sb["fbf2"][:, m % 2, :],
                                func=AF.Sigmoid), "ffsig")
    blk.scalar(act7)

    def gps7(g):
        for m in range(24):
            cx.wait(g, "ffsgq", m + 1)
            cx.inc(g.tensor_tensor(out=sb["fbf2"][:, m % 2, :], in0=sb["fbf2"][:, m % 2, :],
                                   in1=sb["fsg"][:, m % 2, :], op=ALU.mult), "ffg2p")
            cx.wait(g, "ffg2m", m + 1)
            cx.inc(g.dma_start(out=t["gate_sp"][128 * m:128 * (m + 1), :],
                               in_=sb["fbf2"][:, m % 2, :]), "f_g2", 16)
    blk.gpsimd(gps7)

    def vec7(v):
        def sigfix(m):
            cx.wait(v, "ffsig", m + 1)
            v.tensor_scalar(out=sb["fsg"][:, m % 2, :], in0=sb["fsg"][:, m % 2, :],
                            scalar1=65536.0, scalar2=None, op0=ALU.mult)
            v.tensor_scalar(out=sb["fsg"][:, m % 2, :], in0=sb["fsg"][:, m % 2, :],
                            scalar1=MAGIC, scalar2=MAGIC, op0=ALU.add, op1=ALU.subtract)
            cx.inc(v.tensor_scalar(out=sb["fsg"][:, m % 2, :], in0=sb["fsg"][:, m % 2, :],
                                   scalar1=65535.0, scalar2=None, op0=ALU.min), "ffsgq")

        def g2max(m):
            cx.wait(v, "ffg2p", m + 1)
            jk = sb["junk"][:, :, :].rearrange("p a b -> p (a b)")
            v.tensor_scalar(out=jk, in0=sb["fbf2"][:, m % 2, :],
                            scalar1=1.0, scalar2=None, op0=ALU.mult, op1=ALU.max,
                            accum_out=sb["fmx"][:, 6, m:m + 1])
            cx.inc(v.tensor_scalar(out=jk, in0=sb["fbf2"][:, m % 2, :],
                                   scalar1=-1.0, scalar2=None, op0=ALU.mult, op1=ALU.max,
                                   accum_out=sb["fmx"][:, 7, m:m + 1]), "ffg2m")

        for m in range(24):
            sigfix(m)
            if m >= 1:
                g2max(m - 1)
        g2max(23)
    blk.vector(vec7)

    armax2(2, [6, 7], 3,
           mkpost(2, [(0, 1)], 2.0 ** -16 / 32767.0, 1e-12, 1.0, 2.0 ** 16),
           ("ffg2m", 24))

    def sync8(s):
        for m in range(24):
            cx.wait(s, "arB2", 16)
            cx.wait(s, "f_upq", 16 * min(m + 1, 24))
            if m >= 2:
                cx.wait(s, "f_op", 16 * (m - 1))
            cx.inc(s.dma_start(out=sb["fbf2"][:, m % 2, :],
                               in_=t["gate_sp"][128 * m:128 * (m + 1), :]), f"g2r{m % 2}", 16)
            cx.inc(s.dma_start(out=sb["fbf"][:, m % 2, :],
                               in_=t["up_sp"][128 * m:128 * (m + 1), :]), f"g2r{m % 2}", 16)
    blk.sync(sync8)

    def act8(s):
        for m in range(24):
            cx.wait(s, f"g2r{m % 2}", 32 * (m // 2 + 1))
            s.activation(out=sb["fft"][:, m % 2, :], in_=sb["fbf2"][:, m % 2, :],
                         func=AF.Identity, scale=sb["fsc"][:, 8:9],
                         bias=sb["magt"][:, 0:1])
            cx.inc(s.activation(out=sb["fbf2"][:, m % 2, :], in_=sb["fft"][:, m % 2, :],
                                func=AF.Identity, scale=sb["fsc"][:, 9:10],
                                bias=sb["fsc"][:, 10:11]), "ffg2q")
    blk.scalar(act8)

    def gps8(g):
        for m in range(24):
            cx.wait(g, "ffg2q", m + 1)
            cx.inc(g.tensor_tensor(out=sb["fbf2"][:, m % 2, :], in0=sb["fbf2"][:, m % 2, :],
                                   in1=sb["fbf"][:, m % 2, :], op=ALU.mult), "ffopre")
            cx.wait(g, "ffoprm", m + 1)
            cx.inc(g.dma_start(out=t["opre_sp"][128 * m:128 * (m + 1), :],
                               in_=sb["fbf2"][:, m % 2, :]), "f_op", 16)
    blk.gpsimd(gps8)

    def vec8(v):
        for m in range(24):
            cx.wait(v, "ffopre", m + 1)
            jk = sb["junk"][:, :, :].rearrange("p a b -> p (a b)")
            v.tensor_scalar(out=jk, in0=sb["fbf2"][:, m % 2, :],
                            scalar1=1.0, scalar2=None, op0=ALU.mult, op1=ALU.max,
                            accum_out=sb["fmx"][:, 8, m:m + 1])
            cx.inc(v.tensor_scalar(out=jk, in0=sb["fbf2"][:, m % 2, :],
                                   scalar1=-1.0, scalar2=None, op0=ALU.mult, op1=ALU.max,
                                   accum_out=sb["fmx"][:, 9, m:m + 1]), "ffoprm")
    blk.vector(vec8)

    armax2(3, [8, 9], 3, mkpost(3, [(0, 1)], 1.0 / 32767.0, 1e-12, 1.0, 1.0),
           ("ffoprm", 24))

    if tapped(cx, "t_ffn"):
        cx.dout("t_xq", [TOK, D], BF16)
        cx.dout("t_r1", [TOK, D])
        cx.dout("t_up", [FI, TOK], BF16)
        cx.dout("t_g2", [FI, TOK], BF16)
        cx.dout("t_opre", [FI, TOK], BF16)
        def syncTf(s):
            cx.wait(s, "f_xq")
            cx.inc(s.dma_start(out=t["t_xq"][:, :], in_=t["xq_sp"][:, :]), "taps", 16)
            cx.wait(s, "f_r1")
            cx.inc(s.dma_start(out=t["t_r1"][:, :], in_=t["r1_sp"][:, :]), "taps", 16)
            cx.wait(s, "f_upq")
            cx.inc(s.dma_start(out=t["t_up"][:, :], in_=t["up_sp"][:, :]), "taps", 16)
            cx.wait(s, "f_g2")
            cx.inc(s.dma_start(out=t["t_g2"][:, :], in_=t["gate_sp"][:, :]), "taps", 16)
            cx.wait(s, "f_op")
            cx.inc(s.dma_start(out=t["t_opre"][:, :], in_=t["opre_sp"][:, :]), "taps", 16)
        blk.sync(syncTf)

    cx.barrier(blk)
    cx.retire("ff_xqt", "dq_ld0", "dq_ld1", "dq1", "dq2", "dq3", "dq35", "dqr", "dqs",
              "ffw0", "ffw1", "ffw2", "ffw3", "ffmm", "ffmxp", "arc1",
              "upr0", "upr1", "ffupq", "gtr0", "gtr1", "ffsig", "ffsgq",
              "ffg2p", "ffg2m", "arc2", "g2r0",
              "g2r1", "ffg2q", "ffopre", "ffoprm", "arc3", "op_ev")
    sfree("fsg", "fbf2", "fbf", "bmx", "wq_bf", "wdq_st", "upgb", "wuT", "xqT")
    sget("wdT", [128, 24, D], BF16)
    sget("oqg", [128, 24, 512], BF16)
    sget("fq", [128, 2, 512], BF16)
    sget("fr1", [128, 2, D], F32)
    sget("fdo", [128, 16, D], BF16)
    sget("fob", [128, 2, D], F32)

    def sync9(s):
        cx.wait(s, "f_ug0", 16 * 56)
        cx.wait(s, "f_op", 16 * 24)
        for kc in range(24):
            cx.inc(s.dma_start(out=sb["wdT"][:, kc, :],
                               in_=t["wd_dq"][:, 128 * kc:128 * (kc + 1)],
                               transpose=True), "ffwd", 16)
        for tg in range(4):
            for kc in range(24):
                n = tg * 24 + kc
                cx.wait(s, "arB3", 16)
                if n >= 2:
                    cx.wait(s, "ffoq", n - 1)
                cx.inc(s.dma_start(out=sb["fq"][:, n % 2, :],
                                   in_=t["opre_sp"][128 * kc:128 * (kc + 1),
                                                    512 * tg:512 * (tg + 1)]),
                       f"fq{n % 2}", 16)
        for i in range(NT):
            cx.wait(s, "f_r1", 16 * 16)
            if i >= 2:
                cx.wait(s, "out_st", 16 * (i - 1))
            cx.inc(s.dma_start(out=sb["fr1"][:, i % 2, :],
                               in_=t["r1_sp"][128 * i:128 * (i + 1), :]), f"fr{i % 2}", 16)
    blk.sync(sync9)

    def act9(s):
        for tg in range(4):
            for kc in range(24):
                n = tg * 24 + kc
                cx.wait(s, f"fq{n % 2}", 16 * (n // 2 + 1))
                if tg >= 1 and kc == 0:
                    cx.wait(s, "ffdmm", 8 * tg)
                s.activation(out=sb["fft"][:, n % 2, 0:512], in_=sb["fq"][:, n % 2, :],
                             func=AF.Identity, scale=sb["fsc"][:, 8:9],
                             bias=sb["magt"][:, 0:1])
                cx.inc(s.activation(out=sb["oqg"][:, kc, :], in_=sb["fft"][:, n % 2, 0:512],
                                    func=AF.Identity, scale=sb["fsc"][:, 9:10],
                                    bias=sb["fsc"][:, 10:11]), "ffoq")
    blk.scalar(act9)

    def ten9(te):
        cx.wait(te, "ffwd", 16 * 24)
        cx.wait(te, "ffev")
        for tg in range(4):
            for tsub in range(4):
                i = tg * 4 + tsub
                cx.wait(te, "ffoq", 24 * (tg + 1))
                if i >= 1:
                    cx.wait(te, "ffdev", 2 * i)
                mma = mmb = None
                for kc in range(24):
                    lh = sb["oqg"][:, kc, 128 * tsub:128 * (tsub + 1)]
                    mma = te.matmul(sb["p_a"][:, :], lh, sb["wdT"][:, kc, 0:512],
                                    start=(kc == 0), stop=(kc == 23))
                    mmb = te.matmul(sb["p_b"][:, :], lh, sb["wdT"][:, kc, 512:1024],
                                    start=(kc == 0), stop=(kc == 23))
                cx.inc(mmb, "ffdmm", 2)
    blk.tensor(ten9)

    def vec9(v):
        for i in range(NT):
            cx.wait(v, "ffdmm", 2 * (i + 1))
            v.tensor_copy(out=sb["fdo"][:, i, 0:512], in_=sb["p_a"][:, :])
            cx.inc(v.tensor_copy(out=sb["fdo"][:, i, 512:1024], in_=sb["p_b"][:, :]),
                   "ffdev", 2)
    blk.vector(vec9)

    def vec10(v):
        for i in range(NT):
            sc4 = sb["lnsc2"][:, i, 0:8]
            lnh_vec_a(cx, v, sb["fdo"][:, i, :], sc4, "l4", i)
            lnh_vec_b(cx, v, sc4, "l4", i)
            lnh_vec_c(cx, v, sc4, "l4", i)
            cx.wait(v, "l4f", i + 1)
            cx.wait(v, f"fr{i % 2}", 16 * (i // 2 + 1))
            cx.inc(v.tensor_add(out=sb["fr1"][:, i % 2, :], in0=sb["fr1"][:, i % 2, :],
                                in1=sb["fob"][:, i % 2, :]), "ffout")
    blk.vector(vec10)

    def act10(s):
        for i in range(NT):
            sc4 = sb["lnsc2"][:, i, 0:8]
            lnh_act_a(cx, s, sc4, "l4", i)
            lnh_act_b(cx, s, sc4, "l4", i)
            if i >= 2:
                cx.wait(s, "ffout", i - 1)
            cx.inc(lnh_act_apply(cx, s, sb["fdo"][:, i, :], sb["fob"][:, i % 2, :],
                                 sc4, "l4", i), "l4f")
    blk.scalar(act10)

    def gps9(g):
        for i in range(NT):
            lnh_gps_nb(cx, g, sb["lnsc2"][:, i, 0:8], "l4", i)
            cx.wait(g, "ffout", i + 1)
            cx.inc(g.dma_start(out=t["out"][128 * i:128 * (i + 1), :],
                               in_=sb["fr1"][:, i % 2, :]), "out_st", 16)
    blk.gpsimd(gps9)


def mkap_sb(handle_slice, n):
    return handle_slice[0:n, 0:1]


def emit_tail(cx, blk):
    def sync(s):
        for nm in ("taps", "pj_sp", "pj_z", "gm_st", "y_done",
                   "out_st", "ag1", "ag2", "pj_bt", "ld", "ldg",
                   "fc_st", "fir_sp0", "fir_sp1", "sc_ost",
                   "sc_spn0", "sc_spn1", "rs", "f_r1", "f_xq", "f_ug0", "f_ug1",
                   "f_g2", "f_op", "f_upq"):
            if cx.n(nm) > 0:
                cx.wait(s, nm)
    blk.sync(sync)



def stage_inputs(core, inputs):
    b, j = core // 4, core % 4
    h = j
    hs = slice(256 * h, 256 * (h + 1))
    g = lambda n: np.asarray(inputs[n])
    m = {}
    x = g("L0_attn_hidden_states")
    m["x_slice"] = np.ascontiguousarray(x[b, TOK * j:TOK * (j + 1), :], np.float32)
    bundle = np.concatenate([
        g("L0_attn_q_proj_w")[hs], g("L0_attn_k_proj_w")[hs],
        g("L0_attn_v_proj_w")[hs], g("L0_attn_b_proj_w")[h:h + 1],
        np.zeros((3, D), np.float32)], 0)
    m["wqkvb_t"] = np.ascontiguousarray(bundle.T, np.float32)
    m["convw"] = np.ascontiguousarray(np.concatenate([
        g("L0_attn_q_conv_w")[hs], g("L0_attn_k_conv_w")[hs],
        g("L0_attn_v_conv_w")[hs]], 0), np.float32)
    m["gw1t"] = np.ascontiguousarray(g("L0_attn_gmix_w1").T, np.float32)
    m["gb1"] = np.ascontiguousarray(g("L0_attn_gmix_b1").reshape(16, 128).T, np.float32)
    m["gw2t"] = np.ascontiguousarray(g("L0_attn_gmix_w2").T, np.float32)
    m["gb2"] = np.ascontiguousarray(g("L0_attn_gmix_b2").reshape(16, 1), np.float32)
    m["alpha_h"] = np.ascontiguousarray(g("L0_attn_alpha")[h].reshape(1, 1), np.float32)
    m["onw"] = np.ascontiguousarray(g("L0_attn_o_norm_w").reshape(2, 128).T, np.float32)
    m["wot"] = np.ascontiguousarray(g("L0_attn_o_proj_w")[:, hs].T, np.float32)
    pad_s = np.zeros((DV, 384), np.float32)
    pad_s[:, 128:135] = g("L0_attn_fir_short_w")[h][:, ::-1]
    pad_l = np.zeros((DV, 384), np.float32)
    pad_l[:, 128:159] = g("L0_attn_fir_long_w")[h][:, ::-1]
    m["fir_s_pad"], m["fir_l_pad"] = pad_s, pad_l
    m["wup"] = np.ascontiguousarray(g("L0_ffn_w_up"), np.float32)
    m["wgate"] = np.ascontiguousarray(g("L0_ffn_w_gate"), np.float32)
    m["wdown"] = np.ascontiguousarray(g("L0_ffn_w_down"), np.float32)
    hsel = np.zeros((128, 4), np.float32); hsel[:, h] = 1.0
    m["hsel"] = hsel
    m["ident"] = np.eye(128, dtype=ml_dtypes.bfloat16)
    m["m_sl_neg"] = (-np.tril(np.ones((128, 128)), -1)).astype(ml_dtypes.bfloat16)
    m["m_su_neg"] = (-np.triu(np.ones((128, 128)), 1)).astype(ml_dtypes.bfloat16)
    m["m_u_incl"] = np.triu(np.ones((128, 128)), 0).astype(ml_dtypes.bfloat16)
    return m


def kernel(**inputs):
    nc = build_program(upto="ffn")
    in_maps = [stage_inputs(c, inputs) for c in range(8)]
    res = run_bass_kernel_spmd(nc, in_maps, core_ids=list(range(8)))
    out = np.zeros((B, L, D), np.float32)
    for c in range(8):
        b, j = c // 4, c % 4
        out[b, TOK * j:TOK * (j + 1), :] = res.results[c]["out"]
    return out
